# version 14
# baseline (speedup 1.0000x reference)
import math
from contextlib import ExitStack

import numpy as np
import ml_dtypes
import concourse.bass as bass
import concourse.mybir as mybir
from concourse.bass_utils import run_bass_kernel_spmd

F32 = mybir.dt.float32
BF16 = mybir.dt.bfloat16
AF = mybir.ActivationFunctionType
ALU = mybir.AluOpType
AX = mybir.AxisListType
bfdt = ml_dtypes.bfloat16

D = 1024
T = 2048
NCORES = 8
NDC = 8
NTC = 16
NQB = 4
HEADS = 16
HD = 64
CAPACITY = 5120
EPS = 1e-5
NEG = -1.0e9

_MODE = "fused"
_DEBUG_DUMP = []



def _host_shared(inputs):
    g = {}
    f32 = np.float32
    g["b0_wq"] = np.ascontiguousarray(inputs["L0_attn_wq"], f32)
    g["b0_wk"] = np.ascontiguousarray(inputs["L0_attn_wk"], f32)
    g["b0_wv"] = np.ascontiguousarray(inputs["L0_attn_wv"], f32)
    g["b0_wo"] = np.ascontiguousarray(inputs["L0_attn_wo"], f32)
    g["b1_wgT"] = np.ascontiguousarray(inputs["L0_ffn_w_gate"].T, f32)
    g["b1_eb"] = np.ascontiguousarray(inputs["L0_ffn_expert_bias"].reshape(1, 8), f32)
    g["b1_sw1T"] = np.ascontiguousarray(inputs["L0_ffn_sw1"].T, f32)
    g["b1_sw3T"] = np.ascontiguousarray(inputs["L0_ffn_sw3"].T, f32)
    g["b1_sw2T"] = np.ascontiguousarray(inputs["L0_ffn_sw2"].T, f32)
    g["b1_w1"] = np.ascontiguousarray(inputs["L0_ffn_w1"].reshape(8 * 1024, 512), f32)
    g["b1_w3"] = np.ascontiguousarray(inputs["L0_ffn_w3"].reshape(8 * 1024, 512), f32)
    g["b1_w2"] = np.ascontiguousarray(inputs["L0_ffn_w2"].reshape(8 * 512, 1024), f32)
    g["b2_wq"] = np.ascontiguousarray(inputs["L1_attn_Wq"], f32)
    g["b2_wk"] = np.ascontiguousarray(inputs["L1_attn_Wk"], f32)
    g["b2_wv"] = np.ascontiguousarray(inputs["L1_attn_Wv"], f32)
    g["b2_wf"] = np.ascontiguousarray(inputs["L1_attn_Wf"], f32)
    for nm, key in (("bq", "L1_attn_bq"), ("bk", "L1_attn_bk"),
                    ("bv", "L1_attn_bv"), ("bf", "L1_attn_bf")):
        g["b2_" + nm] = np.ascontiguousarray(
            inputs[key].reshape(NDC, 128).T, f32)
    g["b3_gwT"] = np.ascontiguousarray(inputs["L1_ffn_gate_w"].T, f32)
    g["b3_wegT"] = np.ascontiguousarray(
        inputs["L1_ffn_we_gate"].transpose(0, 2, 1).reshape(8 * 1024, 1024), f32)
    g["b3_weuT"] = np.ascontiguousarray(
        inputs["L1_ffn_we_up"].transpose(0, 2, 1).reshape(8 * 1024, 1024), f32)
    g["b3_wedT"] = np.ascontiguousarray(
        inputs["L1_ffn_we_down"].transpose(0, 2, 1).reshape(8 * 1024, 1024), f32)
    g["b3_wsgT"] = np.ascontiguousarray(inputs["L1_ffn_ws_gate"].T, f32)
    g["b3_wsuT"] = np.ascontiguousarray(inputs["L1_ffn_ws_up"].T, f32)
    g["b3_wsdT"] = np.ascontiguousarray(inputs["L1_ffn_ws_down"].T, f32)

    g["c_id32"] = np.eye(128, dtype=f32)
    g["c_idbf"] = np.eye(128, dtype=f32).astype(bfdt)
    jj = np.arange(128)[None, :]
    pp = np.arange(128)[:, None]
    g["c_tri_fwd"] = np.where(pp <= jj, 0.0, NEG).astype(f32)
    g["c_tri_bwd"] = np.where(pp >= jj, 0.0, NEG).astype(f32)
    inv_freq = 1.0 / (10000.0 ** (np.arange(0, HD, 2, dtype=np.float64) / HD))
    ang = np.concatenate(
        [(128 * tc + np.arange(128, dtype=np.float64))[:, None] * inv_freq[None, :]
         for tc in range(NTC)], axis=1)
    g["c_cos"] = np.cos(ang).astype(f32)
    g["c_sin"] = np.sin(ang).astype(f32)
    half = HEADS // 2
    base = 24.0 ** (1.0 / half)
    slopes = 1.0 / base ** np.arange(1, half + 1, dtype=np.float64)
    tab = np.zeros((128, HEADS * 16), dtype=f32)
    for h in range(HEADS):
        s = slopes[h % half] * (1.0 if h < half else -1.0)
        for kc in range(16):
            tab[:, 16 * h + kc] = (s * (np.arange(128) + 128 * kc)).astype(f32)
    g["c_slopek"] = tab
    g["c_ustrict"] = np.triu(np.ones((128, 128), dtype=f32), 1)
    g["c_u16"] = np.triu(np.ones((16, 16), dtype=f32), 1)
    qb = np.zeros((HEADS, T), dtype=f32)
    for h in range(HEADS):
        s = slopes[h % half] * (1.0 if h < half else -1.0)
        qb[h] = (-s * np.arange(T)).astype(f32)
    g["c_qbias"] = qb
    g["c_qbias_bf"] = qb.astype(bfdt)
    return g


def _host_core(core):
    f32 = np.float32
    g = {}
    pos = np.zeros((128, 32), dtype=f32)
    for tc in range(NTC):
        for k in range(2):
            pos[:, 2 * tc + k] = 4096 * core + 256 * tc + 2 * np.arange(128) + k
    g["c_pos"] = pos
    g["c_maskcol"] = (np.arange(8) < core).astype(f32).reshape(8, 1)
    g["c_coreflag"] = np.full((128, 1), 1.0 if core == 0 else 0.0, dtype=f32)
    return g


def _host_inputs(inputs):
    shared = _host_shared(inputs)
    x = inputs["L0_attn_x"]
    maps = []
    for c in range(NCORES):
        m = dict(shared)
        m.update(_host_core(c))
        m["x_in"] = np.ascontiguousarray(x[c].T.astype(np.float32))
        maps.append(m)
    return maps



class Ctx:
    def __init__(self, nc, stack):
        self.nc = nc
        self.stack = stack
        self.t = {}
        self._sems = {}
        self.c = {}

    def sem(self, name):
        if name not in self._sems:
            self._sems[name] = self.stack.enter_context(self.nc.semaphore(name))
            self.c[name] = 0
        return self._sems[name]

    def bump(self, name, n=1):
        self.sem(name)
        self.c[name] += n
        return self.c[name]

    def sb(self, name, shape, dtype, stack=None):
        self._n = getattr(self, "_n", {})
        k = self._n.get(name, 0)
        self._n[name] = k + 1
        uname = name if k == 0 else f"{name}_{k}"
        h = (stack or self.stack).enter_context(
            self.nc.sbuf_tensor(uname, shape, dtype))
        self.t[name] = h
        return h

    def ps(self, name, shape, dtype=F32, stack=None):
        h = (stack or self.stack).enter_context(
            self.nc.psum_tensor(name, shape, dtype))
        self.t[name] = h
        return h




def _wview(dram, nch, ncols, row0=0, col0=0):
    rs = dram.shape[1]
    return bass.AP(dram, row0 * rs + col0, [[rs, 128], [128 * rs, nch], [1, ncols]])



def emit_ln_stats_from_dram(C, x_dram, xst2=None, sq2=None):
    nc = C.nc
    xst, tmp1 = C.t["XST"], C.t["TMP1"]
    ones128f = C.t["ones128f"]
    psa, psb = C.t["PSA"], C.t["PSB"]
    ld0 = C.c["ld"]; mm0 = C.c["mm"]; ev0 = C.c["ev"]; vv_0 = C.c["vv"]
    gg_0 = C.c["gg"]
    depth = 2 if xst2 is not None else 1
    assert depth == 2

    def stg(c):
        return xst[:, 0:T] if (xst2 is None or c % 2 == 0) else xst2

    def sqb(c):
        return tmp1[:, 0:T] if (sq2 is None or c % 2 == 0) else sq2

    la0 = C.c["lda"] if "lda" in C.c else C.bump("lda", 0)
    lb0 = C.c["ldb"] if "ldb" in C.c else C.bump("ldb", 0)

    def ldsem(c):
        return C.sem("lda") if c % 2 == 0 else C.sem("ldb")

    def ldval(c):
        return (la0 if c % 2 == 0 else lb0) + 16 * (c // 2 + 1)

    with nc.Block() as blk:
        @blk.sync
        def _(sync):
            for c in range(NDC):
                if c >= depth:
                    sync.wait_ge(C.sem("mm"), mm0 + c - depth + 1)
                sync.dma_start(out=stg(c), in_=x_dram[128 * c:128 * (c + 1), :]
                               ).then_inc(ldsem(c), 16)

        @blk.vector
        def _(v):
            for c in range(0, NDC, 2):
                v.wait_ge(ldsem(c), ldval(c))
                if c >= depth:
                    v.wait_ge(C.sem("mm"), mm0 + c - depth + 1)
                v.drain()
                v.tensor_tensor(sqb(c), stg(c), stg(c), op=ALU.mult
                                ).then_inc(C.sem("vv"), 1)

        @blk.gpsimd
        def _(g):
            for c in range(1, NDC, 2):
                g.wait_ge(ldsem(c), ldval(c))
                if c >= depth:
                    g.wait_ge(C.sem("mm"), mm0 + c - depth + 1)
                g.drain()
                g.tensor_tensor(sqb(c), stg(c), stg(c), op=ALU.mult
                                ).then_inc(C.sem("gg"), 1)

        @blk.tensor
        def _(t):
            for c in range(NDC):
                if c % 2 == 0:
                    t.wait_ge(C.sem("vv"), vv_0 + c // 2 + 1)
                else:
                    t.wait_ge(C.sem("gg"), gg_0 + c // 2 + 1)
                for s4 in range(4):
                    t.matmul(psa[0:1, 512 * s4:512 * (s4 + 1)], ones128f[:, :],
                             stg(c)[:, 512 * s4:512 * (s4 + 1)],
                             start=(c == 0), stop=(c == NDC - 1))
                for s4 in range(4):
                    mmi = t.matmul(psb[0:1, 512 * s4:512 * (s4 + 1)], ones128f[:, :],
                                   sqb(c)[:, 512 * s4:512 * (s4 + 1)],
                                   start=(c == 0), stop=(c == NDC - 1))
                mmi.then_inc(C.sem("mm"), 1)
    C.c["lda"] = la0 + 16 * (NDC // 2)
    C.c["ldb"] = lb0 + 16 * (NDC // 2)
    C.c["vv"] = vv_0 + NDC // 2
    C.c["gg"] = gg_0 + NDC // 2
    C.c["mm"] = mm0 + NDC


def emit_ln_tail(C, width, col0, mode, x_dram=None, out_bf=None,
                 ysrc=None, store=None, extra_store_wait=True,
                 xst2=None, sq2=None, acp=None, bcp=None):
    nc = C.nc
    xst, tmp1 = C.t["XST"], C.t["TMP1"]
    id32 = C.t["id32"]
    ones1f = C.t["ones1f"]
    psa, psb = C.t["PSA"], C.t["PSB"]
    stt, mean, var, rstd, bcol = (C.t["STT"], C.t["MEAN"], C.t["VAR"],
                                  C.t["RSTD"], C.t["BCOL"])
    arow = C.t["XST"]
    brow = C.t["TMP1"]
    ntile = width // 128
    nslice = width // 512

    ev0 = C.c["ev"]; mm0 = C.c["mm"]; vv0 = C.c["vv"]
    with nc.Block() as blk:
        @blk.scalar
        def _(s):
            s.copy(arow[0:1, 0:width], psa[0:1, 0:width])
            s.copy(brow[0:1, 0:width], psb[0:1, 0:width]).then_inc(C.sem("ev"), 1)

        @blk.tensor
        def _(t):
            t.wait_ge(C.sem("ev"), ev0 + 1)
            tri_ = None
            for q, rsrc in ((0, arow), (1, brow)):
                for tl in range(ntile):
                    tri_ = t.transpose(psa[:, ntile * q + tl:ntile * q + tl + 1],
                                       rsrc[0:1, 128 * tl:128 * (tl + 1)],
                                       id32[0:1, 0:1])
            tri_.then_inc(C.sem("mm"), 1)

        @blk.vector
        def _(v):
            v.wait_ge(C.sem("mm"), mm0 + 1)
            v.tensor_copy(stt[:, 0:2 * ntile], psa[:, 0:2 * ntile])
            v.drain()
            v.tensor_scalar_mul(mean[:, 0:ntile], stt[:, 0:ntile], 1.0 / D)
            v.tensor_scalar_mul(var[:, 0:ntile], stt[:, ntile:2 * ntile], 1.0 / D)
            v.drain()
            v.tensor_tensor(stt[:, 48:48 + ntile], mean[:, 0:ntile], mean[:, 0:ntile],
                            op=ALU.mult)
            v.drain()
            v.tensor_tensor(var[:, 0:ntile], var[:, 0:ntile], stt[:, 48:48 + ntile],
                            op=ALU.subtract)
            v.drain()
            v.tensor_scalar_add(var[:, 0:ntile], var[:, 0:ntile], EPS
                                ).then_inc(C.sem("vv"), 1)
    C.c["ev"] = ev0 + 1
    C.c["mm"] = mm0 + 1
    C.c["vv"] = vv0 + 1

    ev0 = C.c["ev"]; mm0 = C.c["mm"]; vv0 = C.c["vv"]
    with nc.Block() as blk:
        @blk.scalar
        def _(s):
            s.activation(out=rstd[:, 0:ntile], in_=var[:, 0:ntile], func=AF.Ln)
            s.drain()
            s.activation(out=rstd[:, 0:ntile], in_=rstd[:, 0:ntile], func=AF.Exp,
                         scale=-0.5).then_inc(C.sem("ev"), 1)

        @blk.vector
        def _(v):
            v.wait_ge(C.sem("ev"), ev0 + 1)
            v.scalar_tensor_tensor(bcol[:, 0:ntile], mean[:, 0:ntile], -1.0,
                                   rstd[:, 0:ntile], op0=ALU.mult, op1=ALU.mult
                                   ).then_inc(C.sem("vv"), 1)

        @blk.tensor
        def _(t):
            t.wait_ge(C.sem("vv"), vv0 + 1)
            for tl in range(ntile):
                t.transpose(psa[0:1, 128 * tl:128 * (tl + 1)],
                            rstd[:, tl:tl + 1], id32[:, :])
            tri_ = None
            for tl in range(ntile):
                tri_ = t.transpose(psb[0:1, 128 * tl:128 * (tl + 1)],
                                   bcol[:, tl:tl + 1], id32[:, :])
            tri_.then_inc(C.sem("mm"), 1)
    C.c["ev"] = ev0 + 1
    C.c["vv"] = vv0 + 1
    C.c["mm"] = mm0 + 1

    ev0 = C.c["ev"]; mm0 = C.c["mm"]
    with nc.Block() as blk:
        @blk.scalar
        def _(s):
            s.copy(arow[0:1, 0:width], psa[0:1, 0:width])
            s.copy(brow[0:1, 0:width], psb[0:1, 0:width]).then_inc(C.sem("ev"), 1)
            s.wait_ge(C.sem("mm"), mm0 + 1)
            s.copy(acp, psa[:, 0:width])
            s.copy(bcp, psb[:, 0:width]).then_inc(C.sem("ev"), 1)

        @blk.tensor
        def _(t):
            t.wait_ge(C.sem("ev"), ev0 + 1)
            for s4 in range(nslice):
                t.matmul(psa[:, 512 * s4:512 * (s4 + 1)], ones1f[:, :],
                         arow[0:1, 512 * s4:512 * (s4 + 1)], start=True, stop=True)
            mmi = None
            for s4 in range(nslice):
                mmi = t.matmul(psb[:, 512 * s4:512 * (s4 + 1)], ones1f[:, :],
                               brow[0:1, 512 * s4:512 * (s4 + 1)], start=True, stop=True)
            mmi.then_inc(C.sem("mm"), 1)
    C.c["ev"] = ev0 + 2
    C.c["mm"] = mm0 + 1
    cp_done = C.c["ev"]

    ld0 = C.c["ld"]; vv0 = C.c["vv"]; mm_done = C.c["mm"]; st0 = C.c["st"]
    depth = 2 if xst2 is not None else 1

    def stg(c):
        return (xst[:, 0:width] if (xst2 is None or c % 2 == 0)
                else xst2[:, 0:width])

    def sqb(c):
        return (tmp1[:, 0:width] if (sq2 is None or c % 2 == 0)
                else sq2[:, 0:width])

    la0 = C.c["lda"]
    lb0 = C.c["ldb"]
    sa0 = C.c["sta"] if "sta" in C.c else C.bump("sta", 0)
    sb0 = C.c["stb"] if "stb" in C.c else C.bump("stb", 0)

    def ldsem(c):
        return C.sem("lda") if c % 2 == 0 else C.sem("ldb")

    def ldval(c):
        return (la0 if c % 2 == 0 else lb0) + 16 * (c // 2 + 1)

    def stsem(c):
        return C.sem("sta") if c % 2 == 0 else C.sem("stb")

    def stval(c):
        return (sa0 if c % 2 == 0 else sb0) + 16 * (c // 2 + 1)

    gg0 = C.c["gg"]

    def chsem(c):
        return C.sem("vv") if c % 2 == 0 else C.sem("gg")

    def chval(c):
        return (vv0 if c % 2 == 0 else gg0) + 2 * (c // 2 + 1)

    with nc.Block() as blk:
        @blk.sync
        def _(sync):
            for c in range(NDC):
                if mode == "post" and c >= depth:
                    sync.wait_ge(stsem(c - depth), stval(c - depth))
                elif c >= depth:
                    sync.wait_ge(chsem(c - depth), chval(c - depth))
                sync.dma_start(out=stg(c),
                               in_=x_dram[128 * c:128 * (c + 1), col0:col0 + width]
                               ).then_inc(ldsem(c), 16)

        def apply_chunks(eng, par, sem_name, atile, btile):
            for c in range(par, NDC, 2):
                eng.wait_ge(ldsem(c), ldval(c))
                if mode == "post" and depth == 2 and c >= depth:
                    eng.wait_ge(stsem(c - depth), stval(c - depth))
                eng.drain()
                if mode == "pre":
                    eng.tensor_tensor(sqb(c), stg(c), atile, op=ALU.mult)
                    eng.drain()
                    eng.tensor_tensor(out_bf[:, T * c + col0:T * c + col0 + width],
                                      sqb(c), btile, op=ALU.add
                                      ).then_inc(C.sem(sem_name), 2)
                else:
                    eng.tensor_tensor(sqb(c), ysrc(c), atile, op=ALU.mult)
                    eng.drain()
                    eng.tensor_tensor(sqb(c), sqb(c), btile, op=ALU.add)
                    eng.drain()
                    eng.tensor_tensor(stg(c), sqb(c), stg(c), op=ALU.add
                                      ).then_inc(C.sem(sem_name), 2)

        @blk.vector
        def _(v):
            v.wait_ge(C.sem("mm"), mm_done)
            apply_chunks(v, 0, "vv", psa[:, 0:width], psb[:, 0:width])

        @blk.gpsimd
        def _(g):
            g.wait_ge(C.sem("ev"), cp_done)
            apply_chunks(g, 1, "gg", acp, bcp)

        if mode == "post":
            @blk.scalar
            def _(s):
                for c in range(NDC):
                    s.wait_ge(chsem(c), chval(c))
                    s.dma_start(out=store[128 * c:128 * (c + 1), col0:col0 + width],
                                in_=stg(c)).then_inc(stsem(c), 16)
                if extra_store_wait:
                    s.wait_ge(C.sem("sta"), sa0 + 16 * (NDC // 2))
                    s.wait_ge(C.sem("stb"), sb0 + 16 * (NDC // 2))
    C.c["lda"] = la0 + 16 * (NDC // 2)
    C.c["ldb"] = lb0 + 16 * (NDC // 2)
    C.c["vv"] = vv0 + NDC
    C.c["gg"] = gg0 + NDC
    if mode == "post":
        C.c["sta"] = sa0 + 16 * (NDC // 2)
        C.c["stb"] = sb0 + 16 * (NDC // 2)



def _kbs_phase1(direction, qc):
    return list(range(0, qc // 4 + 1)) if direction == "fwd" else \
        list(range(qc // 4, 4))


def _kcs_phase2(direction, qb):
    return list(range(0, 4 * qb + 4)) if direction == "fwd" else \
        list(range(4 * qb, 16))


def emit_attn(C, layer, x_in, x_out, bs):
    nc = C.nc
    xn = C.t["XN"]
    scr = C.t["SCR16"]
    psa, psb = C.t["PSA"], C.t["PSB"]
    id32, idbf = C.t["id32"], C.t["idbf"]
    ones1f = C.t["ones1f"]
    tmp1 = C.t["TMP1"]

    qh = C.sb("QH", [128, 8192], BF16, bs)
    kh = C.sb("KH", [128, 8192], BF16, bs)
    vh = C.sb("VH", [128, 8320], BF16, bs)
    oh = C.sb("OH", [128, 8192], BF16, bs)
    pt4 = C.t["XST"][:, :].bitcast(BF16)
    C.t["OTMP"] = C.t["TMP1"][0:64, 1024:1280].bitcast(BF16)
    ptm = C.t["TMP1"][:, 0:512]
    ktm = C.t["TMP1"][:, 512:1024]
    roq = C.sb("ROQ", [128, 1024], BF16, bs)
    rok = C.sb("ROK", [128, 1024], BF16, bs)
    mhall = C.sb("MHALL", [128, 128], F32, bs)
    mqc = C.sb("MQC", [16, 8], F32, bs)
    m8 = C.sb("M8", [8, 16], F32, bs)
    mrows = C.sb("MROWS", [1, 160], F32, bs)
    mcols32 = C.sb("MCOLS32", [128, 32], F32, bs)
    biast = C.sb("BIAST", [128, 512], F32, bs)
    lrec = C.t["TMP1"][0:1, 0:512]
    lb = C.t["TMP1"][0:64, 512:1024]
    wsa = C.sb("WSA", [128, 4096], BF16, bs)
    wsb = C.sb("WSB", [128, 4096], BF16, bs)
    wsc = C.sb("WSC", [128, 4096], BF16, bs)
    qbra = qbrb = None
    if layer == 1:
        bqc = C.sb("BQC", [128, 8], F32, bs)
        bkc = C.sb("BKC", [128, 8], F32, bs)
        bvc = C.sb("BVC", [128, 8], F32, bs)
        bfc = C.sb("BFC", [128, 8], F32, bs)
        bvb = C.sb("BVB", [128, 8], BF16, bs)
        qbra = C.sb("QBRA", [1, 2048], BF16, bs)
        qbrb = C.sb("QBRB", [1, 2048], BF16, bs)
        bfold = C.sb("BFOLD", [128, 8], F32, bs)
        bfold2 = C.sb("BFOLD2", [128, 8], F32, bs)
    wq_d = C.t["b0_wq" if layer == 0 else "b2_wq"]
    wk_d = C.t["b0_wk" if layer == 0 else "b2_wk"]
    wv_d = C.t["b0_wv" if layer == 0 else "b2_wv"]
    wo_d = C.t["b0_wo" if layer == 0 else "b2_wf"]

    def load_w_half(g, dram, slot, half):
        for dc in range(NDC):
            g.dma_start(out=slot[:, 512 * dc:512 * (dc + 1)],
                        in_=dram[128 * dc:128 * (dc + 1), 512 * half:512 * (half + 1)]
                        ).then_inc(C.sem("gd"), 16)
        C.c["gd"] += 16 * 8

    def load_wo_half(g, slot, half):
        for rc in range(4):
            g.dma_start(out=slot[:, 1024 * rc:1024 * (rc + 1)],
                        in_=wo_d[512 * half + 128 * rc:512 * half + 128 * (rc + 1), :]
                        ).then_inc(C.sem("gd"), 16)
        C.c["gd"] += 16 * 4

    with nc.Block() as blk:
        @blk.gpsimd
        def _(g):
            load_w_half(g, wq_d, wsa, 0)
            load_w_half(g, wk_d, wsb, 0)
            load_w_half(g, wv_d, wsc, 0)
        if layer == 1:
            ld0 = C.c["ld"]

            @blk.sync
            def _(sync):
                for dst, nm in ((bqc, "b2_bq"), (bkc, "b2_bk"),
                                (bvc, "b2_bv"), (bfc, "b2_bf")):
                    sync.dma_start(out=dst[:, :], in_=C.t[nm][:, :]
                                   ).then_inc(C.sem("ld"), 16)

            @blk.vector
            def _(v):
                v.wait_ge(C.sem("ld"), ld0 + 64)
                v.tensor_scalar_mul(bqc[:, :], bqc[:, :], 0.125)
                v.tensor_copy(bvb[:, :], bvc[:, :])
            C.c["ld"] = ld0 + 64

    axst2 = qh[:, 0:4096].bitcast(F32)
    asq2 = kh[:, 0:4096].bitcast(F32)
    emit_ln_stats_from_dram(C, x_in, xst2=axst2, sq2=asq2)
    emit_ln_tail(C, T, 0, "pre", x_dram=x_in, out_bf=xn, xst2=axst2, sq2=asq2,
                 acp=vh[:, 0:4096].bitcast(F32), bcp=vh[:, 4096:8192].bitcast(F32))

    for half in range(2):
        direction_of = lambda hh: ("fwd" if (8 * half + hh) < 8 else "bwd") \
            if layer == 1 else "fwd"

        gd_need = C.c["gd"]
        if layer == 0:
            _qkv_l0(C, half, gd_need, qh, kh, vh, ptm, ktm, roq, rok,
                    wsa, wsb, wsc)
        else:
            _qkv_l1(C, half, gd_need, qh, kh, vh, ptm, wsa, wsb, wsc, bqc, bkc)

        woslot = wsc if half == 0 else wsa
        with nc.Block() as blk:
            @blk.gpsimd
            def _(g):
                if half == 0:
                    load_w_half(g, wq_d, wsa, 1)
                    load_w_half(g, wk_d, wsb, 1)
                load_wo_half(g, woslot, half)
        wo_gd_need = C.c["gd"]

        with nc.Block() as blk:
            @blk.vector
            def _(v):
                ap = vh.ap().rearrange("p (tc h j) -> p tc h j", h=8, j=65)
                v.memset(ap[:, :, :, 64:65], 1.0)

        _attn_half(C, layer, half, qh, kh, vh, oh, pt4,
                   mhall, mqc, m8, mrows, mcols32, biast, lrec, lb, qbra, qbrb)

        if layer == 1:
            _bfold(C, half, woslot, C.c["gd"], bvb, bfc, bfold, bfold2)

        _wo_proj(C, layer, half, woslot, C.c["gd"], oh, scr,
                 bfold if layer == 1 else None,
                 bfold2 if layer == 1 else None)

        if half == 0:
            with nc.Block() as blk:
                @blk.gpsimd
                def _(g):
                    load_w_half(g, wv_d, wsc, 1)

    def ysrc(c):
        return scr[:, T * c:T * (c + 1)]
    emit_ln_stats_mid(C, ysrc, T, src_dtype=BF16,
                      sq2=kh[:, 0:4096].bitcast(F32))
    emit_ln_tail(C, T, 0, "post", x_dram=x_in, ysrc=ysrc, store=x_out,
                 xst2=axst2, sq2=asq2,
                 acp=vh[:, 0:4096].bitcast(F32), bcp=vh[:, 4096:8192].bitcast(F32))


def emit_ln_stats_mid(C, src_ap_fn, width, src_dtype=F32, nchunk=NDC, sq2=None):
    nc = C.nc
    tmp1 = C.t["TMP1"]
    ones128f, ones128b = C.t["ones128f"], C.t["ones128b"]
    psa, psb = C.t["PSA"], C.t["PSB"]
    mm0 = C.c["mm"]; ev0 = C.c["ev"]
    nslice = width // 512
    ones = ones128f if src_dtype == F32 else ones128b
    vv_0 = C.c["vv"]; gg_0 = C.c["gg"]

    def sqb(c):
        return tmp1[:, 0:width] if (sq2 is None or c % 2 == 0) else sq2

    with nc.Block() as blk:
        @blk.vector
        def _(v):
            for c in range(0, nchunk, 2 if sq2 is not None else 1):
                if sq2 is not None and c >= 2:
                    v.wait_ge(C.sem("mm"), mm0 + c - 1)
                elif sq2 is None and c > 0:
                    v.wait_ge(C.sem("mm"), mm0 + c)
                v.drain()
                v.tensor_tensor(sqb(c), src_ap_fn(c), src_ap_fn(c),
                                op=ALU.mult).then_inc(C.sem("vv"), 1)

        if sq2 is not None:
            @blk.gpsimd
            def _(g):
                for c in range(1, nchunk, 2):
                    if c >= 2:
                        g.wait_ge(C.sem("mm"), mm0 + c - 1)
                    g.drain()
                    g.tensor_tensor(sqb(c), src_ap_fn(c), src_ap_fn(c),
                                    op=ALU.mult).then_inc(C.sem("gg"), 1)

        @blk.tensor
        def _(t):
            for c in range(nchunk):
                if sq2 is None or c % 2 == 0:
                    t.wait_ge(C.sem("vv"), vv_0 + (c // 2 if sq2 is not None
                                                   else c) + 1)
                else:
                    t.wait_ge(C.sem("gg"), gg_0 + c // 2 + 1)
                for s4 in range(nslice):
                    t.matmul(psa[0:1, 512 * s4:512 * (s4 + 1)], ones[:, :],
                             src_ap_fn(c)[:, 512 * s4:512 * (s4 + 1)],
                             start=(c == 0), stop=(c == nchunk - 1))
                for s4 in range(nslice):
                    mmi = t.matmul(psb[0:1, 512 * s4:512 * (s4 + 1)], ones128f[:, :],
                                   sqb(c)[:, 512 * s4:512 * (s4 + 1)],
                                   start=(c == 0), stop=(c == nchunk - 1))
                mmi.then_inc(C.sem("mm"), 1)
    if sq2 is not None:
        C.c["vv"] = vv_0 + (nchunk + 1) // 2
        C.c["gg"] = gg_0 + nchunk // 2
    else:
        C.c["vv"] = vv_0 + nchunk
    C.c["mm"] = mm0 + nchunk


def _qkv_l0(C, half, gd_need, qh, kh, vh, ptm, ktm, roq, rok,
            wsa, wsb, wsc):
    nc = C.nc
    xn = C.t["XN"]
    psa, psb = C.t["PSA"], C.t["PSB"]
    idbf = C.t["idbf"]
    cos, sin = C.t["cos"], C.t["sin"]
    tmp1 = C.t["TMP1"]
    mm0 = C.c["mm"]; ev0 = C.c["ev"]; vv0 = C.c["vv"]; gg0 = C.c["gg"]

    xstf = C.t["XST"]

    def pp(tc):
        return psa if tc % 2 == 0 else psb

    def tp(tc):
        return (psb if tc % 2 == 0 else psa)[:, :].bitcast(BF16)

    def ptmf(tc):
        return tmp1[:, 0:512] if tc % 2 == 0 else tmp1[:, 1024:1536]

    def ktmf(tc):
        return tmp1[:, 512:1024] if tc % 2 == 0 else tmp1[:, 1536:2048]

    def rq(tc):
        return roq[:, 512 * (tc % 2):512 * (tc % 2) + 512]

    def rk(tc):
        return rok[:, 512 * (tc % 2):512 * (tc % 2) + 512]

    def rope(eng, src, dst, tc, toff, cnt_sem):
        si = src.rearrange("p (h f two) -> p h f two", two=2, f=32)
        do = dst.rearrange("p (h f two) -> p h f two", two=2, f=32)
        ev_i, od_i = si[:, :, :, 0], si[:, :, :, 1]
        ev_o, od_o = do[:, :, :, 0], do[:, :, :, 1]
        sl = [xstf[:, toff + 256 * k:toff + 256 * (k + 1)].rearrange(
            "p (h f) -> p h f", f=32) for k in range(4)]
        cosb = bass.AP(cos, 32 * tc, [[512, 128], [0, 8], [1, 32]])
        sinb = bass.AP(sin, 32 * tc, [[512, 128], [0, 8], [1, 32]])
        eng.drain()
        eng.tensor_tensor(sl[0], ev_i, cosb, op=ALU.mult)
        eng.tensor_tensor(sl[1], od_i, sinb, op=ALU.mult)
        eng.tensor_tensor(sl[2], ev_i, sinb, op=ALU.mult)
        eng.tensor_tensor(sl[3], od_i, cosb, op=ALU.mult)
        eng.drain()
        eng.tensor_tensor(ev_o, sl[0], sl[1], op=ALU.subtract)
        eng.tensor_tensor(od_o, sl[2], sl[3], op=ALU.add
                          ).then_inc(C.sem(cnt_sem), 1)

    def emit_transp(t, tc):
        t.wait_ge(C.sem("vv"), vv0 + tc + 1)
        t.wait_ge(C.sem("gg"), gg0 + tc + 1)
        if tc >= 2:
            t.wait_ge(C.sem("ev"), ev0 + 8 * (tc - 2) + 8)
        for i in range(4):
            t.transpose(tp(tc)[:, 3072 + 128 * i:3072 + 128 * (i + 1)],
                        rq(tc)[:, 128 * i:128 * (i + 1)], idbf[:, :])
        tri_ = None
        for i in range(4):
            tri_ = t.transpose(tp(tc)[:, 3584 + 128 * i:3584 + 128 * (i + 1)],
                               rk(tc)[:, 128 * i:128 * (i + 1)], idbf[:, :])
        tri_.then_inc(C.sem("mm"), 1)

    with nc.Block() as blk:
        @blk.tensor
        def _(t):
            t.wait_ge(C.sem("gd"), gd_need)
            for tc in range(NTC):
                if tc >= 2:
                    t.wait_ge(C.sem("ev"), ev0 + 8 * tc - 12)
                for dc in range(NDC):
                    t.matmul(pp(tc)[:, 0:512],
                             xn[:, 2048 * dc + 128 * tc:2048 * dc + 128 * (tc + 1)],
                             wsa[:, 512 * dc:512 * (dc + 1)],
                             start=(dc == 0), stop=(dc == NDC - 1))
                for dc in range(NDC):
                    t.matmul(pp(tc)[:, 512:1024],
                             xn[:, 2048 * dc + 128 * tc:2048 * dc + 128 * (tc + 1)],
                             wsb[:, 512 * dc:512 * (dc + 1)],
                             start=(dc == 0), stop=(dc == NDC - 1))
                for dc in range(NDC):
                    mmi = t.matmul(pp(tc)[:, 1024:1536],
                                   xn[:, 2048 * dc + 128 * tc:2048 * dc + 128 * (tc + 1)],
                                   wsc[:, 512 * dc:512 * (dc + 1)],
                                   start=(dc == 0), stop=(dc == NDC - 1))
                mmi.then_inc(C.sem("mm"), 1)
                if tc > 0:
                    emit_transp(t, tc - 1)
            emit_transp(t, NTC - 1)

        @blk.scalar
        def _(s):
            for tc in range(NTC):
                s.wait_ge(C.sem("mm"), mm0 + (1 if tc == 0 else 2 * tc))
                if tc >= 2:
                    s.wait_ge(C.sem("vv"), vv0 + tc - 1)
                    s.wait_ge(C.sem("gg"), gg0 + tc - 1)
                s.activation(out=ptmf(tc), in_=pp(tc)[:, 0:512], func=AF.Copy,
                             scale=0.125)
                s.activation(out=ktmf(tc), in_=pp(tc)[:, 512:1024], func=AF.Copy)
                pvh = vh.ap().rearrange("p (tc h j) -> p tc h j", h=8, j=65)
                psv = pp(tc).ap().rearrange("p (b h j) -> p b h j", b=4, j=64)[:, 2]
                s.activation(out=pvh[:, tc, :, 0:64], in_=psv, func=AF.Copy
                             ).then_inc(C.sem("ev"), 4)
                s.wait_ge(C.sem("mm"),
                          mm0 + (2 * tc + 3 if tc < NTC - 1 else 2 * NTC))
                s.copy(bass.AP(qh, 128 * tc, [[8192, 128], [2048, 4], [1, 128]]),
                       tp(tc)[:, 3072:3584])
                s.copy(bass.AP(kh, 128 * tc, [[8192, 128], [2048, 4], [1, 128]]),
                       tp(tc)[:, 3584:4096]).then_inc(C.sem("ev"), 4)

        @blk.vector
        def _(v):
            for tc in range(NTC):
                v.wait_ge(C.sem("ev"), ev0 + 8 * tc + 4)
                rope(v, ptmf(tc), rq(tc), tc, 0, "vv")

        @blk.gpsimd
        def _(g):
            for tc in range(NTC):
                g.wait_ge(C.sem("ev"), ev0 + 8 * tc + 4)
                rope(g, ktmf(tc), rk(tc), tc, 1024, "gg")
    C.c["mm"] = mm0 + 2 * NTC
    C.c["ev"] = ev0 + 8 * NTC
    C.c["vv"] = vv0 + NTC
    C.c["gg"] = gg0 + NTC


def _qkv_l1(C, half, gd_need, qh, kh, vh, ptm, wsa, wsb, wsc, bqc, bkc):
    nc = C.nc
    xn = C.t["XN"]
    psa = C.t["PSA"]
    mm0 = C.c["mm"]; vv0 = C.c["vv"]; ev0 = C.c["ev"]

    with nc.Block() as blk:
        @blk.tensor
        def _(t):
            t.wait_ge(C.sem("gd"), gd_need)
            idx = 0
            for proj, ws in ((0, wsa), (1, wsb)):
                for mc4 in range(4):
                    for ts4 in range(4):
                        if idx >= 4:
                            t.wait_ge(C.sem("vv"), vv0 + idx - 3)
                        for dc in range(NDC):
                            mmi = t.matmul(
                                psa[:, 512 * (idx % 4):512 * (idx % 4 + 1)],
                                wsa[:, 512 * dc + 128 * mc4:512 * dc + 128 * (mc4 + 1)]
                                if proj == 0 else
                                wsb[:, 512 * dc + 128 * mc4:512 * dc + 128 * (mc4 + 1)],
                                xn[:, 2048 * dc + 512 * ts4:2048 * dc + 512 * (ts4 + 1)],
                                start=(dc == 0), stop=(dc == NDC - 1))
                        mmi.then_inc(C.sem("mm"), 1)
                        idx += 1

        @blk.vector
        def _(v):
            idx = 0
            for proj in range(2):
                for mc4 in range(4):
                    for ts4 in range(4):
                        v.wait_ge(C.sem("mm"), mm0 + idx + 1)
                        dst = qh if proj == 0 else kh
                        bias = bqc if proj == 0 else bkc
                        if proj == 0:
                            v.tensor_scalar(
                                dst[:, 2048 * mc4 + 512 * ts4:2048 * mc4 + 512 * (ts4 + 1)],
                                psa[:, 512 * (idx % 4):512 * (idx % 4 + 1)],
                                0.125, bias[:, 4 * half + mc4:4 * half + mc4 + 1],
                                op0=ALU.mult, op1=ALU.add).then_inc(C.sem("vv"), 1)
                        else:
                            v.tensor_scalar(
                                dst[:, 2048 * mc4 + 512 * ts4:2048 * mc4 + 512 * (ts4 + 1)],
                                psa[:, 512 * (idx % 4):512 * (idx % 4 + 1)],
                                bias[:, 4 * half + mc4:4 * half + mc4 + 1], None,
                                op0=ALU.add).then_inc(C.sem("vv"), 1)
                        idx += 1
    C.c["mm"] = mm0 + 32
    C.c["vv"] = vv0 + 32

    mm0 = C.c["mm"]; ev0 = C.c["ev"]
    with nc.Block() as blk:
        @blk.tensor
        def _(t):
            for tc in range(NTC):
                if tc > 1:
                    t.wait_ge(C.sem("ev"), ev0 + tc - 1)
                for dc in range(NDC):
                    mmi = t.matmul(
                        psa[:, 512 * (tc % 2):512 * (tc % 2 + 1)],
                        xn[:, 2048 * dc + 128 * tc:2048 * dc + 128 * (tc + 1)],
                        wsc[:, 512 * dc:512 * (dc + 1)],
                        start=(dc == 0), stop=(dc == NDC - 1))
                mmi.then_inc(C.sem("mm"), 1)

        @blk.scalar
        def _(s):
            pvh = vh.ap().rearrange("p (tc h j) -> p tc h j", h=8, j=65)
            for tc in range(NTC):
                s.wait_ge(C.sem("mm"), mm0 + tc + 1)
                psv = psa.ap().rearrange("p (b h j) -> p b h j", b=4, j=64)[:, tc % 2]
                s.activation(out=pvh[:, tc, :, 0:64], in_=psv, func=AF.Copy
                             ).then_inc(C.sem("ev"), 1)
    C.c["mm"] = mm0 + NTC
    C.c["ev"] = ev0 + NTC


def _tile_geom(direction, qb, kc):
    m = kc - 4 * qb
    if direction == "fwd":
        if m < 0:
            return 0, 512, None
        return 128 * m, 512, 128 * m
    else:
        if m > 3:
            return 0, 512, None
        return 0, 128 * m + 128, 128 * m


def _attn_half(C, layer, half, qh, kh, vh, oh, pt4,
               mhall, mqc, m8, mrows, mcols32, biast, lrec, lb, qbra, qbrb):
    nc = C.nc
    psa, psb = C.t["PSA"], C.t["PSB"]
    id32 = C.t["id32"]
    ones1f = C.t["ones1f"]
    ones1b = C.t["ones1b"]

    def direction(hh):
        return ("fwd" if (8 * half + hh) < 8 else "bwd") if layer == 1 else "fwd"

    qsq = pt4[:, 0:2048]
    ksq = pt4[:, 2048:4096]
    mm0 = C.c["mm"]; vv0 = C.c["vv"]; ev0 = C.c["ev"]
    with nc.Block() as blk:
        @blk.scalar
        def _(s):
            for ch in range(4):
                if ch > 0:
                    s.wait_ge(C.sem("mm"), mm0 + 2 * ch)
                    s.drain()
                s.activation(out=qsq, in_=qh[:, 2048 * ch:2048 * (ch + 1)],
                             func=AF.Square)
                s.activation(out=ksq, in_=kh[:, 2048 * ch:2048 * (ch + 1)],
                             func=AF.Square).then_inc(C.sem("ev"), 1)

        @blk.tensor
        def _(t):
            ones128b = C.t["ones128b"]
            for hh in range(8):
                hb = hh % 2
                t.wait_ge(C.sem("ev"), ev0 + hh // 2 + 1)
                if hh >= 1:
                    t.wait_ge(C.sem("vv"), vv0 + hh)
                for s4 in range(4):
                    t.matmul(psa[0:1, 512 * s4:512 * (s4 + 1)],
                             ones128b[64 * hb:64 * (hb + 1), :],
                             qsq[64 * hb:64 * (hb + 1), 512 * s4:512 * (s4 + 1)],
                             start=True, stop=True)
                mmi = None
                for s4 in range(4):
                    mmi = t.matmul(psb[0:1, 512 * s4:512 * (s4 + 1)],
                                   ones128b[64 * hb:64 * (hb + 1), :],
                                   ksq[64 * hb:64 * (hb + 1), 512 * s4:512 * (s4 + 1)],
                                   start=True, stop=True)
                mmi.then_inc(C.sem("mm"), 1)

        @blk.vector
        def _(v):
            for hh in range(8):
                v.wait_ge(C.sem("mm"), mm0 + hh + 1)
                v.reduce_max(mrows[0:1, 4 * hh:4 * (hh + 1)],
                             psa[0:1, 0:2048].rearrange(
                                 "p (a b) -> p a b", b=512), axis=AX.X)
                v.reduce_max(mrows[0:1, 32 + hh:33 + hh],
                             psb[0:1, 0:2048].rearrange(
                                 "p (a b) -> p a b", b=2048), axis=AX.X
                             ).then_inc(C.sem("vv"), 1)
    C.c["mm"] = mm0 + 8
    C.c["vv"] = vv0 + 8
    C.c["ev"] = ev0 + 4

    vv0 = C.c["vv"]; ev0 = C.c["ev"]; mm0 = C.c["mm"]
    with nc.Block() as blk:
        @blk.vector
        def _(v):
            v.tensor_tensor(mrows[0:1, 64:96], mrows[0:1, 0:32],
                            bass.AP(mrows, 32, [[160, 1], [1, 8], [0, 4]]),
                            op=ALU.mult).then_inc(C.sem("vv"), 1)

        @blk.scalar
        def _(s):
            s.wait_ge(C.sem("vv"), vv0 + 1)
            s.activation(out=mrows[0:1, 96:128], in_=mrows[0:1, 64:96], func=AF.Ln)
            s.drain()
            s.activation(out=mrows[0:1, 128:160], in_=mrows[0:1, 96:128],
                         func=AF.Exp, scale=0.5).then_inc(C.sem("ev"), 1)
    C.c["vv"] = vv0 + 1
    C.c["ev"] = ev0 + 1

    vv0 = C.c["vv"]; ev0 = C.c["ev"]; mm0 = C.c["mm"]
    with nc.Block() as blk:
        @blk.vector
        def _(v):
            v.wait_ge(C.sem("ev"), ev0)
            v.tensor_scalar_add(mrows[0:1, 128:160], mrows[0:1, 128:160], 1.0
                                ).then_inc(C.sem("vv"), 1)

        @blk.tensor
        def _(t):
            t.wait_ge(C.sem("vv"), vv0 + 1)
            t.matmul(psb[:, 256:288], ones1f[:, :], mrows[0:1, 128:160],
                     start=True, stop=True).then_inc(C.sem("mm"), 1)
    C.c["mm"] = mm0 + 1
    C.c["vv"] = vv0 + 1

    mm0 = C.c["mm"]; vv0 = C.c["vv"]
    with nc.Block() as blk:
        @blk.vector
        def _(v):
            v.wait_ge(C.sem("mm"), mm0)
            v.tensor_copy(mcols32[:, :], psb[:, 256:288])
            v.drain()
            if layer == 0:
                v.tensor_scalar_mul(biast[:, 0:32], mcols32[:, :], -1.0
                                    ).then_inc(C.sem("vv"), 1)
            else:
                slopek = C.t["slopek"]
                vvi = None
                for hh in range(8):
                    h = 8 * half + hh
                    for qb in range(NQB):
                        vvi = v.tensor_scalar(
                            biast[:, 64 * hh + 16 * qb:64 * hh + 16 * (qb + 1)],
                            slopek[:, 16 * h:16 * (h + 1)],
                            mcols32[:, 4 * hh + qb:4 * hh + qb + 1], None,
                            op0=ALU.subtract)
                vvi.then_inc(C.sem("vv"), 1)
    C.c["vv"] = vv0 + 1

    iters = []
    for hh in range(8):
        for qb in range(NQB):
            kcs = _kcs_phase2(direction(hh), qb)
            geoms = [_tile_geom(direction(hh), qb, kc) for kc in kcs]
            iters.append((hh, qb, kcs, geoms))
    idx_last = {}
    for ii, (hh, qb, kcs, geoms) in enumerate(iters):
        idx_last[hh] = ii
    mm0 = C.c["mm"]; vv0 = C.c["vv"]; ev0 = C.c["ev"]; st0 = C.c["st"]
    ld0 = C.c["ld"]
    vtile = []
    viter = []
    stile = []
    siter = []
    vc = 0; sc = 0
    ntl = 0
    lrec_s = []
    iter_last = []
    for (hh, qb, kcs, geoms) in iters:
        for (lo, hi, tri0) in geoms:
            if tri0 is not None:
                vc += 1
            if (lo, hi) != (0, 512):
                vc += 1
            vtile.append(vc)
            sc += 1
            stile.append(sc)
            ntl += 1
        iter_last.append(ntl - 1)
        vc += 1
        viter.append(vc)
        sc += 1
        lrec_s.append(sc)
        sc += 1
        siter.append(sc)
    LOOK0 = 3
    mm_st = [0] * ntl
    mm_pv = [0] * ntl
    mm_bc = [0] * len(iters)
    last_set = set(iter_last)
    last_to_iter = {tgl: ii for ii, tgl in enumerate(iter_last)}
    mc = 0
    for tg in range(min(LOOK0, ntl)):
        mc += 1
        mm_st[tg] = mc
    for tg in range(ntl):
        if tg + LOOK0 < ntl:
            mc += 1
            mm_st[tg + LOOK0] = mc
        mc += 1
        mm_pv[tg] = mc
        if tg in last_set:
            mc += 1
            mm_bc[last_to_iter[tg]] = mc

    with nc.Block() as blk:
        @blk.sync
        def _(sync):
            if layer == 1:
                k = 0
                seen = set()
                for ii, (hh, qb, kcs, geoms) in enumerate(iters):
                    if hh not in seen:
                        seen.add(hh)
                        h = 8 * half + hh
                        if hh >= 2:
                            sync.wait_ge(C.sem("ev"), ev0 + siter[idx_last[hh - 2]])
                        sync.dma_start(out=(qbra if hh % 2 == 0 else qbrb)[0:1, :],
                                       in_=C.t["c_qbias_bf"][h:h + 1, :]
                                       ).then_inc(C.sem("ld"), 16)
                        k += 1
                        sync.wait_ge(C.sem("ld"), ld0 + 16 * k)

        tiles = []
        for ii, (hh, qb, kcs, geoms) in enumerate(iters):
            for i, kc in enumerate(kcs):
                tiles.append((ii, hh, qb, i, kc, len(kcs)))
        LOOK = 3

        @blk.tensor
        def _(t):
            def emit_st(tg):
                ii, hh, qb, i, kc, nk = tiles[tg]
                ch, r0 = hh // 2, 64 * (hh % 2)
                qbr = qbra if hh % 2 == 0 else qbrb
                if layer == 1 and qb == 0 and i == 0:
                    t.wait_ge(C.sem("ld"), ld0 + 16 * (hh + 1))
                par = tg % 3
                if tg >= 3:
                    t.wait_ge(C.sem("ev"), ev0 + stile[tg - 3])
                bank = psa[:, 512 * par:512 * (par + 1)]
                mmi = t.matmul(
                    bank,
                    kh[r0:r0 + 64, 2048 * ch + 128 * kc:2048 * ch + 128 * (kc + 1)],
                    qh[r0:r0 + 64, 2048 * ch + 512 * qb:2048 * ch + 512 * (qb + 1)],
                    start=True, stop=(layer == 0))
                if layer == 1:
                    mmi = t.matmul(bank, ones1b[0:1, :],
                                   qbr[0:1, 512 * qb:512 * (qb + 1)],
                                   start=False, stop=True)
                mmi.then_inc(C.sem("mm"), 1)

            def emit_pv(tg):
                ii, hh, qb, i, kc, nk = tiles[tg]
                par = tg % 3
                t.wait_ge(C.sem("ev"), ev0 + stile[tg])
                if ii >= 2 and i == 0:
                    t.wait_ge(C.sem("vv"), vv0 + viter[ii - 2])
                t.matmul(psb[0:65, 512 * (ii % 2):512 * (ii % 2) + 512],
                         vh[:, 520 * kc + 65 * hh:520 * kc + 65 * (hh + 1)],
                         pt4[:, 512 * par:512 * (par + 1)],
                         start=(i == 0), stop=(i == nk - 1)
                         ).then_inc(C.sem("mm"), 1)
                if i == nk - 1:
                    t.wait_ge(C.sem("ev"), ev0 + lrec_s[ii])
                    t.matmul(psa[0:64, 2048 - 512:2048], ones1f[0:1, 0:64],
                             lrec[0:1, :], start=True, stop=True
                             ).then_inc(C.sem("mm"), 1)

            for tg in range(min(LOOK, len(tiles))):
                emit_st(tg)
            for tg in range(len(tiles)):
                if tg + LOOK < len(tiles):
                    emit_st(tg + LOOK)
                emit_pv(tg)

        @blk.vector
        def _(v):
            tg = 0
            for ii, (hh, qb, kcs, geoms) in enumerate(iters):
                ch, r0 = hh // 2, 64 * (hh % 2)
                tri = C.t["tri_fwd"] if direction(hh) == "fwd" else C.t["tri_bwd"]
                for i, kc in enumerate(kcs):
                    par = tg % 3
                    lo, hi, tri0 = geoms[i]
                    if tri0 is not None:
                        v.wait_ge(C.sem("mm"), mm0 + mm_st[tg])
                        v.tensor_tensor(
                            psa[:, 512 * par + tri0:512 * par + tri0 + 128],
                            psa[:, 512 * par + tri0:512 * par + tri0 + 128],
                            tri[:, :], op=ALU.add).then_inc(C.sem("vv"), 1)
                    if (lo, hi) != (0, 512):
                        dead = (0, lo) if direction(hh) == "fwd" else (hi, 512)
                        if tg >= 3:
                            v.wait_ge(C.sem("mm"), mm0 + mm_pv[tg - 3])
                        v.memset(pt4[:, 512 * par + dead[0]:512 * par + dead[1]],
                                 0.0).then_inc(C.sem("vv"), 1)
                    tg += 1
                v.wait_ge(C.sem("ev"), ev0 + siter[ii])
                if hh % 2 == 0:
                    v.tensor_tensor(
                        oh[0:64, 2048 * ch + 512 * qb:2048 * ch + 512 * (qb + 1)],
                        psb[0:64, 512 * (ii % 2):512 * (ii % 2) + 512],
                        lb[:, :], op=ALU.mult).then_inc(C.sem("vv"), 1)
                else:
                    if ii >= 1:
                        v.wait_ge(C.sem("st"), st0 + 16 * _nodd(iters, ii))
                    v.tensor_tensor(
                        C.t["OTMP"], psb[0:64, 512 * (ii % 2):512 * (ii % 2) + 512],
                        lb[:, :], op=ALU.mult).then_inc(C.sem("vv"), 1)

        @blk.scalar
        def _(s):
            tg = 0
            nodd = 0
            for ii, (hh, qb, kcs, geoms) in enumerate(iters):
                ch = hh // 2
                for i, kc in enumerate(kcs):
                    par = tg % 3
                    lo, hi, tri0 = geoms[i]
                    s.wait_ge(C.sem("mm"), mm0 + mm_st[tg])
                    s.wait_ge(C.sem("vv"), vv0 + vtile[tg])
                    if tg >= 3:
                        s.wait_ge(C.sem("mm"), mm0 + mm_pv[tg - 3])
                    bias = biast[:, 4 * hh + qb:4 * hh + qb + 1] if layer == 0 \
                        else biast[:, 64 * hh + 16 * qb + kc:64 * hh + 16 * qb + kc + 1]
                    s.activation(out=pt4[:, 512 * par + lo:512 * par + hi],
                                 in_=psa[:, 512 * par + lo:512 * par + hi],
                                 func=AF.Exp, bias=bias).then_inc(C.sem("ev"), 1)
                    tg += 1
                s.wait_ge(C.sem("mm"), mm0 + mm_pv[tg - 1])
                s.drain()
                s.activation(out=lrec[0:1, :],
                             in_=psb[64:65, 512 * (ii % 2):512 * (ii % 2) + 512],
                             func=AF.Ln)
                s.drain()
                s.activation(out=lrec[0:1, :], in_=lrec[0:1, :], func=AF.Exp,
                             scale=-1.0).then_inc(C.sem("ev"), 1)
                s.wait_ge(C.sem("mm"), mm0 + mm_bc[ii])
                if ii >= 1:
                    s.wait_ge(C.sem("vv"), vv0 + viter[ii - 1])
                s.copy(lb[:, :], psa[0:64, 2048 - 512:2048]).then_inc(C.sem("ev"), 1)
                if hh % 2 == 1:
                    s.wait_ge(C.sem("vv"), vv0 + viter[ii])
                    s.dma_start(
                        out=oh[64:128, 2048 * ch + 512 * qb:2048 * ch + 512 * (qb + 1)],
                        in_=C.t["OTMP"]).then_inc(C.sem("st"), 16)
                    nodd += 1
    C.c["mm"] = mm0 + mm_bc[-1]
    C.c["vv"] = vv0 + viter[-1]
    C.c["ev"] = ev0 + siter[-1]
    C.c["st"] = st0 + 16 * sum(1 for (hh, qb, k, g) in iters if hh % 2 == 1)
    if layer == 1:
        C.c["ld"] = ld0 + 16 * 8


def _nodd(iters, ii):
    return sum(1 for (hh, qb, k, g) in iters[:ii] if hh % 2 == 1)


def _bfold(C, half, wso, wo_gd_need, bvb, bfc, bfold, bfold2):
    nc = C.nc
    psa = C.t["PSA"]
    mm0 = C.c["mm"]
    with nc.Block() as blk:
        @blk.tensor
        def _(t):
            t.wait_ge(C.sem("gd"), wo_gd_need)
            for mc in range(8):
                for rc in range(4):
                    mmi = t.matmul(psa[:, 1024 + mc:1024 + mc + 1],
                                   wso[:, 1024 * rc + 128 * mc:1024 * rc + 128 * (mc + 1)],
                                   bvb[:, 4 * half + rc:4 * half + rc + 1],
                                   start=(rc == 0), stop=(rc == 3))
            mmi.then_inc(C.sem("mm"), 1)

        @blk.vector
        def _(v):
            v.wait_ge(C.sem("mm"), mm0 + 1)
            if half == 0:
                v.tensor_tensor(bfold[:, :], psa[:, 1024:1032], bfc[:, :], op=ALU.add)
            else:
                v.tensor_copy(bfold2[:, :], psa[:, 1024:1032])
    C.c["mm"] = mm0 + 1


def _wo_proj(C, layer, half, wso, wo_gd_need, oh, scr, bfold, bfold2):
    nc = C.nc
    psa = C.t["PSA"]
    mm0 = C.c["mm"]; vv0 = C.c["vv"]; ev0 = C.c["ev"]
    use_scalar = (layer == 0 and half == 0)
    st_need = C.c["st"]
    with nc.Block() as blk:
        @blk.tensor
        def _(t):
            t.wait_ge(C.sem("gd"), wo_gd_need)
            t.wait_ge(C.sem("st"), st_need)
            idx = 0
            for qq in range(4):
                for mc in range(8):
                    if idx >= 2:
                        if use_scalar:
                            t.wait_ge(C.sem("ev"), ev0 + idx - 1)
                        else:
                            t.wait_ge(C.sem("vv"), vv0 + idx - 1)
                    for rc in range(4):
                        mmi = t.matmul(
                            psa[:, 512 * (idx % 2):512 * (idx % 2 + 1)],
                            wso[:, 1024 * rc + 128 * mc:1024 * rc + 128 * (mc + 1)],
                            oh[:, 2048 * rc + 512 * qq:2048 * rc + 512 * (qq + 1)],
                            start=(rc == 0), stop=(rc == 3))
                    mmi.then_inc(C.sem("mm"), 1)
                    idx += 1

        if use_scalar:
            @blk.scalar
            def _(s):
                idx = 0
                for qq in range(4):
                    for mc in range(8):
                        s.wait_ge(C.sem("mm"), mm0 + idx + 1)
                        s.copy(scr[:, 2048 * mc + 512 * qq:2048 * mc + 512 * (qq + 1)],
                               psa[:, 512 * (idx % 2):512 * (idx % 2 + 1)]
                               ).then_inc(C.sem("ev"), 1)
                        idx += 1
        else:
            @blk.vector
            def _(v):
                idx = 0
                for qq in range(4):
                    for mc in range(8):
                        v.wait_ge(C.sem("mm"), mm0 + idx + 1)
                        dst = scr[:, 2048 * mc + 512 * qq:2048 * mc + 512 * (qq + 1)]
                        src = psa[:, 512 * (idx % 2):512 * (idx % 2 + 1)]
                        if half == 0:
                            v.tensor_scalar(dst, src,
                                            bfold[:, mc:mc + 1], None, op0=ALU.add
                                            ).then_inc(C.sem("vv"), 1)
                        elif layer == 1:
                            v.tensor_tensor(dst, src, dst, op=ALU.add)
                            v.drain()
                            v.tensor_scalar(dst, dst, bfold2[:, mc:mc + 1],
                                            None, op0=ALU.add
                                            ).then_inc(C.sem("vv"), 1)
                        else:
                            v.tensor_tensor(dst, src, dst, op=ALU.add
                                            ).then_inc(C.sem("vv"), 1)
                        idx += 1
    C.c["mm"] = mm0 + 32
    if use_scalar:
        C.c["ev"] = ev0 + 32
    else:
        C.c["vv"] = vv0 + 32



_DRAM_SPECS = [
    ("b0_wq", [D, D], F32), ("b0_wk", [D, D], F32), ("b0_wv", [D, D], F32),
    ("b0_wo", [D, D], F32),
    ("b1_wgT", [D, 8], F32), ("b1_eb", [1, 8], F32),
    ("b1_sw1T", [D, D], F32), ("b1_sw3T", [D, D], F32), ("b1_sw2T", [D, D], F32),
    ("b1_w1", [8 * 1024, 512], F32), ("b1_w3", [8 * 1024, 512], F32),
    ("b1_w2", [8 * 512, 1024], F32),
    ("b2_wq", [D, D], F32), ("b2_wk", [D, D], F32), ("b2_wv", [D, D], F32),
    ("b2_wf", [D, D], F32),
    ("b2_bq", [128, 8], F32), ("b2_bk", [128, 8], F32), ("b2_bv", [128, 8], F32),
    ("b2_bf", [128, 8], F32),
    ("b3_gwT", [D, 8], F32),
    ("b3_wegT", [8 * 1024, 1024], F32), ("b3_weuT", [8 * 1024, 1024], F32),
    ("b3_wedT", [8 * 1024, 1024], F32),
    ("b3_wsgT", [D, 2048], F32), ("b3_wsuT", [D, 2048], F32),
    ("b3_wsdT", [2048, D], F32),
    ("c_id32", [128, 128], F32), ("c_idbf", [128, 128], BF16),
    ("c_tri_fwd", [128, 128], F32), ("c_tri_bwd", [128, 128], F32),
    ("c_cos", [128, 512], F32), ("c_sin", [128, 512], F32),
    ("c_slopek", [128, HEADS * 16], F32),
    ("c_ustrict", [128, 128], F32), ("c_u16", [16, 16], F32),
    ("c_qbias", [HEADS, T], F32),
    ("c_qbias_bf", [HEADS, T], BF16),
    ("c_pos", [128, 32], F32), ("c_maskcol", [8, 1], F32),
    ("c_coreflag", [128, 1], F32),
]

_SB_CONSTS = [
    ("id32", "c_id32", [128, 128], F32),
    ("idbf", "c_idbf", [128, 128], BF16),
    ("tri_fwd", "c_tri_fwd", [128, 128], F32),
    ("tri_bwd", "c_tri_bwd", [128, 128], F32),
    ("cos", "c_cos", [128, 512], F32),
    ("sin", "c_sin", [128, 512], F32),
    ("slopek", "c_slopek", [128, HEADS * 16], F32),
    ("ustrict", "c_ustrict", [128, 128], F32),
    ("u16", "c_u16", [16, 16], F32),
    ("pos", "c_pos", [128, 32], F32),
    ("maskcol", "c_maskcol", [8, 1], F32),
    ("coreflag", "c_coreflag", [128, 1], F32),
    ("ebrow", "b1_eb", [1, 8], F32),
]


def _emit_init(C):
    nc = C.nc
    C.sb("XN", [128, 16384], BF16)
    C.sb("SCR16", [128, 16384], BF16)
    C.sb("XST", [128, 2048], F32)
    C.sb("TMP1", [128, 2048], F32)
    C.sb("STT", [128, 64], F32)
    C.sb("MEAN", [128, 16], F32)
    C.sb("VAR", [128, 16], F32)
    C.sb("RSTD", [128, 16], F32)
    C.sb("BCOL", [128, 16], F32)
    C.sb("ones1f", [1, 128], F32)
    C.sb("ones128f", [128, 1], F32)
    C.sb("ones128b", [128, 1], BF16)
    C.sb("ones1b", [1, 128], BF16)
    for nm, dn, shape, dt in _SB_CONSTS:
        C.sb(nm, shape, dt)
    C.ps("PSA", [128, 2048], F32)
    C.ps("PSB", [128, 2048], F32)
    ld0 = C.c.get("ld", 0)
    n = len(_SB_CONSTS)
    with nc.Block() as blk:
        @blk.sync
        def _(sync):
            for nm, dn, shape, dt in _SB_CONSTS:
                sync.dma_start(out=C.t[nm][:, :], in_=C.t[dn][:, :]
                               ).then_inc(C.sem("ld"), 16)
            sync.wait_ge(C.sem("ld"), ld0 + 16 * n)

        @blk.gpsimd
        def _(g):
            g.memset(C.t["ones1f"][:, :], 1.0)
            g.memset(C.t["ones128f"][:, :], 1.0)
            g.memset(C.t["ones128b"][:, :], 1.0)
            g.memset(C.t["ones1b"][:, :], 1.0)
    C.c["ld"] = ld0 + 16 * n


def _emit_final_wait(C):
    nc = C.nc
    with nc.Block() as blk:
        @blk.sync
        def _(sync):
            sync.wait_ge(C.sem("st"), C.c["st"])
            sync.wait_ge(C.sem("ld"), C.c["ld"])
            sync.wait_ge(C.sem("gd"), C.c["gd"])
            sync.wait_ge(C.sem("sta"), C.c["sta"])
            sync.wait_ge(C.sem("stb"), C.c["stb"])
            sync.wait_ge(C.sem("lda"), C.c["lda"])
            sync.wait_ge(C.sem("ldb"), C.c["ldb"])
            if "swl" in C.c:
                sync.wait_ge(C.sem("swl"), C.c["swl"])


def _build_program(blocks):
    nc = bass.Bass("TRN2", target_bir_lowering=False)
    dram = {}
    for nm, shape, dt in _DRAM_SPECS:
        dram[nm] = nc.dram_tensor(nm, shape, dt, kind="ExternalInput")
    x_in = nc.dram_tensor("x_in", [D, T], F32, kind="ExternalInput")
    xi = [x_in]
    for i in range(len(blocks) - 1):
        xi.append(nc.dram_tensor(f"xmid{i}", [D, T], F32))
    last_tm = (blocks[-1] == "f1")
    if last_tm:
        x_out = nc.dram_tensor("x_out", [T, D], F32, kind="ExternalOutput")
    else:
        x_out = nc.dram_tensor("x_out", [D, T], F32, kind="ExternalOutput")
    xi.append(x_out)

    with ExitStack() as stack:
        C = Ctx(nc, stack)
        for nm in dram:
            C.t[nm] = dram[nm]
        for s in ("ld", "gd", "st", "mm", "ev", "vv", "gg", "cc", "lda", "ldb", "sta", "stb"):
            C.sem(s)
        _emit_init(C)
        for i, b in enumerate(blocks):
            with ExitStack() as bs:
                if b == "a0":
                    emit_attn(C, 0, xi[i], xi[i + 1], bs)
                elif b == "a1":
                    emit_attn(C, 1, xi[i], xi[i + 1], bs)
                elif b == "f0":
                    emit_ffn0(C, xi[i], xi[i + 1], bs)
                elif b == "f1":
                    emit_ffn1(C, xi[i], xi[i + 1], bs)
        if _DEBUG_DUMP:
            dbg = {}
            for nm in _DEBUG_DUMP:
                t_ = C.t[nm]
                dbg[nm] = nc.dram_tensor("dbg_" + nm, list(t_.shape),
                                         t_.dtype, kind="ExternalOutput")
            st0 = C.c["st"]
            with nc.Block() as blk:
                @blk.sync
                def _(sync):
                    for nm in _DEBUG_DUMP:
                        sync.dma_start(out=dbg[nm].ap(), in_=C.t[nm][:, :]
                                       ).then_inc(C.sem("st"), 16)
                    sync.wait_ge(C.sem("st"), st0 + 16 * len(_DEBUG_DUMP))
            C.c["st"] = st0 + 16 * len(_DEBUG_DUMP)
        _emit_final_wait(C)
    return nc


_BLOCK_ORDER = ["a0", "f0", "a1", "f1"]


def kernel(**inputs):
    maps = _host_inputs(inputs)
    if _MODE == "fused":
        nc = _build_program(_BLOCK_ORDER)
        res = run_bass_kernel_spmd(nc, maps, core_ids=list(range(NCORES)))
        return np.stack([np.asarray(res.results[c]["x_out"], np.float32)
                         for c in range(NCORES)], axis=0)
    cur = [m["x_in"] for m in maps]
    for b in _BLOCK_ORDER:
        nc = _build_program([b])
        in_maps = []
        for c in range(NCORES):
            m = dict(maps[c])
            m["x_in"] = np.ascontiguousarray(cur[c])
            in_maps.append(m)
        res = run_bass_kernel_spmd(nc, in_maps, core_ids=list(range(NCORES)))
        cur = [np.asarray(res.results[c]["x_out"], np.float32)
               for c in range(NCORES)]
    return np.stack(cur, axis=0)




def _emit_gate_softmax(C, bs, wgT_name, with_groups):
    nc = C.nc
    xn = C.t["XN"]
    psa, psb = C.t["PSA"], C.t["PSB"]
    id32 = C.t["id32"]
    ones1f = C.t["ones1f"]
    wgt = C.sb("WGT", [128, 64], BF16, bs)
    lg8 = C.t["TMP1"]
    s0 = C.sb("S0T", [128, 128], F32, bs)
    sb_ = C.sb("SBT", [128, 128], F32, bs)
    e1 = C.sb("E1T", [128, 128], F32, bs)
    e2 = C.sb("E2T", [128, 128], F32, bs)
    sm = C.sb("SMT", [128, 128], F32, bs)
    gsc = C.sb("GSC", [128, 64], F32, bs)
    sml = C.sb("SML", [128, 96], F32, bs)
    ebb = C.sb("EBB", [128, 8], F32, bs)

    def smlb(col, inner):
        return bass.AP(sml, col, [[96, 128], [1, 16], [0, inner]])

    def t3(ap_):
        return ap_.rearrange("p (t e) -> p t e", e=8)

    gd0 = C.c["gd"]
    with nc.Block() as blk:
        @blk.gpsimd
        def _(g):
            g.dma_start(out=wgt[:, :],
                        in_=_wview(C.t[wgT_name], 8, 8)
                        ).then_inc(C.sem("gd"), 16)
    C.c["gd"] = gd0 + 16

    mm0 = C.c["mm"]; ev0 = C.c["ev"]
    with nc.Block() as blk:
        @blk.tensor
        def _(t):
            t.wait_ge(C.sem("gd"), C.c["gd"])
            for dc in range(NDC):
                for s4 in range(4):
                    mmi = t.matmul(psa[0:8, 512 * s4:512 * (s4 + 1)],
                                   wgt[:, 8 * dc:8 * (dc + 1)],
                                   xn[:, 2048 * dc + 512 * s4:2048 * dc + 512 * (s4 + 1)],
                                   start=(dc == 0), stop=(dc == NDC - 1))
            mmi.then_inc(C.sem("mm"), 1)
            t.matmul(psb[:, 128:136], ones1f[:, :], C.t["ebrow"][0:1, :],
                     start=True, stop=True).then_inc(C.sem("mm"), 1)

        @blk.scalar
        def _(s):
            s.wait_ge(C.sem("mm"), mm0 + 1)
            s.copy(lg8[0:8, 0:2048], psa[0:8, 0:2048]).then_inc(C.sem("ev"), 1)
    C.c["mm"] = mm0 + 2
    C.c["ev"] = ev0 + 1

    mm0 = C.c["mm"]; vv0 = C.c["vv"]
    with nc.Block() as blk:
        @blk.tensor
        def _(t):
            tri_ = None
            for tl in range(NTC):
                tri_ = t.transpose(psb[:, 8 * tl:8 * (tl + 1)],
                                   lg8[0:8, 128 * tl:128 * (tl + 1)], id32[0:8, 0:8])
            tri_.then_inc(C.sem("mm"), 1)

        @blk.vector
        def _(v):
            v.wait_ge(C.sem("mm"), mm0 + 1)
            v.tensor_copy(s0[:, :], psb[:, 0:128])
            v.tensor_copy(ebb[:, :], psb[:, 128:136])
            v.drain()
            v.reduce_max(sml[:, 0:16], t3(s0[:, :]), axis=AX.X)
            v.drain()
            v.tensor_tensor(sb_[:, :], s0[:, :], smlb(0, 8), op=ALU.subtract
                            ).then_inc(C.sem("vv"), 1)
    C.c["mm"] = mm0 + 1
    C.c["vv"] = vv0 + 1

    vv0 = C.c["vv"]; ev0 = C.c["ev"]
    with nc.Block() as blk:
        @blk.scalar
        def _(s):
            s.activation(out=s0[:, :], in_=sb_[:, :], func=AF.Exp
                         ).then_inc(C.sem("ev"), 1)
    C.c["ev"] = ev0 + 1

    vv0 = C.c["vv"]
    with nc.Block() as blk:
        @blk.vector
        def _(v):
            v.reduce_sum(sml[:, 16:32], t3(s0[:, :]), axis=AX.X)
            v.drain()
            v.reciprocal(sml[:, 16:32], sml[:, 16:32])
            v.drain()
            v.tensor_tensor(s0[:, :], s0[:, :], smlb(16, 8), op=ALU.mult)
            v.drain()
            if with_groups:
                v.tensor_tensor(sb_[:, :], s0[:, :],
                                bass.AP(ebb, 0, [[8, 128], [0, 16], [1, 8]]),
                                op=ALU.add)
                v.drain()
                v.tensor_tensor(gsc[:, :],
                                bass.AP(sb_, 0, [[128, 128], [8, 16], [2, 4]]),
                                bass.AP(sb_, 1, [[128, 128], [8, 16], [2, 4]]),
                                op=ALU.add)
                v.drain()
                g3 = gsc.ap().rearrange("p (t g) -> p t g", g=4)
                v.reduce_max(sml[:, 32:48], g3, axis=AX.X)
                v.drain()
                v.tensor_tensor(sm[:, 0:64], gsc[:, :], smlb(32, 4), op=ALU.is_equal)
                v.drain()
                v.scalar_tensor_tensor(gsc[:, :], sm[:, 0:64], -1e9, gsc[:, :],
                                       op0=ALU.mult, op1=ALU.add)
                v.drain()
                v.reduce_max(sml[:, 48:64], g3, axis=AX.X)
                v.drain()
                v.tensor_tensor(sm[:, 64:128], gsc[:, :], smlb(48, 4),
                                op=ALU.is_equal)
                v.drain()
                v.tensor_tensor(sm[:, 0:64], sm[:, 0:64], sm[:, 64:128], op=ALU.add)
                v.drain()
                v.tensor_scalar(sm[:, 0:64], sm[:, 0:64], 1e9, -1e9,
                                op0=ALU.mult, op1=ALU.add)
                v.drain()
                v.tensor_tensor(sb_[:, :], sb_[:, :],
                                bass.AP(sm, 0, [[128, 128], [4, 16], [1, 4], [0, 2]]),
                                op=ALU.add)
                v.drain()
            else:
                v.tensor_copy(sb_[:, :], s0[:, :])
                v.drain()
            v.reduce_max(sml[:, 32:48], t3(sb_[:, :]), axis=AX.X)
            v.drain()
            v.tensor_tensor(e1[:, :], sb_[:, :], smlb(32, 8), op=ALU.is_equal)
            v.drain()
            v.tensor_tensor(sm[:, 0:128], e1[:, :], s0[:, :], op=ALU.mult)
            v.drain()
            v.reduce_sum(sml[:, 64:80], t3(sm[:, 0:128]), axis=AX.X)
            v.scalar_tensor_tensor(sb_[:, :], e1[:, :], -1e9, sb_[:, :],
                                   op0=ALU.mult, op1=ALU.add)
            v.drain()
            v.reduce_max(sml[:, 48:64], t3(sb_[:, :]), axis=AX.X)
            v.drain()
            v.tensor_tensor(e2[:, :], sb_[:, :], smlb(48, 8), op=ALU.is_equal)
            v.drain()
            v.tensor_tensor(sm[:, 0:128], e2[:, :], s0[:, :], op=ALU.mult)
            v.drain()
            v.reduce_sum(sml[:, 80:96], t3(sm[:, 0:128]), axis=AX.X
                         ).then_inc(C.sem("vv"), 1)
    C.c["vv"] = vv0 + 1
    return dict(e1=e1, e2=e2, w1=sml[:, 64:80], w2=sml[:, 80:96], s0=s0, sml=sml,
                sm=sm)


def emit_ffn0(C, x_in, x_out, bs):
    nc = C.nc
    xn = C.t["XN"]
    scr = C.t["SCR16"]
    psa, psb = C.t["PSA"], C.t["PSB"]
    id32, idbf = C.t["id32"], C.t["idbf"]
    ones1f, ones128f = C.t["ones1f"], C.t["ones128f"]
    tmp1 = C.t["TMP1"]
    pos = C.t["pos"]
    coreflag = C.t["coreflag"]

    agx_in = nc.dram_tensor("f0_agx_in", [128, 64], BF16)
    agx_out = nc.dram_tensor("f0_agx_out", [1024, 64], BF16)
    arc_in = nc.dram_tensor("f0_arc_in", [1, 8], F32)
    arc_out = nc.dram_tensor("f0_arc_out", [1, 8], F32)
    arp_in = nc.dram_tensor("f0_arp_in", [128, 64], F32)
    arp_out = nc.dram_tensor("f0_arp_out", [128, 64], F32)

    fxst2 = scr[:, 0:4096].bitcast(F32)
    fsq2 = scr[:, 4096:8192].bitcast(F32)
    emit_ln_stats_from_dram(C, x_in, xst2=fxst2, sq2=fsq2)
    emit_ln_tail(C, T, 0, "pre", x_dram=x_in, out_bf=xn, xst2=fxst2, sq2=fsq2,
                 acp=scr[:, 8192:12288].bitcast(F32),
                 bcp=scr[:, 12288:16384].bitcast(F32))

    pts = C.sb("PTS", [128, 64], F32, bs)
    ws1 = C.sb("WSH1", [128, 8192], BF16, bs)
    ws2 = C.sb("WSH2", [128, 8192], BF16, bs)
    ws3 = C.sb("WSH3", [128, 8192], BF16, bs)

    def load_sq(g, dram_name, slot):
        g.dma_start(out=slot[:, :],
                    in_=_wview(C.t[dram_name], 8, 1024)
                    ).then_inc(C.sem("swl"), 16)
        C.c["swl"] = C.c.get("swl", 0) + 16

    with nc.Block() as blk:
        @blk.gpsimd
        def _(g):
            load_sq(g, "b1_sw1T", ws1)
            load_sq(g, "b1_sw3T", ws2)
            load_sq(g, "b1_sw2T", ws3)
    sw_need = C.c["swl"]

    gd0 = C.c["gd"]; cc0 = C.c["cc"]
    with nc.Block() as blk:
        @blk.gpsimd
        def _(g):
            g.dma_start(out=agx_in[:, :],
                        in_=xn.ap().rearrange("p (c t) -> p c t", t=2048)[:, :, 0:8]
                        ).then_inc(C.sem("gd"), 16)
            g.wait_ge(C.sem("gd"), gd0 + 16)
            g.collective_compute(
                "AllGather", ALU.bypass, replica_groups=[list(range(NCORES))],
                ins=[agx_in.ap().opt()], outs=[agx_out.ap().opt()],
            ).then_inc(C.sem("cc"), 1)
    C.c["gd"] = gd0 + 16
    C.c["cc"] = cc0 + 1

    with ExitStack() as bs2:
        R = _emit_gate_softmax(C, bs2, "b1_wgT", with_groups=True)
        e1, e2, w1c, w2c = R["e1"], R["e2"], R["w1"], R["w2"]
        mbuf = C.sb("MBUF", [128, 2048], BF16, bs2)
        mwt = C.sb("MWT", [64, 4096], BF16, bs2)
        afl = C.sb("AFL", [64, 512], BF16, bs2)
        bfl = C.sb("BFL", [64, 512], BF16, bs2)
        ht = C.sb("HT", [128, 256], BF16, bs2)
        xfh = C.sb("XFH", [128, 64], BF16, bs2)
        ofs = C.sb("OFS", [128, 16], F32, bs2)
        cl8 = C.sb("CL8", [8, 1], F32, bs2)
        crow = C.sb("CROW", [1, 16], F32, bs2)
        hsb = C.sb("HSB", [64, 512], BF16, bs2)
        asb = C.sb("ASB", [8, 1024], BF16, bs2)
        mwsc = C.sb("MWSC", [128, 64], BF16, bs2)
        wss1 = C.sb("WSS1", [128, 4096], BF16, bs2)
        wss2 = C.sb("WSS2", [128, 4096], BF16, bs2)
        wss3 = C.sb("WSS3", [128, 4096], BF16, bs2)
        wss4 = C.sb("WSS4", [128, 4096], BF16, bs2)

        mm0 = C.c["mm"]; ev0 = C.c["ev"]
        with nc.Block() as blk:
            @blk.tensor
            def _(t):
                for tl in range(NTC):
                    t.matmul(psa[0:8, 0:1], e1[:, 8 * tl:8 * (tl + 1)],
                             ones128f[:, :], start=(tl == 0), stop=False)
                    mmi = t.matmul(psa[0:8, 0:1], e2[:, 8 * tl:8 * (tl + 1)],
                                   ones128f[:, :], start=False,
                                   stop=(tl == NTC - 1))
                mmi.then_inc(C.sem("mm"), 1)

            @blk.scalar
            def _(s):
                s.wait_ge(C.sem("mm"), mm0 + 1)
                s.copy(cl8[:, :], psa[0:8, 0:1]).then_inc(C.sem("ev"), 1)
        C.c["mm"] = mm0 + 1
        C.c["ev"] = ev0 + 1

        gd0 = C.c["gd"]; cc0 = C.c["cc"]
        with nc.Block() as blk:
            @blk.gpsimd
            def _(g):
                g.dma_start(out=bass.AP(arc_in, 0, [[1, 8], [1, 1]]),
                            in_=cl8[:, :]).then_inc(C.sem("gd"), 16)
                g.wait_ge(C.sem("gd"), gd0 + 16)
                g.collective_compute(
                    "AllReduce", ALU.add, replica_groups=[list(range(NCORES))],
                    ins=[arc_in.ap().opt()], outs=[arc_out.ap().opt()],
                ).then_inc(C.sem("cc"), 1)
                g.wait_ge(C.sem("cc"), cc0 + 1)
                g.dma_start(out=crow[:, 0:8], in_=arc_out[:, :]).then_inc(C.sem("gd"), 16)
                g.dma_start(out=xfh[:, :], in_=agx_out[0:128, :]).then_inc(C.sem("gd"), 16)
                g.wait_ge(C.sem("gd"), gd0 + 48)
        C.c["gd"] = gd0 + 48
        C.c["cc"] = cc0 + 1

        vv0 = C.c["vv"]; mm0 = C.c["mm"]
        with nc.Block() as blk:
            @blk.vector
            def _(v):
                v.tensor_tensor_scan(crow[0:1, 8:16],
                                     bass.AP(ones1f, 0, [[128, 1], [1, 8]]),
                                     crow[0:1, 0:8], 0.0,
                                     op0=ALU.mult, op1=ALU.add)
                v.drain()
                v.tensor_tensor(crow[0:1, 0:8], crow[0:1, 8:16], crow[0:1, 0:8],
                                op=ALU.subtract).then_inc(C.sem("vv"), 1)

            @blk.tensor
            def _(t):
                t.wait_ge(C.sem("vv"), vv0 + 1)
                t.matmul(psb[:, 256:264], ones1f[:, :], crow[0:1, 0:8],
                         start=True, stop=True)
                t.matmul(psb[:, 264:272], ones1f[:, :], crow[0:1, 8:16],
                         start=True, stop=True).then_inc(C.sem("mm"), 1)
        C.c["vv"] = vv0 + 1
        C.c["mm"] = mm0 + 1
        vv0 = C.c["vv"]; mm0 = C.c["mm"]
        with nc.Block() as blk:
            @blk.vector
            def _(v):
                v.tensor_copy(ofs[:, :], psb[:, 256:272]).then_inc(C.sem("vv"), 1)
        C.c["vv"] = vv0 + 1

        gd0 = C.c["gd"]
        mm0 = C.c["mm"]; ev0 = C.c["ev"]; st0 = C.c["st"]
        with nc.Block() as blk:
            @blk.gpsimd
            def _(g):
                for sseg in range(8):
                    s1 = wss1 if sseg % 2 == 0 else wss3
                    s2 = wss2 if sseg % 2 == 0 else wss4
                    if sseg >= 2:
                        g.wait_ge(C.sem("mm"), mm0 + sseg - 1)
                    g.dma_start(out=s1[:, :],
                                in_=_wview(C.t["b1_w1"], 8, 512, row0=1024 * sseg)
                                ).then_inc(C.sem("gd"), 16)
                    g.dma_start(out=s2[:, :],
                                in_=_wview(C.t["b1_w3"], 8, 512, row0=1024 * sseg)
                                ).then_inc(C.sem("gd"), 16)
                    g.wait_ge(C.sem("gd"), gd0 + 32 * (sseg + 1))

            @blk.tensor
            def _(t):
                for sseg in range(8):
                    s1 = wss1 if sseg % 2 == 0 else wss3
                    s2 = wss2 if sseg % 2 == 0 else wss4
                    t.wait_ge(C.sem("gd"), gd0 + 32 * (sseg + 1))
                    for dc in range(NDC):
                        t.matmul(psa[0:8, 0:512], xfh[:, 8 * dc:8 * (dc + 1)],
                                 s1[:, 512 * dc:512 * (dc + 1)],
                                 start=(dc == 0), stop=(dc == NDC - 1))
                    for dc in range(NDC):
                        mmi = t.matmul(psa[0:8, 512:1024], xfh[:, 8 * dc:8 * (dc + 1)],
                                       s2[:, 512 * dc:512 * (dc + 1)],
                                       start=(dc == 0), stop=(dc == NDC - 1))
                    mmi.then_inc(C.sem("mm"), 1)
                    t.wait_ge(C.sem("ev"), ev0 + sseg + 1)

            @blk.scalar
            def _(s):
                for sseg in range(8):
                    s.wait_ge(C.sem("mm"), mm0 + sseg + 1)
                    s.wait_ge(C.sem("st"), st0 + 32 * sseg)
                    s.copy(asb[:, 0:512], psa[0:8, 0:512])
                    s.copy(asb[:, 512:1024], psa[0:8, 512:1024]).then_inc(C.sem("ev"), 1)
                    s.drain()
                    s.dma_start(out=afl[8 * sseg:8 * (sseg + 1), :],
                                in_=asb[:, 0:512]).then_inc(C.sem("st"), 16)
                    s.dma_start(out=bfl[8 * sseg:8 * (sseg + 1), :],
                                in_=asb[:, 512:1024]).then_inc(C.sem("st"), 16)
        C.c["gd"] = gd0 + 32 * 8
        C.c["mm"] = mm0 + 8
        C.c["ev"] = ev0 + 8
        C.c["st"] = st0 + 32 * 8

        vv0 = C.c["vv"]; mm0 = C.c["mm"]; ev0 = C.c["ev"]; st0 = C.c["st"]
        with nc.Block() as blk:
            @blk.vector
            def _(v):
                for sc in range(32):
                    tl, k = sc // 2, sc % 2
                    ek = e1 if k == 0 else e2
                    wc = w1c if k == 0 else w2c
                    pc = pos[:, 2 * tl + k:2 * tl + k + 1]
                    v.tensor_scalar(R["sm"][:, 0:8], ofs[:, 0:8], pc, None,
                                    op0=ALU.is_le)
                    v.tensor_scalar(R["sm"][:, 8:16], ofs[:, 8:16], pc, None,
                                    op0=ALU.is_gt)
                    v.drain()
                    v.tensor_tensor(R["sm"][:, 0:8], R["sm"][:, 0:8],
                                    R["sm"][:, 8:16], op=ALU.mult)
                    v.drain()
                    v.tensor_tensor(mbuf[:, 64 * sc:64 * (sc + 1)],
                                    bass.AP(ek.tensor if hasattr(ek, "tensor") else ek,
                                            8 * tl, [[128, 128], [0, 8], [1, 8]]),
                                    bass.AP(R["sm"], 0, [[128, 128], [1, 8], [0, 8]]),
                                    op=ALU.mult)
                    v.drain()
                    if sc >= 1:
                        v.wait_ge(C.sem("mm"), mm0 + sc)
                    v.tensor_scalar(mwsc[:, :], mbuf[:, 64 * sc:64 * (sc + 1)],
                                    wc[:, tl:tl + 1], None, op0=ALU.mult
                                    ).then_inc(C.sem("vv"), 1)

            @blk.tensor
            def _(t):
                for sc in range(32):
                    t.wait_ge(C.sem("vv"), vv0 + sc + 1)
                    tri_ = t.transpose(psb[:, :].bitcast(BF16)[0:64, 3584:3712],
                                       mwsc[:, :], idbf[:, :])
                    tri_.then_inc(C.sem("mm"), 1)
                    t.wait_ge(C.sem("ev"), ev0 + sc + 1)

            @blk.scalar
            def _(s):
                for sc in range(32):
                    s.wait_ge(C.sem("mm"), mm0 + sc + 1)
                    s.copy(mwt[0:64, 128 * sc:128 * (sc + 1)],
                           psb[:, :].bitcast(BF16)[0:64, 3584:3712]
                           ).then_inc(C.sem("ev"), 1)
        C.c["vv"] = vv0 + 32
        C.c["mm"] = mm0 + 32
        C.c["ev"] = ev0 + 32

        mm0 = C.c["mm"]; ev0 = C.c["ev"]; vv0 = C.c["vv"]
        st_need = C.c["st"]
        with nc.Block() as blk:
            @blk.tensor
            def _(t):
                t.wait_ge(C.sem("st"), st_need)
                for sc in range(32):
                    par = sc % 2
                    if sc >= 2:
                        t.wait_ge(C.sem("vv"), vv0 + sc - 1)
                    t.matmul(psa[:, 1024 * par:1024 * par + 512],
                             mwt[:, 128 * sc:128 * (sc + 1)], afl[:, :],
                             start=True, stop=True).then_inc(C.sem("mm"), 1)
                    t.matmul(psa[:, 1024 * par + 512:1024 * par + 1024],
                             mwt[:, 128 * sc:128 * (sc + 1)], bfl[:, :],
                             start=True, stop=True).then_inc(C.sem("mm"), 1)
                    t.wait_ge(C.sem("vv"), vv0 + sc + 1)
                    t.matmul(psb[0:64, 0:512], mbuf[:, 64 * sc:64 * (sc + 1)],
                             scr[:, 512 * par:512 * (par + 1)],
                             start=(sc == 0), stop=(sc == 31)).then_inc(C.sem("mm"), 1)

            @blk.scalar
            def _(s):
                for sc in range(32):
                    par = sc % 2
                    s.wait_ge(C.sem("mm"), mm0 + 3 * sc + 1)
                    if sc >= 1:
                        s.wait_ge(C.sem("vv"), vv0 + sc)
                    s.activation(out=tmp1[:, 0:512], in_=psa[:, 1024 * par:1024 * par + 512],
                                 func=AF.Silu).then_inc(C.sem("ev"), 1)

            @blk.vector
            def _(v):
                for sc in range(32):
                    par = sc % 2
                    v.wait_ge(C.sem("ev"), ev0 + sc + 1)
                    v.wait_ge(C.sem("mm"), mm0 + 3 * sc + 2)
                    v.tensor_tensor(scr[:, 512 * par:512 * (par + 1)],
                                    psa[:, 1024 * par + 512:1024 * par + 1024],
                                    tmp1[:, 0:512], op=ALU.mult).then_inc(C.sem("vv"), 1)
        C.c["mm"] = mm0 + 3 * 32
        C.c["ev"] = ev0 + 32
        C.c["vv"] = vv0 + 32

        mm0 = C.c["mm"]; ev0 = C.c["ev"]; vv0 = C.c["vv"]
        with nc.Block() as blk:
            @blk.scalar
            def _(s):
                s.copy(hsb[:, :], psb[0:64, 0:512]).then_inc(C.sem("ev"), 1)

            @blk.tensor
            def _(t):
                t.wait_ge(C.sem("ev"), ev0 + 1)
                tri_ = None
                for hc in range(4):
                    tri_ = t.transpose(
                        psb[:, :].bitcast(BF16)[:, 1024 + 64 * hc:1024 + 64 * (hc + 1)],
                        hsb[:, 128 * hc:128 * (hc + 1)], idbf[0:64, 0:64])
                tri_.then_inc(C.sem("mm"), 1)

            @blk.vector
            def _(v):
                v.wait_ge(C.sem("mm"), mm0 + 1)
                v.tensor_copy(ht[:, :], psb[:, :].bitcast(BF16)[:, 1024:1280]
                              ).then_inc(C.sem("vv"), 1)
        C.c["ev"] = ev0 + 1
        C.c["mm"] = mm0 + 1
        C.c["vv"] = vv0 + 1

        gd0 = C.c["gd"]; mm0 = C.c["mm"]; ev0 = C.c["ev"]
        with nc.Block() as blk:
            @blk.gpsimd
            def _(g):
                for sseg in range(8):
                    if sseg > 0:
                        g.wait_ge(C.sem("mm"), mm0 + sseg)
                    g.dma_start(out=wss1[:, :],
                                in_=_wview(C.t["b1_w2"], 4, 1024, row0=512 * sseg)
                                ).then_inc(C.sem("gd"), 16)

            @blk.tensor
            def _(t):
                for sseg in range(8):
                    t.wait_ge(C.sem("gd"), gd0 + 16 * (sseg + 1))
                    for dm in range(8):
                        for hc in range(4):
                            mmi = t.matmul(
                                psb[:, 512 + 8 * dm:512 + 8 * (dm + 1)],
                                wss1[:, 1024 * hc + 128 * dm:1024 * hc + 128 * (dm + 1)],
                                ht[:, 64 * hc + 8 * sseg:64 * hc + 8 * (sseg + 1)],
                                start=(sseg == 0 and dm == 0 and hc == 0),
                                stop=(sseg == 7 and dm == 7 and hc == 3))
                    mmi.then_inc(C.sem("mm"), 1)

            @blk.scalar
            def _(s):
                s.wait_ge(C.sem("mm"), mm0 + 8)
                s.copy(pts[:, :], psb[:, 512:576]).then_inc(C.sem("ev"), 1)
        C.c["gd"] = gd0 + 16 * 8
        C.c["mm"] = mm0 + 8
        C.c["ev"] = ev0 + 1

        gd0 = C.c["gd"]; cc0 = C.c["cc"]
        with nc.Block() as blk:
            @blk.gpsimd
            def _(g):
                g.dma_start(out=arp_in[:, :], in_=pts[:, :]).then_inc(C.sem("gd"), 16)
                g.wait_ge(C.sem("gd"), gd0 + 16)
                g.collective_compute(
                    "AllReduce", ALU.add, replica_groups=[list(range(NCORES))],
                    ins=[arp_in.ap().opt()], outs=[arp_out.ap().opt()],
                ).then_inc(C.sem("cc"), 1)
                g.wait_ge(C.sem("cc"), cc0 + 1)
                g.dma_start(out=pts[:, :], in_=arp_out[:, :]).then_inc(C.sem("gd"), 16)
                g.wait_ge(C.sem("gd"), gd0 + 32)
        C.c["gd"] = gd0 + 32
        C.c["cc"] = cc0 + 1

    pts = C.t["PTS"]
    yh = C.sb("YH", [128, 8192], F32, bs)

    mm0 = C.c["mm"]; ev0 = C.c["ev"]; vv0 = C.c["vv"]
    with nc.Block() as blk:
        @blk.tensor
        def _(t):
            t.wait_ge(C.sem("swl"), sw_need)
            idx = 0
            for mc in range(8):
                for s4 in range(4):
                    if idx >= 2:
                        t.wait_ge(C.sem("vv"), vv0 + idx - 1)
                    for dc in range(NDC):
                        t.matmul(psa[:, 1024 * (idx % 2):1024 * (idx % 2) + 512],
                                 ws1[:, 1024 * dc + 128 * mc:1024 * dc + 128 * (mc + 1)],
                                 xn[:, 2048 * dc + 512 * s4:2048 * dc + 512 * (s4 + 1)],
                                 start=(dc == 0), stop=(dc == NDC - 1))
                    for dc in range(NDC):
                        mmi = t.matmul(
                            psa[:, 1024 * (idx % 2) + 512:1024 * (idx % 2) + 1024],
                            ws2[:, 1024 * dc + 128 * mc:1024 * dc + 128 * (mc + 1)],
                            xn[:, 2048 * dc + 512 * s4:2048 * dc + 512 * (s4 + 1)],
                            start=(dc == 0), stop=(dc == NDC - 1))
                    mmi.then_inc(C.sem("mm"), 1)
                    idx += 1

        @blk.scalar
        def _(s):
            idx = 0
            for mc in range(8):
                for s4 in range(4):
                    s.wait_ge(C.sem("mm"), mm0 + idx + 1)
                    if idx >= 1:
                        s.wait_ge(C.sem("vv"), vv0 + idx)
                    s.activation(out=tmp1[:, 0:512],
                                 in_=psa[:, 1024 * (idx % 2):1024 * (idx % 2) + 512],
                                 func=AF.Silu).then_inc(C.sem("ev"), 1)
                    idx += 1

        @blk.vector
        def _(v):
            idx = 0
            for mc in range(8):
                for s4 in range(4):
                    v.wait_ge(C.sem("ev"), ev0 + idx + 1)
                    v.tensor_tensor(scr[:, 2048 * mc + 512 * s4:2048 * mc + 512 * (s4 + 1)],
                                    psa[:, 1024 * (idx % 2) + 512:1024 * (idx % 2) + 1024],
                                    tmp1[:, 0:512], op=ALU.mult).then_inc(C.sem("vv"), 1)
                    idx += 1
    C.c["mm"] = mm0 + 32
    C.c["ev"] = ev0 + 32
    C.c["vv"] = vv0 + 32

    for th in range(2):
        mm0 = C.c["mm"]; vv0 = C.c["vv"]
        with nc.Block() as blk:
            @blk.tensor
            def _(t):
                idx = 0
                for mc in range(8):
                    for s2 in range(2):
                        if idx >= 2:
                            t.wait_ge(C.sem("vv"), vv0 + idx - 1)
                        for dc in range(NDC):
                            mmi = t.matmul(
                                psa[:, 512 * (idx % 2):512 * (idx % 2) + 512],
                                ws3[:, 1024 * dc + 128 * mc:1024 * dc + 128 * (mc + 1)],
                                scr[:, 2048 * dc + 1024 * th + 512 * s2:
                                    2048 * dc + 1024 * th + 512 * (s2 + 1)],
                                start=(dc == 0), stop=(dc == NDC - 1))
                        mmi.then_inc(C.sem("mm"), 1)
                        idx += 1

            @blk.vector
            def _(v):
                idx = 0
                for mc in range(8):
                    for s2 in range(2):
                        v.wait_ge(C.sem("mm"), mm0 + idx + 1)
                        v.tensor_copy(yh[:, 1024 * mc + 512 * s2:1024 * mc + 512 * (s2 + 1)],
                                      psa[:, 512 * (idx % 2):512 * (idx % 2) + 512]
                                      ).then_inc(C.sem("vv"), 1)
                        idx += 1
        C.c["mm"] = mm0 + 16
        C.c["vv"] = vv0 + 16

        if th == 0:
            vv0 = C.c["vv"]
            with nc.Block() as blk:
                @blk.vector
                def _(v):
                    for mc in range(8):
                        v.drain()
                        v.scalar_tensor_tensor(
                            yh[:, 1024 * mc:1024 * mc + 8],
                            pts[:, 8 * mc:8 * (mc + 1)], coreflag[:, 0:1],
                            yh[:, 1024 * mc:1024 * mc + 8],
                            op0=ALU.mult, op1=ALU.add)
                    v.engine_nop().then_inc(C.sem("vv"), 1)
            C.c["vv"] = vv0 + 1

        def ysrc(c, _th=th):
            return yh[:, 1024 * c:1024 * (c + 1)]
        emit_ln_stats_mid(C, ysrc, 1024, src_dtype=F32,
                          sq2=ws1[:, 4096:6144].bitcast(F32))
        emit_ln_tail(C, 1024, 1024 * th, "post", x_dram=x_in, ysrc=ysrc,
                     store=x_out, xst2=ws1[:, 0:2048].bitcast(F32),
                     sq2=ws1[:, 2048:4096].bitcast(F32),
                     acp=ws1[:, 6144:8192].bitcast(F32),
                     bcp=ws2[:, 0:2048].bitcast(F32))


def emit_ffn1(C, x_in, x_out, bs):
    nc = C.nc
    xn = C.t["XN"]
    scr = C.t["SCR16"]
    psa, psb = C.t["PSA"], C.t["PSB"]
    id32 = C.t["id32"]
    ones1f, ones128f = C.t["ones1f"], C.t["ones128f"]
    ustrict = C.t["ustrict"]
    maskcol = C.t["maskcol"]
    tmp1 = C.t["TMP1"]

    agc_in = nc.dram_tensor("f1_agc_in", [1, 8], F32)
    agc_out = nc.dram_tensor("f1_agc_out", [8, 8], F32)
    rt16 = nc.dram_tensor("f1_rt16", [1, 128], F32)

    f1xst2 = scr[:, 0:4096].bitcast(F32)
    f1sq2 = scr[:, 4096:8192].bitcast(F32)
    emit_ln_stats_from_dram(C, x_in, xst2=f1xst2, sq2=f1sq2)
    emit_ln_tail(C, T, 0, "pre", x_dram=x_in, out_bf=xn, xst2=f1xst2, sq2=f1sq2,
                 acp=scr[:, 8192:12288].bitcast(F32),
                 bcp=scr[:, 12288:16384].bitcast(F32))

    yh = C.sb("YF", [128, 16384], F32, bs)
    comb = C.sb("COMB", [128, 128], F32, bs)

    with ExitStack() as bs2:
        R = _emit_gate_softmax(C, bs2, "b3_gwT", with_groups=False)
        e1, e2, w1c, w2c, sml, sm = (R["e1"], R["e2"], R["w1"], R["w2"],
                                     R["sml"], R["sm"])
        te = C.sb("TE", [128, 128], F32, bs2)
        tss = C.sb("TSS", [8, 16], F32, bs2)
        stb = C.sb("STB", [16, 8], F32, bs2)
        clrow = C.sb("CLROW", [1, 8], F32, bs2)
        agc = C.sb("AGC", [8, 8], F32, bs2)
        pfx = C.sb("PFX", [8, 1], F32, bs2)
        pbt = C.sb("PBT", [8, 16], F32, bs2)
        pbf = C.sb("PBF", [16, 8], F32, bs2)
        pbrow = C.sb("PBROW", [1, 128], F32, bs2)
        km = C.sb("KM", [128, 128], F32, bs2)
        wn = C.sb("WN", [128, 64], F32, bs2)

        vv0 = C.c["vv"]
        with nc.Block() as blk:
            @blk.vector
            def _(v):
                v.tensor_tensor(wn[:, 0:16], w1c, w2c, op=ALU.add)
                v.drain()
                v.tensor_scalar_add(wn[:, 0:16], wn[:, 0:16], 1e-20)
                v.drain()
                v.reciprocal(wn[:, 16:32], wn[:, 0:16])
                v.drain()
                v.tensor_tensor(wn[:, 32:48], w1c, wn[:, 16:32], op=ALU.mult)
                v.tensor_tensor(wn[:, 48:64], w2c, wn[:, 16:32], op=ALU.mult)
                v.drain()
                v.tensor_tensor(te[:, :], e1[:, :], e2[:, :], op=ALU.add
                                ).then_inc(C.sem("vv"), 1)
        C.c["vv"] = vv0 + 1

        mm0 = C.c["mm"]; ev0 = C.c["ev"]; vv0 = C.c["vv"]
        with nc.Block() as blk:
            @blk.tensor
            def _(t):
                mmi = None
                for tl in range(NTC):
                    mmi = t.matmul(psa[0:8, tl:tl + 1], te[:, 8 * tl:8 * (tl + 1)],
                                   ones128f[:, :], start=True, stop=True)
                mmi.then_inc(C.sem("mm"), 1)

            @blk.scalar
            def _(s):
                s.wait_ge(C.sem("mm"), mm0 + 1)
                s.copy(tss[:, :], psa[0:8, 0:16]).then_inc(C.sem("ev"), 1)
        C.c["mm"] = mm0 + 1
        C.c["ev"] = ev0 + 1

        mm0 = C.c["mm"]; vv0 = C.c["vv"]; ev0 = C.c["ev"]
        with nc.Block() as blk:
            @blk.tensor
            def _(t):
                t.transpose(psa[0:16, 16:24], tss[:, :], id32[0:8, 0:8]
                            ).then_inc(C.sem("mm"), 1)

            @blk.vector
            def _(v):
                v.wait_ge(C.sem("mm"), mm0 + 1)
                v.tensor_copy(stb[:, :], psa[0:16, 16:24]).then_inc(C.sem("vv"), 1)
        C.c["mm"] = mm0 + 1
        C.c["vv"] = vv0 + 1

        mm0 = C.c["mm"]; ev0 = C.c["ev"]
        with nc.Block() as blk:
            @blk.tensor
            def _(t):
                t.matmul(psa[0:1, 24:32], ones128f[0:16, :], stb[:, :],
                         start=True, stop=True).then_inc(C.sem("mm"), 1)

            @blk.scalar
            def _(s):
                s.wait_ge(C.sem("mm"), mm0 + 1)
                s.copy(clrow[:, :], psa[0:1, 24:32]).then_inc(C.sem("ev"), 1)
        C.c["mm"] = mm0 + 1
        C.c["ev"] = ev0 + 1

        gd0 = C.c["gd"]; cc0 = C.c["cc"]
        with nc.Block() as blk:
            @blk.gpsimd
            def _(g):
                g.dma_start(out=agc_in[:, :], in_=clrow[:, :]).then_inc(C.sem("gd"), 16)
                g.wait_ge(C.sem("gd"), gd0 + 16)
                g.collective_compute(
                    "AllGather", ALU.bypass, replica_groups=[list(range(NCORES))],
                    ins=[agc_in.ap().opt()], outs=[agc_out.ap().opt()],
                ).then_inc(C.sem("cc"), 1)
                g.wait_ge(C.sem("cc"), cc0 + 1)
                g.dma_start(out=agc[:, :], in_=agc_out[:, :]).then_inc(C.sem("gd"), 16)
                g.wait_ge(C.sem("gd"), gd0 + 32)
        C.c["gd"] = gd0 + 32
        C.c["cc"] = cc0 + 1

        mm0 = C.c["mm"]; vv0 = C.c["vv"]; ev0 = C.c["ev"]
        with nc.Block() as blk:
            @blk.tensor
            def _(t):
                t.matmul(psa[0:8, 32:33], agc[:, :], maskcol[:, :],
                         start=True, stop=True).then_inc(C.sem("mm"), 1)

            @blk.vector
            def _(v):
                v.wait_ge(C.sem("mm"), mm0 + 1)
                v.tensor_copy(pfx[:, :], psa[0:8, 32:33])
                v.drain()
                v.tensor_tensor_scan(pbt[:, :],
                                     bass.AP(ones128f, 0, [[1, 8], [0, 16]]),
                                     tss[:, :], 0.0, op0=ALU.mult, op1=ALU.add)
                v.drain()
                v.tensor_tensor(pbt[:, :], pbt[:, :], tss[:, :], op=ALU.subtract)
                v.drain()
                v.tensor_scalar(pbt[:, :], pbt[:, :], pfx[:, 0:1], None,
                                op0=ALU.add).then_inc(C.sem("vv"), 1)
        C.c["mm"] = mm0 + 1
        C.c["vv"] = vv0 + 1

        mm0 = C.c["mm"]; vv0 = C.c["vv"]
        with nc.Block() as blk:
            @blk.tensor
            def _(t):
                t.transpose(psa[0:16, 40:48], pbt[:, :], id32[0:8, 0:8]
                            ).then_inc(C.sem("mm"), 1)

            @blk.vector
            def _(v):
                v.wait_ge(C.sem("mm"), mm0 + 1)
                v.tensor_copy(pbf[:, :], psa[0:16, 40:48]).then_inc(C.sem("vv"), 1)
        C.c["mm"] = mm0 + 1
        C.c["vv"] = vv0 + 1

        gd0 = C.c["gd"]
        with nc.Block() as blk:
            @blk.gpsimd
            def _(g):
                g.dma_start(out=bass.AP(rt16, 0, [[8, 16], [1, 8]]),
                            in_=pbf[:, :]).then_inc(C.sem("gd"), 16)
                g.wait_ge(C.sem("gd"), gd0 + 16)
                g.dma_start(out=pbrow[:, :], in_=rt16[:, :]).then_inc(C.sem("gd"), 16)
                g.wait_ge(C.sem("gd"), gd0 + 32)
        C.c["gd"] = gd0 + 32

        mm0 = C.c["mm"]; vv0 = C.c["vv"]
        with nc.Block() as blk:
            @blk.tensor
            def _(t):
                mmi = None
                for tl in range(NTC):
                    t.matmul(psb[:, 8 * tl:8 * (tl + 1)], ustrict[:, :],
                             te[:, 8 * tl:8 * (tl + 1)],
                             start=(tl == 0), stop=False)
                    mmi = t.matmul(psb[:, 8 * tl:8 * (tl + 1)], ones1f[0:1, :],
                                   pbrow[0:1, 8 * tl:8 * (tl + 1)],
                                   start=False, stop=(tl == NTC - 1))
                mmi.then_inc(C.sem("mm"), 1)

            @blk.vector
            def _(v):
                v.wait_ge(C.sem("mm"), mm0 + 1)
                v.tensor_scalar(km[:, :], psb[:, 0:128], float(CAPACITY), None,
                                op0=ALU.is_lt)
                v.drain()
                v.tensor_tensor(sm[:, 0:128], km[:, :], e1[:, :], op=ALU.mult)
                v.drain()
                v.reduce_sum(wn[:, 0:16], sm[:, 0:128].rearrange(
                    "p (t e) -> p t e", e=8), axis=AX.X)
                v.drain()
                v.tensor_tensor(sm[:, 0:128], km[:, :], e2[:, :], op=ALU.mult)
                v.drain()
                v.reduce_sum(wn[:, 16:32], sm[:, 0:128].rearrange(
                    "p (t e) -> p t e", e=8), axis=AX.X)
                v.drain()
                v.tensor_tensor(wn[:, 32:48], wn[:, 32:48], wn[:, 0:16], op=ALU.mult)
                v.tensor_tensor(wn[:, 48:64], wn[:, 48:64], wn[:, 16:32], op=ALU.mult)
                v.drain()
                for tl in range(NTC):
                    v.tensor_scalar(sm[:, 8 * tl:8 * (tl + 1)],
                                    e1[:, 8 * tl:8 * (tl + 1)],
                                    wn[:, 32 + tl:33 + tl], None, op0=ALU.mult)
                v.drain()
                for tl in range(NTC):
                    v.scalar_tensor_tensor(comb[:, 8 * tl:8 * (tl + 1)],
                                           e2[:, 8 * tl:8 * (tl + 1)],
                                           wn[:, 48 + tl:49 + tl],
                                           sm[:, 8 * tl:8 * (tl + 1)],
                                           op0=ALU.mult, op1=ALU.add)
                v.engine_nop().then_inc(C.sem("vv"), 1)
        C.c["mm"] = mm0 + 1
        C.c["vv"] = vv0 + 1

    wse1 = C.sb("WSE1", [128, 8192], BF16, bs)
    wse2 = C.sb("WSE2", [128, 8192], BF16, bs)
    wse3 = C.sb("WSE3", [128, 8192], BF16, bs)

    def load_we(g, dram_name, e, slot):
        g.dma_start(out=slot[:, :],
                    in_=_wview(C.t[dram_name], 8, 1024, row0=1024 * e)
                    ).then_inc(C.sem("gd"), 16)
        C.c["gd"] += 16

    def load_sh(g, dram_name, sh, slot):
        if dram_name == "b3_wsdT":
            ap = _wview(C.t[dram_name], 8, 1024, row0=1024 * sh)
        else:
            ap = _wview(C.t[dram_name], 8, 1024, col0=1024 * sh)
        g.dma_start(out=slot[:, :], in_=ap).then_inc(C.sem("gd"), 16)
        C.c["gd"] += 16

    with nc.Block() as blk:
        @blk.gpsimd
        def _(g):
            load_we(g, "b3_wegT", 0, wse1)
            load_we(g, "b3_weuT", 0, wse2)
            g.wait_ge(C.sem("gd"), C.c["gd"])
    gu_need = C.c["gd"]
    dn_need = 0

    for e in range(8):
        mm0 = C.c["mm"]; ev0 = C.c["ev"]; vv0 = C.c["vv"]
        with nc.Block() as blk:
            @blk.gpsimd
            def _(g):
                load_we(g, "b3_wedT", e, wse3)
                g.wait_ge(C.sem("gd"), C.c["gd"])
            dn_need = C.c["gd"]

            @blk.tensor
            def _(t):
                t.wait_ge(C.sem("gd"), gu_need)
                idx = 0
                for ic in range(8):
                    for s4 in range(4):
                        if idx >= 2:
                            t.wait_ge(C.sem("vv"), vv0 + idx - 1)
                        for dc in range(NDC):
                            t.matmul(psa[:, 1024 * (idx % 2):1024 * (idx % 2) + 512],
                                     wse1[:, 1024 * dc + 128 * ic:1024 * dc + 128 * (ic + 1)],
                                     xn[:, 2048 * dc + 512 * s4:2048 * dc + 512 * (s4 + 1)],
                                     start=(dc == 0), stop=(dc == NDC - 1))
                        for dc in range(NDC):
                            mmi = t.matmul(
                                psa[:, 1024 * (idx % 2) + 512:1024 * (idx % 2) + 1024],
                                wse2[:, 1024 * dc + 128 * ic:1024 * dc + 128 * (ic + 1)],
                                xn[:, 2048 * dc + 512 * s4:2048 * dc + 512 * (s4 + 1)],
                                start=(dc == 0), stop=(dc == NDC - 1))
                        mmi.then_inc(C.sem("mm"), 1)
                        idx += 1

            @blk.scalar
            def _(s):
                idx = 0
                for ic in range(8):
                    for s4 in range(4):
                        s.wait_ge(C.sem("mm"), mm0 + idx + 1)
                        if idx >= 1:
                            s.wait_ge(C.sem("vv"), vv0 + idx)
                        s.activation(out=tmp1[:, 0:512],
                                     in_=psa[:, 1024 * (idx % 2):1024 * (idx % 2) + 512],
                                     func=AF.Silu).then_inc(C.sem("ev"), 1)
                        idx += 1

            @blk.vector
            def _(v):
                idx = 0
                for ic in range(8):
                    for s4 in range(4):
                        v.wait_ge(C.sem("ev"), ev0 + idx + 1)
                        v.tensor_tensor(
                            scr[:, 2048 * ic + 512 * s4:2048 * ic + 512 * (s4 + 1)],
                            psa[:, 1024 * (idx % 2) + 512:1024 * (idx % 2) + 1024],
                            tmp1[:, 0:512], op=ALU.mult).then_inc(C.sem("vv"), 1)
                        idx += 1
        C.c["mm"] = mm0 + 32
        C.c["ev"] = ev0 + 32
        C.c["vv"] = vv0 + 32

        mm0 = C.c["mm"]; vv0 = C.c["vv"]
        with nc.Block() as blk:
            @blk.gpsimd
            def _(g):
                if e < 7:
                    load_we(g, "b3_wegT", e + 1, wse1)
                    load_we(g, "b3_weuT", e + 1, wse2)
                else:
                    load_sh(g, "b3_wsgT", 0, wse1)
                    load_sh(g, "b3_wsuT", 0, wse2)
                g.wait_ge(C.sem("gd"), C.c["gd"])
            gu_need = C.c["gd"]

            @blk.tensor
            def _(t):
                t.wait_ge(C.sem("gd"), dn_need)
                for tc in range(NTC):
                    if tc >= 2:
                        t.wait_ge(C.sem("vv"), vv0 + tc - 1)
                    for ic in range(8):
                        for nh in range(2):
                            mmi = t.matmul(
                                psa[:, 1024 * (tc % 2) + 512 * nh:
                                    1024 * (tc % 2) + 512 * (nh + 1)],
                                scr[:, 2048 * ic + 128 * tc:2048 * ic + 128 * (tc + 1)],
                                wse3[:, 1024 * ic + 512 * nh:1024 * ic + 512 * (nh + 1)],
                                start=(ic == 0), stop=(ic == 7))
                    mmi.then_inc(C.sem("mm"), 1)

            @blk.vector
            def _(v):
                for tc in range(NTC):
                    v.wait_ge(C.sem("mm"), mm0 + tc + 1)
                    v.drain()
                    if e == 0:
                        v.tensor_scalar(yh[:, 1024 * tc:1024 * (tc + 1)],
                                        psa[:, 1024 * (tc % 2):1024 * (tc % 2) + 1024],
                                        comb[:, 8 * tc + e:8 * tc + e + 1], None,
                                        op0=ALU.mult).then_inc(C.sem("vv"), 1)
                    else:
                        v.scalar_tensor_tensor(yh[:, 1024 * tc:1024 * (tc + 1)],
                                               psa[:, 1024 * (tc % 2):1024 * (tc % 2) + 1024],
                                               comb[:, 8 * tc + e:8 * tc + e + 1],
                                               yh[:, 1024 * tc:1024 * (tc + 1)],
                                               op0=ALU.mult, op1=ALU.add
                                               ).then_inc(C.sem("vv"), 1)
        C.c["mm"] = mm0 + NTC
        C.c["vv"] = vv0 + NTC

    for sh in range(2):
        mm0 = C.c["mm"]; ev0 = C.c["ev"]; vv0 = C.c["vv"]
        with nc.Block() as blk:
            @blk.gpsimd
            def _(g):
                load_sh(g, "b3_wsdT", sh, wse3)
                g.wait_ge(C.sem("gd"), C.c["gd"])
            dn_need = C.c["gd"]

            @blk.tensor
            def _(t):
                t.wait_ge(C.sem("gd"), gu_need)
                idx = 0
                for ic in range(8):
                    for s4 in range(4):
                        if idx >= 2:
                            t.wait_ge(C.sem("vv"), vv0 + idx - 1)
                        for dc in range(NDC):
                            t.matmul(psa[:, 1024 * (idx % 2):1024 * (idx % 2) + 512],
                                     wse1[:, 1024 * dc + 128 * ic:1024 * dc + 128 * (ic + 1)],
                                     xn[:, 2048 * dc + 512 * s4:2048 * dc + 512 * (s4 + 1)],
                                     start=(dc == 0), stop=(dc == NDC - 1))
                        for dc in range(NDC):
                            mmi = t.matmul(
                                psa[:, 1024 * (idx % 2) + 512:1024 * (idx % 2) + 1024],
                                wse2[:, 1024 * dc + 128 * ic:1024 * dc + 128 * (ic + 1)],
                                xn[:, 2048 * dc + 512 * s4:2048 * dc + 512 * (s4 + 1)],
                                start=(dc == 0), stop=(dc == NDC - 1))
                        mmi.then_inc(C.sem("mm"), 1)
                        idx += 1

            @blk.scalar
            def _(s):
                idx = 0
                for ic in range(8):
                    for s4 in range(4):
                        s.wait_ge(C.sem("mm"), mm0 + idx + 1)
                        if idx >= 1:
                            s.wait_ge(C.sem("vv"), vv0 + idx)
                        s.activation(out=tmp1[:, 0:512],
                                     in_=psa[:, 1024 * (idx % 2):1024 * (idx % 2) + 512],
                                     func=AF.Silu).then_inc(C.sem("ev"), 1)
                        idx += 1

            @blk.vector
            def _(v):
                idx = 0
                for ic in range(8):
                    for s4 in range(4):
                        v.wait_ge(C.sem("ev"), ev0 + idx + 1)
                        v.tensor_tensor(
                            scr[:, 2048 * ic + 512 * s4:2048 * ic + 512 * (s4 + 1)],
                            psa[:, 1024 * (idx % 2) + 512:1024 * (idx % 2) + 1024],
                            tmp1[:, 0:512], op=ALU.mult).then_inc(C.sem("vv"), 1)
                        idx += 1
        C.c["mm"] = mm0 + 32
        C.c["ev"] = ev0 + 32
        C.c["vv"] = vv0 + 32

        mm0 = C.c["mm"]; vv0 = C.c["vv"]
        with nc.Block() as blk:
            @blk.gpsimd
            def _(g):
                if sh == 0:
                    load_sh(g, "b3_wsgT", 1, wse1)
                    load_sh(g, "b3_wsuT", 1, wse2)
                g.wait_ge(C.sem("gd"), C.c["gd"])
            gu_need = C.c["gd"]

            @blk.tensor
            def _(t):
                t.wait_ge(C.sem("gd"), dn_need)
                for tc in range(NTC):
                    if tc >= 2:
                        t.wait_ge(C.sem("vv"), vv0 + tc - 1)
                    for ic in range(8):
                        for nh in range(2):
                            mmi = t.matmul(
                                psa[:, 1024 * (tc % 2) + 512 * nh:
                                    1024 * (tc % 2) + 512 * (nh + 1)],
                                scr[:, 2048 * ic + 128 * tc:2048 * ic + 128 * (tc + 1)],
                                wse3[:, 1024 * ic + 512 * nh:1024 * ic + 512 * (nh + 1)],
                                start=(ic == 0), stop=(ic == 7))
                    mmi.then_inc(C.sem("mm"), 1)

            @blk.vector
            def _(v):
                for tc in range(NTC):
                    v.wait_ge(C.sem("mm"), mm0 + tc + 1)
                    v.drain()
                    v.tensor_tensor(yh[:, 1024 * tc:1024 * (tc + 1)],
                                    psa[:, 1024 * (tc % 2):1024 * (tc % 2) + 1024],
                                    yh[:, 1024 * tc:1024 * (tc + 1)], op=ALU.add
                                    ).then_inc(C.sem("vv"), 1)
        C.c["mm"] = mm0 + NTC
        C.c["vv"] = vv0 + NTC

    sqs = C.sb("SQS", [128, 16], F32, bs)
    sms = C.sb("SMS", [128, 16], F32, bs)
    rst2 = C.sb("RST2", [128, 16], F32, bs)
    bc2 = C.sb("BC2", [128, 16], F32, bs)
    sqscr = C.sb("SQSCR", [128, 1024], F32, bs)

    ev0 = C.c["ev"]; vv0 = C.c["vv"]
    with nc.Block() as blk:
        @blk.scalar
        def _(s):
            for tc in range(NTC):
                s.drain()
                s.activation(out=sqscr[:, :], in_=yh[:, 1024 * tc:1024 * (tc + 1)],
                             func=AF.Square, accum_out=sqs[:, tc:tc + 1]
                             ).then_inc(C.sem("ev"), 1)

        @blk.vector
        def _(v):
            for tc in range(NTC):
                v.drain()
                v.reduce_sum(sms[:, tc:tc + 1], yh[:, 1024 * tc:1024 * (tc + 1)],
                             axis=AX.X).then_inc(C.sem("vv"), 1)
    C.c["ev"] = ev0 + NTC
    C.c["vv"] = vv0 + NTC

    ev0 = C.c["ev"]; vv0 = C.c["vv"]
    with nc.Block() as blk:
        @blk.vector
        def _(v):
            v.tensor_scalar_mul(sms[:, :], sms[:, :], 1.0 / D)
            v.drain()
            v.tensor_scalar_mul(sqs[:, :], sqs[:, :], 1.0 / D)
            v.drain()
            v.tensor_tensor(rst2[:, :], sms[:, :], sms[:, :], op=ALU.mult)
            v.drain()
            v.tensor_tensor(sqs[:, :], sqs[:, :], rst2[:, :], op=ALU.subtract)
            v.drain()
            v.tensor_scalar_add(sqs[:, :], sqs[:, :], EPS).then_inc(C.sem("vv"), 1)
    C.c["vv"] = vv0 + 1

    ev0 = C.c["ev"]; vv0 = C.c["vv"]
    with nc.Block() as blk:
        @blk.scalar
        def _(s):
            s.activation(out=rst2[:, :], in_=sqs[:, :], func=AF.Ln)
            s.drain()
            s.activation(out=rst2[:, :], in_=rst2[:, :], func=AF.Exp, scale=-0.5
                         ).then_inc(C.sem("ev"), 1)

        @blk.vector
        def _(v):
            v.wait_ge(C.sem("ev"), ev0 + 1)
            v.scalar_tensor_tensor(bc2[:, :], sms[:, :], -1.0, rst2[:, :],
                                   op0=ALU.mult, op1=ALU.mult).then_inc(C.sem("vv"), 1)
    C.c["ev"] = ev0 + 1
    C.c["vv"] = vv0 + 1

    ld0 = C.c["ld"]; mm0 = C.c["mm"]; vv0 = C.c["vv"]; st0 = C.c["st"]
    xst = C.t["XST"]
    with nc.Block() as blk:
        @blk.sync
        def _(sync):
            for tc in range(NTC):
                if tc > 0:
                    sync.wait_ge(C.sem("mm"), mm0 + tc)
                for dc in range(NDC):
                    sync.dma_start(out=xst[:, 128 * dc:128 * (dc + 1)],
                                   in_=x_in[128 * dc:128 * (dc + 1),
                                            128 * tc:128 * (tc + 1)]
                                   ).then_inc(C.sem("ld"), 16)

        @blk.tensor
        def _(t):
            for tc in range(NTC):
                t.wait_ge(C.sem("ld"), ld0 + 128 * (tc + 1))
                if tc >= 2:
                    t.wait_ge(C.sem("vv"), vv0 + 2 * (tc - 1))
                tri_ = None
                for dc in range(NDC):
                    tri_ = t.transpose(
                        psb[:, 1024 * (tc % 2) + 128 * dc:1024 * (tc % 2) + 128 * (dc + 1)],
                        xst[:, 128 * dc:128 * (dc + 1)], id32[:, :])
                tri_.then_inc(C.sem("mm"), 1)

        @blk.vector
        def _(v):
            for tc in range(NTC):
                v.wait_ge(C.sem("mm"), mm0 + tc + 1)
                if tc > 0:
                    v.wait_ge(C.sem("st"), st0 + 16 * tc)
                v.scalar_tensor_tensor(tmp1[:, 0:1024],
                                       yh[:, 1024 * tc:1024 * (tc + 1)],
                                       rst2[:, tc:tc + 1],
                                       psb[:, 1024 * (tc % 2):1024 * (tc % 2) + 1024],
                                       op0=ALU.mult, op1=ALU.add
                                       ).then_inc(C.sem("vv"), 1)
                v.drain()
                v.tensor_scalar(tmp1[:, 0:1024], tmp1[:, 0:1024],
                                bc2[:, tc:tc + 1], None, op0=ALU.add
                                ).then_inc(C.sem("vv"), 1)
                v.drain()

        @blk.scalar
        def _(s):
            for tc in range(NTC):
                s.wait_ge(C.sem("vv"), vv0 + 2 * (tc + 1))
                s.dma_start(out=x_out[128 * tc:128 * (tc + 1), :],
                            in_=tmp1[:, 0:1024]).then_inc(C.sem("st"), 16)
            s.wait_ge(C.sem("st"), st0 + 16 * NTC)
    C.c["ld"] = ld0 + 128 * NTC
    C.c["mm"] = mm0 + NTC
    C.c["vv"] = vv0 + 2 * NTC
    C.c["st"] = st0 + 16 * NTC


# version 15
# speedup vs baseline: 1.0154x; 1.0154x over previous
import math
from contextlib import ExitStack

import numpy as np
import ml_dtypes
import concourse.bass as bass
import concourse.mybir as mybir
from concourse.bass_utils import run_bass_kernel_spmd

F32 = mybir.dt.float32
BF16 = mybir.dt.bfloat16
AF = mybir.ActivationFunctionType
ALU = mybir.AluOpType
AX = mybir.AxisListType
bfdt = ml_dtypes.bfloat16

D = 1024
T = 2048
NCORES = 8
NDC = 8
NTC = 16
VSP = 1408
NQB = 4
HEADS = 16
HD = 64
CAPACITY = 5120
EPS = 1e-5
NEG = -1.0e9

_MODE = "fused"
_DEBUG_DUMP = []



def _host_shared(inputs):
    g = {}
    f32 = np.float32
    g["b0_wq"] = np.ascontiguousarray(inputs["L0_attn_wq"], f32)
    g["b0_wk"] = np.ascontiguousarray(inputs["L0_attn_wk"], f32)
    g["b0_wv"] = np.ascontiguousarray(inputs["L0_attn_wv"], f32)
    g["b0_wo"] = np.ascontiguousarray(inputs["L0_attn_wo"], f32)
    g["b1_wgT"] = np.ascontiguousarray(inputs["L0_ffn_w_gate"].T, f32)
    g["b1_eb"] = np.ascontiguousarray(inputs["L0_ffn_expert_bias"].reshape(1, 8), f32)
    g["b1_sw1T"] = np.ascontiguousarray(inputs["L0_ffn_sw1"].T, f32)
    g["b1_sw3T"] = np.ascontiguousarray(inputs["L0_ffn_sw3"].T, f32)
    g["b1_sw2T"] = np.ascontiguousarray(inputs["L0_ffn_sw2"].T, f32)
    g["b1_w1"] = np.ascontiguousarray(inputs["L0_ffn_w1"].reshape(8 * 1024, 512), f32)
    g["b1_w3"] = np.ascontiguousarray(inputs["L0_ffn_w3"].reshape(8 * 1024, 512), f32)
    g["b1_w2"] = np.ascontiguousarray(inputs["L0_ffn_w2"].reshape(8 * 512, 1024), f32)
    g["b2_wq"] = np.ascontiguousarray(inputs["L1_attn_Wq"], f32)
    g["b2_wk"] = np.ascontiguousarray(inputs["L1_attn_Wk"], f32)
    g["b2_wv"] = np.ascontiguousarray(inputs["L1_attn_Wv"], f32)
    g["b2_wf"] = np.ascontiguousarray(inputs["L1_attn_Wf"], f32)
    for nm, key in (("bq", "L1_attn_bq"), ("bk", "L1_attn_bk"),
                    ("bv", "L1_attn_bv"), ("bf", "L1_attn_bf")):
        g["b2_" + nm] = np.ascontiguousarray(
            inputs[key].reshape(NDC, 128).T, f32)
    g["b3_gwT"] = np.ascontiguousarray(inputs["L1_ffn_gate_w"].T, f32)
    g["b3_wegT"] = np.ascontiguousarray(
        inputs["L1_ffn_we_gate"].transpose(0, 2, 1).reshape(8 * 1024, 1024), f32)
    g["b3_weuT"] = np.ascontiguousarray(
        inputs["L1_ffn_we_up"].transpose(0, 2, 1).reshape(8 * 1024, 1024), f32)
    g["b3_wedT"] = np.ascontiguousarray(
        inputs["L1_ffn_we_down"].transpose(0, 2, 1).reshape(8 * 1024, 1024), f32)
    g["b3_wsgT"] = np.ascontiguousarray(inputs["L1_ffn_ws_gate"].T, f32)
    g["b3_wsuT"] = np.ascontiguousarray(inputs["L1_ffn_ws_up"].T, f32)
    g["b3_wsdT"] = np.ascontiguousarray(inputs["L1_ffn_ws_down"].T, f32)

    g["c_id32"] = np.eye(128, dtype=f32)
    g["c_idbf"] = np.eye(128, dtype=f32).astype(bfdt)
    jj = np.arange(128)[None, :]
    pp = np.arange(128)[:, None]
    g["c_tri_fwd"] = np.where(pp <= jj, 0.0, NEG).astype(f32)
    g["c_tri_bwd"] = np.where(pp >= jj, 0.0, NEG).astype(f32)
    inv_freq = 1.0 / (10000.0 ** (np.arange(0, HD, 2, dtype=np.float64) / HD))
    ang = np.concatenate(
        [(128 * tc + np.arange(128, dtype=np.float64))[:, None] * inv_freq[None, :]
         for tc in range(NTC)], axis=1)
    g["c_cos"] = np.cos(ang).astype(f32)
    g["c_sin"] = np.sin(ang).astype(f32)
    half = HEADS // 2
    base = 24.0 ** (1.0 / half)
    slopes = 1.0 / base ** np.arange(1, half + 1, dtype=np.float64)
    tab = np.zeros((128, HEADS * 16), dtype=f32)
    for h in range(HEADS):
        s = slopes[h % half] * (1.0 if h < half else -1.0)
        for kc in range(16):
            tab[:, 16 * h + kc] = (s * (np.arange(128) + 128 * kc)).astype(f32)
    g["c_slopek"] = tab
    g["c_ustrict"] = np.triu(np.ones((128, 128), dtype=f32), 1)
    g["c_u16"] = np.triu(np.ones((16, 16), dtype=f32), 1)
    qb = np.zeros((HEADS, T), dtype=f32)
    for h in range(HEADS):
        s = slopes[h % half] * (1.0 if h < half else -1.0)
        qb[h] = (-s * np.arange(T)).astype(f32)
    g["c_qbias"] = qb
    g["c_qbias_bf"] = qb.astype(bfdt)
    return g


def _host_core(core):
    f32 = np.float32
    g = {}
    pos = np.zeros((128, 32), dtype=f32)
    for tc in range(NTC):
        for k in range(2):
            pos[:, 2 * tc + k] = 4096 * core + 256 * tc + 2 * np.arange(128) + k
    g["c_pos"] = pos
    g["c_maskcol"] = (np.arange(8) < core).astype(f32).reshape(8, 1)
    g["c_coreflag"] = np.full((128, 1), 1.0 if core == 0 else 0.0, dtype=f32)
    return g


def _host_inputs(inputs):
    shared = _host_shared(inputs)
    x = inputs["L0_attn_x"]
    maps = []
    for c in range(NCORES):
        m = dict(shared)
        m.update(_host_core(c))
        m["x_in"] = np.ascontiguousarray(x[c].T.astype(np.float32))
        maps.append(m)
    return maps



class Ctx:
    def __init__(self, nc, stack):
        self.nc = nc
        self.stack = stack
        self.t = {}
        self._sems = {}
        self.c = {}

    def sem(self, name):
        if name not in self._sems:
            self._sems[name] = self.stack.enter_context(self.nc.semaphore(name))
            self.c[name] = 0
        return self._sems[name]

    def bump(self, name, n=1):
        self.sem(name)
        self.c[name] += n
        return self.c[name]

    def sb(self, name, shape, dtype, stack=None):
        self._n = getattr(self, "_n", {})
        k = self._n.get(name, 0)
        self._n[name] = k + 1
        uname = name if k == 0 else f"{name}_{k}"
        h = (stack or self.stack).enter_context(
            self.nc.sbuf_tensor(uname, shape, dtype))
        self.t[name] = h
        return h

    def ps(self, name, shape, dtype=F32, stack=None):
        h = (stack or self.stack).enter_context(
            self.nc.psum_tensor(name, shape, dtype))
        self.t[name] = h
        return h




def _wview(dram, nch, ncols, row0=0, col0=0):
    rs = dram.shape[1]
    return bass.AP(dram, row0 * rs + col0, [[rs, 128], [128 * rs, nch], [1, ncols]])



def emit_ln_stats_from_dram(C, x_dram, xst2=None, sq2=None):
    nc = C.nc
    xst, tmp1 = C.t["XST"], C.t["TMP1"]
    ones128f = C.t["ones128f"]
    psa, psb = C.t["PSA"], C.t["PSB"]
    ld0 = C.c["ld"]; mm0 = C.c["mm"]; ev0 = C.c["ev"]; vv_0 = C.c["vv"]
    gg_0 = C.c["gg"]
    depth = 2 if xst2 is not None else 1

    def stg(c):
        return xst[:, 0:T] if (xst2 is None or c % 2 == 0) else xst2

    def sqb(c):
        return tmp1[:, 0:T] if (sq2 is None or c % 2 == 0) else sq2

    la0 = C.c["lda"] if "lda" in C.c else C.bump("lda", 0)
    lb0 = C.c["ldb"] if "ldb" in C.c else C.bump("ldb", 0)

    def ldsem(c):
        return C.sem("lda") if c % 2 == 0 else C.sem("ldb")

    def ldval(c):
        return (la0 if c % 2 == 0 else lb0) + 16 * (c // 2 + 1)

    with nc.Block() as blk:
        @blk.sync
        def _(sync):
            for c in range(NDC):
                if c >= depth:
                    sync.wait_ge(C.sem("mm"), mm0 + c - depth + 1)
                sync.dma_start(out=stg(c), in_=x_dram[128 * c:128 * (c + 1), :]
                               ).then_inc(ldsem(c), 16)

        @blk.vector
        def _(v):
            for c in range(NDC):
                v.wait_ge(ldsem(c), ldval(c))
                if c >= depth:
                    v.wait_ge(C.sem("mm"), mm0 + c - depth + 1)
                v.drain()
                v.tensor_tensor(sqb(c)[:, 0:VSP], stg(c)[:, 0:VSP],
                                stg(c)[:, 0:VSP], op=ALU.mult
                                ).then_inc(C.sem("vv"), 1)

        @blk.gpsimd
        def _(g):
            for c in range(NDC):
                g.wait_ge(ldsem(c), ldval(c))
                if c >= depth:
                    g.wait_ge(C.sem("mm"), mm0 + c - depth + 1)
                g.drain()
                g.tensor_tensor(sqb(c)[:, VSP:T], stg(c)[:, VSP:T],
                                stg(c)[:, VSP:T], op=ALU.mult
                                ).then_inc(C.sem("gg"), 1)

        @blk.tensor
        def _(t):
            for c in range(NDC):
                t.wait_ge(C.sem("vv"), vv_0 + c + 1)
                t.wait_ge(C.sem("gg"), gg_0 + c + 1)
                for s4 in range(4):
                    t.matmul(psa[0:1, 512 * s4:512 * (s4 + 1)], ones128f[:, :],
                             stg(c)[:, 512 * s4:512 * (s4 + 1)],
                             start=(c == 0), stop=(c == NDC - 1))
                for s4 in range(4):
                    mmi = t.matmul(psb[0:1, 512 * s4:512 * (s4 + 1)], ones128f[:, :],
                                   sqb(c)[:, 512 * s4:512 * (s4 + 1)],
                                   start=(c == 0), stop=(c == NDC - 1))
                mmi.then_inc(C.sem("mm"), 1)
    C.c["lda"] = la0 + 16 * (NDC // 2)
    C.c["ldb"] = lb0 + 16 * (NDC // 2)
    C.c["vv"] = vv_0 + NDC
    C.c["gg"] = gg_0 + NDC
    C.c["mm"] = mm0 + NDC


def emit_ln_tail(C, width, col0, mode, x_dram=None, out_bf=None,
                 ysrc=None, store=None, extra_store_wait=True,
                 xst2=None, sq2=None, acp=None, bcp=None):
    nc = C.nc
    xst, tmp1 = C.t["XST"], C.t["TMP1"]
    id32 = C.t["id32"]
    ones1f = C.t["ones1f"]
    psa, psb = C.t["PSA"], C.t["PSB"]
    stt, mean, var, rstd, bcol = (C.t["STT"], C.t["MEAN"], C.t["VAR"],
                                  C.t["RSTD"], C.t["BCOL"])
    arow = C.t["XST"]
    brow = C.t["TMP1"]
    ntile = width // 128
    nslice = width // 512

    ev0 = C.c["ev"]; mm0 = C.c["mm"]; vv0 = C.c["vv"]
    with nc.Block() as blk:
        @blk.scalar
        def _(s):
            s.copy(arow[0:1, 0:width], psa[0:1, 0:width])
            s.copy(brow[0:1, 0:width], psb[0:1, 0:width]).then_inc(C.sem("ev"), 1)

        @blk.tensor
        def _(t):
            t.wait_ge(C.sem("ev"), ev0 + 1)
            tri_ = None
            for q, rsrc in ((0, arow), (1, brow)):
                for tl in range(ntile):
                    tri_ = t.transpose(psa[:, ntile * q + tl:ntile * q + tl + 1],
                                       rsrc[0:1, 128 * tl:128 * (tl + 1)],
                                       id32[0:1, 0:1])
            tri_.then_inc(C.sem("mm"), 1)

        @blk.vector
        def _(v):
            v.wait_ge(C.sem("mm"), mm0 + 1)
            v.tensor_copy(stt[:, 0:2 * ntile], psa[:, 0:2 * ntile])
            v.drain()
            v.tensor_scalar_mul(mean[:, 0:ntile], stt[:, 0:ntile], 1.0 / D)
            v.tensor_scalar_mul(var[:, 0:ntile], stt[:, ntile:2 * ntile], 1.0 / D)
            v.drain()
            v.tensor_tensor(stt[:, 48:48 + ntile], mean[:, 0:ntile], mean[:, 0:ntile],
                            op=ALU.mult)
            v.drain()
            v.tensor_tensor(var[:, 0:ntile], var[:, 0:ntile], stt[:, 48:48 + ntile],
                            op=ALU.subtract)
            v.drain()
            v.tensor_scalar_add(var[:, 0:ntile], var[:, 0:ntile], EPS
                                ).then_inc(C.sem("vv"), 1)
    C.c["ev"] = ev0 + 1
    C.c["mm"] = mm0 + 1
    C.c["vv"] = vv0 + 1

    ev0 = C.c["ev"]; mm0 = C.c["mm"]; vv0 = C.c["vv"]
    with nc.Block() as blk:
        @blk.scalar
        def _(s):
            s.activation(out=rstd[:, 0:ntile], in_=var[:, 0:ntile], func=AF.Ln)
            s.drain()
            s.activation(out=rstd[:, 0:ntile], in_=rstd[:, 0:ntile], func=AF.Exp,
                         scale=-0.5).then_inc(C.sem("ev"), 1)

        @blk.vector
        def _(v):
            v.wait_ge(C.sem("ev"), ev0 + 1)
            v.scalar_tensor_tensor(bcol[:, 0:ntile], mean[:, 0:ntile], -1.0,
                                   rstd[:, 0:ntile], op0=ALU.mult, op1=ALU.mult
                                   ).then_inc(C.sem("vv"), 1)

        @blk.tensor
        def _(t):
            t.wait_ge(C.sem("vv"), vv0 + 1)
            for tl in range(ntile):
                t.transpose(psa[0:1, 128 * tl:128 * (tl + 1)],
                            rstd[:, tl:tl + 1], id32[:, :])
            tri_ = None
            for tl in range(ntile):
                tri_ = t.transpose(psb[0:1, 128 * tl:128 * (tl + 1)],
                                   bcol[:, tl:tl + 1], id32[:, :])
            tri_.then_inc(C.sem("mm"), 1)
    C.c["ev"] = ev0 + 1
    C.c["vv"] = vv0 + 1
    C.c["mm"] = mm0 + 1

    ev0 = C.c["ev"]; mm0 = C.c["mm"]
    wsp = (width * 11 // 16) // 64 * 64
    with nc.Block() as blk:
        @blk.scalar
        def _(s):
            s.copy(arow[0:1, 0:width], psa[0:1, 0:width])
            s.copy(brow[0:1, 0:width], psb[0:1, 0:width]).then_inc(C.sem("ev"), 1)
            s.wait_ge(C.sem("mm"), mm0 + 1)
            s.copy(acp[:, wsp:width], psa[:, wsp:width])
            s.copy(bcp[:, wsp:width], psb[:, wsp:width]).then_inc(C.sem("ev"), 1)

        @blk.tensor
        def _(t):
            t.wait_ge(C.sem("ev"), ev0 + 1)
            for s4 in range(nslice):
                t.matmul(psa[:, 512 * s4:512 * (s4 + 1)], ones1f[:, :],
                         arow[0:1, 512 * s4:512 * (s4 + 1)], start=True, stop=True)
            mmi = None
            for s4 in range(nslice):
                mmi = t.matmul(psb[:, 512 * s4:512 * (s4 + 1)], ones1f[:, :],
                               brow[0:1, 512 * s4:512 * (s4 + 1)], start=True, stop=True)
            mmi.then_inc(C.sem("mm"), 1)
    C.c["ev"] = ev0 + 2
    C.c["mm"] = mm0 + 1
    cp_done = C.c["ev"]

    ld0 = C.c["ld"]; vv0 = C.c["vv"]; mm_done = C.c["mm"]; st0 = C.c["st"]
    depth = 2 if xst2 is not None else 1

    def stg(c):
        return (xst[:, 0:width] if (xst2 is None or c % 2 == 0)
                else xst2[:, 0:width])

    def sqb(c):
        return (tmp1[:, 0:width] if (sq2 is None or c % 2 == 0)
                else sq2[:, 0:width])

    la0 = C.c["lda"]
    lb0 = C.c["ldb"]
    sa0 = C.c["sta"] if "sta" in C.c else C.bump("sta", 0)
    sb0 = C.c["stb"] if "stb" in C.c else C.bump("stb", 0)

    def ldsem(c):
        return C.sem("lda") if c % 2 == 0 else C.sem("ldb")

    def ldval(c):
        return (la0 if c % 2 == 0 else lb0) + 16 * (c // 2 + 1)

    def stsem(c):
        return C.sem("sta") if c % 2 == 0 else C.sem("stb")

    def stval(c):
        return (sa0 if c % 2 == 0 else sb0) + 16 * (c // 2 + 1)

    gg0 = C.c["gg"]

    with nc.Block() as blk:
        @blk.sync
        def _(sync):
            for c in range(NDC):
                if mode == "post" and c >= depth:
                    sync.wait_ge(stsem(c - depth), stval(c - depth))
                elif c >= depth:
                    sync.wait_ge(C.sem("vv"), vv0 + 2 * (c - depth + 1))
                    sync.wait_ge(C.sem("gg"), gg0 + 2 * (c - depth + 1))
                sync.dma_start(out=stg(c),
                               in_=x_dram[128 * c:128 * (c + 1), col0:col0 + width]
                               ).then_inc(ldsem(c), 16)

        def apply_chunks(eng, lo, hi, sem_name, atile, btile):
            for c in range(NDC):
                eng.wait_ge(ldsem(c), ldval(c))
                if mode == "post" and depth == 2 and c >= depth:
                    eng.wait_ge(stsem(c - depth), stval(c - depth))
                eng.drain()
                if mode == "pre":
                    eng.tensor_tensor(sqb(c)[:, lo:hi], stg(c)[:, lo:hi], atile,
                                      op=ALU.mult)
                    eng.drain()
                    eng.tensor_tensor(
                        out_bf[:, T * c + col0 + lo:T * c + col0 + hi],
                        sqb(c)[:, lo:hi], btile, op=ALU.add
                        ).then_inc(C.sem(sem_name), 2)
                else:
                    eng.tensor_tensor(sqb(c)[:, lo:hi], ysrc(c)[:, lo:hi], atile,
                                      op=ALU.mult)
                    eng.drain()
                    eng.tensor_tensor(sqb(c)[:, lo:hi], sqb(c)[:, lo:hi], btile,
                                      op=ALU.add)
                    eng.drain()
                    eng.tensor_tensor(stg(c)[:, lo:hi], sqb(c)[:, lo:hi],
                                      stg(c)[:, lo:hi], op=ALU.add
                                      ).then_inc(C.sem(sem_name), 2)

        @blk.vector
        def _(v):
            v.wait_ge(C.sem("mm"), mm_done)
            apply_chunks(v, 0, wsp, "vv", psa[:, 0:wsp], psb[:, 0:wsp])

        @blk.gpsimd
        def _(g):
            g.wait_ge(C.sem("ev"), cp_done)
            apply_chunks(g, wsp, width, "gg",
                         acp[:, wsp:width], bcp[:, wsp:width])

        if mode == "post":
            @blk.scalar
            def _(s):
                for c in range(NDC):
                    s.wait_ge(C.sem("vv"), vv0 + 2 * (c + 1))
                    s.wait_ge(C.sem("gg"), gg0 + 2 * (c + 1))
                    s.dma_start(out=store[128 * c:128 * (c + 1), col0:col0 + width],
                                in_=stg(c)).then_inc(stsem(c), 16)
                if extra_store_wait:
                    s.wait_ge(C.sem("sta"), sa0 + 16 * (NDC // 2))
                    s.wait_ge(C.sem("stb"), sb0 + 16 * (NDC // 2))
    C.c["lda"] = la0 + 16 * (NDC // 2)
    C.c["ldb"] = lb0 + 16 * (NDC // 2)
    C.c["vv"] = vv0 + 2 * NDC
    C.c["gg"] = gg0 + 2 * NDC
    if mode == "post":
        C.c["sta"] = sa0 + 16 * (NDC // 2)
        C.c["stb"] = sb0 + 16 * (NDC // 2)



def _kbs_phase1(direction, qc):
    return list(range(0, qc // 4 + 1)) if direction == "fwd" else \
        list(range(qc // 4, 4))


def _kcs_phase2(direction, qb):
    return list(range(0, 4 * qb + 4)) if direction == "fwd" else \
        list(range(4 * qb, 16))


def emit_attn(C, layer, x_in, x_out, bs):
    nc = C.nc
    xn = C.t["XN"]
    scr = C.t["SCR16"]
    psa, psb = C.t["PSA"], C.t["PSB"]
    id32, idbf = C.t["id32"], C.t["idbf"]
    ones1f = C.t["ones1f"]
    tmp1 = C.t["TMP1"]

    qh = C.sb("QH", [128, 8192], BF16, bs)
    kh = C.sb("KH", [128, 8192], BF16, bs)
    vh = C.sb("VH", [128, 8320], BF16, bs)
    oh = C.sb("OH", [128, 8192], BF16, bs)
    pt4 = C.t["XST"][:, :].bitcast(BF16)
    C.t["OTMP"] = C.t["TMP1"][0:64, 1024:1280].bitcast(BF16)
    ptm = C.t["TMP1"][:, 0:512]
    ktm = C.t["TMP1"][:, 512:1024]
    roq = C.sb("ROQ", [128, 1024], BF16, bs)
    rok = C.sb("ROK", [128, 1024], BF16, bs)
    mhall = C.sb("MHALL", [128, 128], F32, bs)
    mqc = C.sb("MQC", [16, 8], F32, bs)
    m8 = C.sb("M8", [8, 16], F32, bs)
    mrows = C.sb("MROWS", [1, 160], F32, bs)
    mcols32 = C.sb("MCOLS32", [128, 32], F32, bs)
    biast = C.sb("BIAST", [128, 512], F32, bs)
    lrec = C.t["TMP1"][0:1, 0:512]
    lb = C.t["TMP1"][0:64, 512:1024]
    wsa = C.sb("WSA", [128, 4096], BF16, bs)
    wsb = C.sb("WSB", [128, 4096], BF16, bs)
    wsc = C.sb("WSC", [128, 4096], BF16, bs)
    qbra = qbrb = None
    if layer == 1:
        bqc = C.sb("BQC", [128, 8], F32, bs)
        bkc = C.sb("BKC", [128, 8], F32, bs)
        bvc = C.sb("BVC", [128, 8], F32, bs)
        bfc = C.sb("BFC", [128, 8], F32, bs)
        bvb = C.sb("BVB", [128, 8], BF16, bs)
        qbra = C.sb("QBRA", [1, 2048], BF16, bs)
        qbrb = C.sb("QBRB", [1, 2048], BF16, bs)
        bfold = C.sb("BFOLD", [128, 8], F32, bs)
        bfold2 = C.sb("BFOLD2", [128, 8], F32, bs)
    wq_d = C.t["b0_wq" if layer == 0 else "b2_wq"]
    wk_d = C.t["b0_wk" if layer == 0 else "b2_wk"]
    wv_d = C.t["b0_wv" if layer == 0 else "b2_wv"]
    wo_d = C.t["b0_wo" if layer == 0 else "b2_wf"]

    def load_w_half(g, dram, slot, half):
        for dc in range(NDC):
            g.dma_start(out=slot[:, 512 * dc:512 * (dc + 1)],
                        in_=dram[128 * dc:128 * (dc + 1), 512 * half:512 * (half + 1)]
                        ).then_inc(C.sem("gd"), 16)
        C.c["gd"] += 16 * 8

    def load_wo_half(g, slot, half):
        for rc in range(4):
            g.dma_start(out=slot[:, 1024 * rc:1024 * (rc + 1)],
                        in_=wo_d[512 * half + 128 * rc:512 * half + 128 * (rc + 1), :]
                        ).then_inc(C.sem("gd"), 16)
        C.c["gd"] += 16 * 4

    with nc.Block() as blk:
        @blk.gpsimd
        def _(g):
            load_w_half(g, wq_d, wsa, 0)
            load_w_half(g, wk_d, wsb, 0)
            load_w_half(g, wv_d, wsc, 0)
        if layer == 1:
            ld0 = C.c["ld"]

            @blk.sync
            def _(sync):
                for dst, nm in ((bqc, "b2_bq"), (bkc, "b2_bk"),
                                (bvc, "b2_bv"), (bfc, "b2_bf")):
                    sync.dma_start(out=dst[:, :], in_=C.t[nm][:, :]
                                   ).then_inc(C.sem("ld"), 16)

            @blk.vector
            def _(v):
                v.wait_ge(C.sem("ld"), ld0 + 64)
                v.tensor_scalar_mul(bqc[:, :], bqc[:, :], 0.125)
                v.tensor_copy(bvb[:, :], bvc[:, :])
            C.c["ld"] = ld0 + 64

    axst2 = qh[:, 0:4096].bitcast(F32)
    asq2 = kh[:, 0:4096].bitcast(F32)
    emit_ln_stats_from_dram(C, x_in, xst2=axst2, sq2=asq2)
    emit_ln_tail(C, T, 0, "pre", x_dram=x_in, out_bf=xn, xst2=axst2, sq2=asq2,
                 acp=vh[:, 0:4096].bitcast(F32), bcp=vh[:, 4096:8192].bitcast(F32))

    for half in range(2):
        direction_of = lambda hh: ("fwd" if (8 * half + hh) < 8 else "bwd") \
            if layer == 1 else "fwd"

        gd_need = C.c["gd"]
        if layer == 0:
            _qkv_l0(C, half, gd_need, qh, kh, vh, ptm, ktm, roq, rok,
                    wsa, wsb, wsc)
        else:
            _qkv_l1(C, half, gd_need, qh, kh, vh, ptm, wsa, wsb, wsc, bqc, bkc)

        woslot = wsc if half == 0 else wsa
        with nc.Block() as blk:
            @blk.gpsimd
            def _(g):
                if half == 0:
                    load_w_half(g, wq_d, wsa, 1)
                    load_w_half(g, wk_d, wsb, 1)
                load_wo_half(g, woslot, half)
        wo_gd_need = C.c["gd"]

        with nc.Block() as blk:
            @blk.vector
            def _(v):
                ap = vh.ap().rearrange("p (tc h j) -> p tc h j", h=8, j=65)
                v.memset(ap[:, :, :, 64:65], 1.0)

        _attn_half(C, layer, half, qh, kh, vh, oh, pt4,
                   mhall, mqc, m8, mrows, mcols32, biast, lrec, lb, qbra, qbrb)

        if layer == 1:
            _bfold(C, half, woslot, C.c["gd"], bvb, bfc, bfold, bfold2)

        _wo_proj(C, layer, half, woslot, C.c["gd"], oh, scr,
                 bfold if layer == 1 else None,
                 bfold2 if layer == 1 else None)

        if half == 0:
            with nc.Block() as blk:
                @blk.gpsimd
                def _(g):
                    load_w_half(g, wv_d, wsc, 1)

    def ysrc(c):
        return scr[:, T * c:T * (c + 1)]
    emit_ln_stats_mid(C, ysrc, T, src_dtype=BF16)
    emit_ln_tail(C, T, 0, "post", x_dram=x_in, ysrc=ysrc, store=x_out,
                 xst2=axst2, sq2=asq2,
                 acp=vh[:, 0:4096].bitcast(F32), bcp=vh[:, 4096:8192].bitcast(F32))


def emit_ln_stats_mid(C, src_ap_fn, width, src_dtype=F32, nchunk=NDC):
    nc = C.nc
    tmp1 = C.t["TMP1"]
    ones128f, ones128b = C.t["ones128f"], C.t["ones128b"]
    psa, psb = C.t["PSA"], C.t["PSB"]
    mm0 = C.c["mm"]; ev0 = C.c["ev"]
    nslice = width // 512
    ones = ones128f if src_dtype == F32 else ones128b
    vv_0 = C.c["vv"]; gg_0 = C.c["gg"]
    wsp = (width * 11 // 16) // 64 * 64
    with nc.Block() as blk:
        @blk.vector
        def _(v):
            for c in range(nchunk):
                if c > 0:
                    v.wait_ge(C.sem("mm"), mm0 + c)
                v.drain()
                v.tensor_tensor(tmp1[:, 0:wsp], src_ap_fn(c)[:, 0:wsp],
                                src_ap_fn(c)[:, 0:wsp],
                                op=ALU.mult).then_inc(C.sem("vv"), 1)

        @blk.gpsimd
        def _(g):
            for c in range(nchunk):
                if c > 0:
                    g.wait_ge(C.sem("mm"), mm0 + c)
                g.drain()
                g.tensor_tensor(tmp1[:, wsp:width], src_ap_fn(c)[:, wsp:width],
                                src_ap_fn(c)[:, wsp:width],
                                op=ALU.mult).then_inc(C.sem("gg"), 1)

        @blk.tensor
        def _(t):
            for c in range(nchunk):
                t.wait_ge(C.sem("vv"), vv_0 + c + 1)
                t.wait_ge(C.sem("gg"), gg_0 + c + 1)
                for s4 in range(nslice):
                    t.matmul(psa[0:1, 512 * s4:512 * (s4 + 1)], ones[:, :],
                             src_ap_fn(c)[:, 512 * s4:512 * (s4 + 1)],
                             start=(c == 0), stop=(c == nchunk - 1))
                for s4 in range(nslice):
                    mmi = t.matmul(psb[0:1, 512 * s4:512 * (s4 + 1)], ones128f[:, :],
                                   tmp1[:, 512 * s4:512 * (s4 + 1)],
                                   start=(c == 0), stop=(c == nchunk - 1))
                mmi.then_inc(C.sem("mm"), 1)
    C.c["vv"] = vv_0 + nchunk
    C.c["gg"] = gg_0 + nchunk
    C.c["mm"] = mm0 + nchunk


def _qkv_l0(C, half, gd_need, qh, kh, vh, ptm, ktm, roq, rok,
            wsa, wsb, wsc):
    nc = C.nc
    xn = C.t["XN"]
    psa, psb = C.t["PSA"], C.t["PSB"]
    idbf = C.t["idbf"]
    cos, sin = C.t["cos"], C.t["sin"]
    tmp1 = C.t["TMP1"]
    mm0 = C.c["mm"]; ev0 = C.c["ev"]; vv0 = C.c["vv"]; gg0 = C.c["gg"]

    xstf = C.t["XST"]

    def pp(tc):
        return psa if tc % 2 == 0 else psb

    def tp(tc):
        return (psb if tc % 2 == 0 else psa)[:, :].bitcast(BF16)

    def ptmf(tc):
        return tmp1[:, 0:512] if tc % 2 == 0 else tmp1[:, 1024:1536]

    def ktmf(tc):
        return tmp1[:, 512:1024] if tc % 2 == 0 else tmp1[:, 1536:2048]

    def rq(tc):
        return roq[:, 512 * (tc % 2):512 * (tc % 2) + 512]

    def rk(tc):
        return rok[:, 512 * (tc % 2):512 * (tc % 2) + 512]

    def rope(eng, src, dst, tc, toff, cnt_sem):
        si = src.rearrange("p (h f two) -> p h f two", two=2, f=32)
        do = dst.rearrange("p (h f two) -> p h f two", two=2, f=32)
        ev_i, od_i = si[:, :, :, 0], si[:, :, :, 1]
        ev_o, od_o = do[:, :, :, 0], do[:, :, :, 1]
        sl = [xstf[:, toff + 256 * k:toff + 256 * (k + 1)].rearrange(
            "p (h f) -> p h f", f=32) for k in range(4)]
        cosb = bass.AP(cos, 32 * tc, [[512, 128], [0, 8], [1, 32]])
        sinb = bass.AP(sin, 32 * tc, [[512, 128], [0, 8], [1, 32]])
        eng.drain()
        eng.tensor_tensor(sl[0], ev_i, cosb, op=ALU.mult)
        eng.tensor_tensor(sl[1], od_i, sinb, op=ALU.mult)
        eng.tensor_tensor(sl[2], ev_i, sinb, op=ALU.mult)
        eng.tensor_tensor(sl[3], od_i, cosb, op=ALU.mult)
        eng.drain()
        eng.tensor_tensor(ev_o, sl[0], sl[1], op=ALU.subtract)
        eng.tensor_tensor(od_o, sl[2], sl[3], op=ALU.add
                          ).then_inc(C.sem(cnt_sem), 1)

    def emit_transp(t, tc):
        t.wait_ge(C.sem("vv"), vv0 + tc + 1)
        t.wait_ge(C.sem("gg"), gg0 + tc + 1)
        if tc >= 2:
            t.wait_ge(C.sem("ev"), ev0 + 8 * (tc - 2) + 8)
        for i in range(4):
            t.transpose(tp(tc)[:, 3072 + 128 * i:3072 + 128 * (i + 1)],
                        rq(tc)[:, 128 * i:128 * (i + 1)], idbf[:, :])
        tri_ = None
        for i in range(4):
            tri_ = t.transpose(tp(tc)[:, 3584 + 128 * i:3584 + 128 * (i + 1)],
                               rk(tc)[:, 128 * i:128 * (i + 1)], idbf[:, :])
        tri_.then_inc(C.sem("mm"), 1)

    with nc.Block() as blk:
        @blk.tensor
        def _(t):
            t.wait_ge(C.sem("gd"), gd_need)
            for tc in range(NTC):
                if tc >= 2:
                    t.wait_ge(C.sem("ev"), ev0 + 8 * tc - 12)
                for dc in range(NDC):
                    t.matmul(pp(tc)[:, 0:512],
                             xn[:, 2048 * dc + 128 * tc:2048 * dc + 128 * (tc + 1)],
                             wsa[:, 512 * dc:512 * (dc + 1)],
                             start=(dc == 0), stop=(dc == NDC - 1))
                for dc in range(NDC):
                    t.matmul(pp(tc)[:, 512:1024],
                             xn[:, 2048 * dc + 128 * tc:2048 * dc + 128 * (tc + 1)],
                             wsb[:, 512 * dc:512 * (dc + 1)],
                             start=(dc == 0), stop=(dc == NDC - 1))
                for dc in range(NDC):
                    mmi = t.matmul(pp(tc)[:, 1024:1536],
                                   xn[:, 2048 * dc + 128 * tc:2048 * dc + 128 * (tc + 1)],
                                   wsc[:, 512 * dc:512 * (dc + 1)],
                                   start=(dc == 0), stop=(dc == NDC - 1))
                mmi.then_inc(C.sem("mm"), 1)
                if tc > 0:
                    emit_transp(t, tc - 1)
            emit_transp(t, NTC - 1)

        @blk.scalar
        def _(s):
            for tc in range(NTC):
                s.wait_ge(C.sem("mm"), mm0 + (1 if tc == 0 else 2 * tc))
                if tc >= 2:
                    s.wait_ge(C.sem("vv"), vv0 + tc - 1)
                    s.wait_ge(C.sem("gg"), gg0 + tc - 1)
                s.activation(out=ptmf(tc), in_=pp(tc)[:, 0:512], func=AF.Copy,
                             scale=0.125)
                s.activation(out=ktmf(tc), in_=pp(tc)[:, 512:1024], func=AF.Copy)
                pvh = vh.ap().rearrange("p (tc h j) -> p tc h j", h=8, j=65)
                psv = pp(tc).ap().rearrange("p (b h j) -> p b h j", b=4, j=64)[:, 2]
                s.activation(out=pvh[:, tc, :, 0:64], in_=psv, func=AF.Copy
                             ).then_inc(C.sem("ev"), 4)
                s.wait_ge(C.sem("mm"),
                          mm0 + (2 * tc + 3 if tc < NTC - 1 else 2 * NTC))
                s.copy(bass.AP(qh, 128 * tc, [[8192, 128], [2048, 4], [1, 128]]),
                       tp(tc)[:, 3072:3584])
                s.copy(bass.AP(kh, 128 * tc, [[8192, 128], [2048, 4], [1, 128]]),
                       tp(tc)[:, 3584:4096]).then_inc(C.sem("ev"), 4)

        @blk.vector
        def _(v):
            for tc in range(NTC):
                v.wait_ge(C.sem("ev"), ev0 + 8 * tc + 4)
                rope(v, ptmf(tc), rq(tc), tc, 0, "vv")

        @blk.gpsimd
        def _(g):
            for tc in range(NTC):
                g.wait_ge(C.sem("ev"), ev0 + 8 * tc + 4)
                rope(g, ktmf(tc), rk(tc), tc, 1024, "gg")
    C.c["mm"] = mm0 + 2 * NTC
    C.c["ev"] = ev0 + 8 * NTC
    C.c["vv"] = vv0 + NTC
    C.c["gg"] = gg0 + NTC


def _qkv_l1(C, half, gd_need, qh, kh, vh, ptm, wsa, wsb, wsc, bqc, bkc):
    nc = C.nc
    xn = C.t["XN"]
    psa = C.t["PSA"]
    mm0 = C.c["mm"]; vv0 = C.c["vv"]; ev0 = C.c["ev"]

    with nc.Block() as blk:
        @blk.tensor
        def _(t):
            t.wait_ge(C.sem("gd"), gd_need)
            idx = 0
            for proj, ws in ((0, wsa), (1, wsb)):
                for mc4 in range(4):
                    for ts4 in range(4):
                        if idx >= 4:
                            t.wait_ge(C.sem("vv"), vv0 + idx - 3)
                        for dc in range(NDC):
                            mmi = t.matmul(
                                psa[:, 512 * (idx % 4):512 * (idx % 4 + 1)],
                                wsa[:, 512 * dc + 128 * mc4:512 * dc + 128 * (mc4 + 1)]
                                if proj == 0 else
                                wsb[:, 512 * dc + 128 * mc4:512 * dc + 128 * (mc4 + 1)],
                                xn[:, 2048 * dc + 512 * ts4:2048 * dc + 512 * (ts4 + 1)],
                                start=(dc == 0), stop=(dc == NDC - 1))
                        mmi.then_inc(C.sem("mm"), 1)
                        idx += 1

        @blk.vector
        def _(v):
            idx = 0
            for proj in range(2):
                for mc4 in range(4):
                    for ts4 in range(4):
                        v.wait_ge(C.sem("mm"), mm0 + idx + 1)
                        dst = qh if proj == 0 else kh
                        bias = bqc if proj == 0 else bkc
                        if proj == 0:
                            v.tensor_scalar(
                                dst[:, 2048 * mc4 + 512 * ts4:2048 * mc4 + 512 * (ts4 + 1)],
                                psa[:, 512 * (idx % 4):512 * (idx % 4 + 1)],
                                0.125, bias[:, 4 * half + mc4:4 * half + mc4 + 1],
                                op0=ALU.mult, op1=ALU.add).then_inc(C.sem("vv"), 1)
                        else:
                            v.tensor_scalar(
                                dst[:, 2048 * mc4 + 512 * ts4:2048 * mc4 + 512 * (ts4 + 1)],
                                psa[:, 512 * (idx % 4):512 * (idx % 4 + 1)],
                                bias[:, 4 * half + mc4:4 * half + mc4 + 1], None,
                                op0=ALU.add).then_inc(C.sem("vv"), 1)
                        idx += 1
    C.c["mm"] = mm0 + 32
    C.c["vv"] = vv0 + 32

    mm0 = C.c["mm"]; ev0 = C.c["ev"]
    with nc.Block() as blk:
        @blk.tensor
        def _(t):
            for tc in range(NTC):
                if tc > 1:
                    t.wait_ge(C.sem("ev"), ev0 + tc - 1)
                for dc in range(NDC):
                    mmi = t.matmul(
                        psa[:, 512 * (tc % 2):512 * (tc % 2 + 1)],
                        xn[:, 2048 * dc + 128 * tc:2048 * dc + 128 * (tc + 1)],
                        wsc[:, 512 * dc:512 * (dc + 1)],
                        start=(dc == 0), stop=(dc == NDC - 1))
                mmi.then_inc(C.sem("mm"), 1)

        @blk.scalar
        def _(s):
            pvh = vh.ap().rearrange("p (tc h j) -> p tc h j", h=8, j=65)
            for tc in range(NTC):
                s.wait_ge(C.sem("mm"), mm0 + tc + 1)
                psv = psa.ap().rearrange("p (b h j) -> p b h j", b=4, j=64)[:, tc % 2]
                s.activation(out=pvh[:, tc, :, 0:64], in_=psv, func=AF.Copy
                             ).then_inc(C.sem("ev"), 1)
    C.c["mm"] = mm0 + NTC
    C.c["ev"] = ev0 + NTC


def _tile_geom(direction, qb, kc):
    m = kc - 4 * qb
    if direction == "fwd":
        if m < 0:
            return 0, 512, None
        return 128 * m, 512, 128 * m
    else:
        if m > 3:
            return 0, 512, None
        return 0, 128 * m + 128, 128 * m


def _attn_half(C, layer, half, qh, kh, vh, oh, pt4,
               mhall, mqc, m8, mrows, mcols32, biast, lrec, lb, qbra, qbrb):
    nc = C.nc
    psa, psb = C.t["PSA"], C.t["PSB"]
    id32 = C.t["id32"]
    ones1f = C.t["ones1f"]
    ones1b = C.t["ones1b"]

    def direction(hh):
        return ("fwd" if (8 * half + hh) < 8 else "bwd") if layer == 1 else "fwd"

    qsq = pt4[:, 0:2048]
    ksq = pt4[:, 2048:4096]
    mm0 = C.c["mm"]; vv0 = C.c["vv"]; ev0 = C.c["ev"]
    with nc.Block() as blk:
        @blk.scalar
        def _(s):
            for ch in range(4):
                if ch > 0:
                    s.wait_ge(C.sem("mm"), mm0 + 2 * ch)
                    s.drain()
                s.activation(out=qsq, in_=qh[:, 2048 * ch:2048 * (ch + 1)],
                             func=AF.Square)
                s.activation(out=ksq, in_=kh[:, 2048 * ch:2048 * (ch + 1)],
                             func=AF.Square).then_inc(C.sem("ev"), 1)

        @blk.tensor
        def _(t):
            ones128b = C.t["ones128b"]
            for hh in range(8):
                hb = hh % 2
                t.wait_ge(C.sem("ev"), ev0 + hh // 2 + 1)
                if hh >= 1:
                    t.wait_ge(C.sem("vv"), vv0 + hh)
                for s4 in range(4):
                    t.matmul(psa[0:1, 512 * s4:512 * (s4 + 1)],
                             ones128b[64 * hb:64 * (hb + 1), :],
                             qsq[64 * hb:64 * (hb + 1), 512 * s4:512 * (s4 + 1)],
                             start=True, stop=True)
                mmi = None
                for s4 in range(4):
                    mmi = t.matmul(psb[0:1, 512 * s4:512 * (s4 + 1)],
                                   ones128b[64 * hb:64 * (hb + 1), :],
                                   ksq[64 * hb:64 * (hb + 1), 512 * s4:512 * (s4 + 1)],
                                   start=True, stop=True)
                mmi.then_inc(C.sem("mm"), 1)

        @blk.vector
        def _(v):
            for hh in range(8):
                v.wait_ge(C.sem("mm"), mm0 + hh + 1)
                v.reduce_max(mrows[0:1, 4 * hh:4 * (hh + 1)],
                             psa[0:1, 0:2048].rearrange(
                                 "p (a b) -> p a b", b=512), axis=AX.X)
                v.reduce_max(mrows[0:1, 32 + hh:33 + hh],
                             psb[0:1, 0:2048].rearrange(
                                 "p (a b) -> p a b", b=2048), axis=AX.X
                             ).then_inc(C.sem("vv"), 1)
    C.c["mm"] = mm0 + 8
    C.c["vv"] = vv0 + 8
    C.c["ev"] = ev0 + 4

    vv0 = C.c["vv"]; ev0 = C.c["ev"]; mm0 = C.c["mm"]
    with nc.Block() as blk:
        @blk.vector
        def _(v):
            v.tensor_tensor(mrows[0:1, 64:96], mrows[0:1, 0:32],
                            bass.AP(mrows, 32, [[160, 1], [1, 8], [0, 4]]),
                            op=ALU.mult).then_inc(C.sem("vv"), 1)

        @blk.scalar
        def _(s):
            s.wait_ge(C.sem("vv"), vv0 + 1)
            s.activation(out=mrows[0:1, 96:128], in_=mrows[0:1, 64:96], func=AF.Ln)
            s.drain()
            s.activation(out=mrows[0:1, 128:160], in_=mrows[0:1, 96:128],
                         func=AF.Exp, scale=0.5).then_inc(C.sem("ev"), 1)
    C.c["vv"] = vv0 + 1
    C.c["ev"] = ev0 + 1

    vv0 = C.c["vv"]; ev0 = C.c["ev"]; mm0 = C.c["mm"]
    with nc.Block() as blk:
        @blk.vector
        def _(v):
            v.wait_ge(C.sem("ev"), ev0)
            v.tensor_scalar_add(mrows[0:1, 128:160], mrows[0:1, 128:160], 1.0
                                ).then_inc(C.sem("vv"), 1)

        @blk.tensor
        def _(t):
            t.wait_ge(C.sem("vv"), vv0 + 1)
            t.matmul(psb[:, 256:288], ones1f[:, :], mrows[0:1, 128:160],
                     start=True, stop=True).then_inc(C.sem("mm"), 1)
    C.c["mm"] = mm0 + 1
    C.c["vv"] = vv0 + 1

    mm0 = C.c["mm"]; vv0 = C.c["vv"]
    with nc.Block() as blk:
        @blk.vector
        def _(v):
            v.wait_ge(C.sem("mm"), mm0)
            v.tensor_copy(mcols32[:, :], psb[:, 256:288])
            v.drain()
            if layer == 0:
                v.tensor_scalar_mul(biast[:, 0:32], mcols32[:, :], -1.0
                                    ).then_inc(C.sem("vv"), 1)
            else:
                slopek = C.t["slopek"]
                vvi = None
                for hh in range(8):
                    h = 8 * half + hh
                    for qb in range(NQB):
                        vvi = v.tensor_scalar(
                            biast[:, 64 * hh + 16 * qb:64 * hh + 16 * (qb + 1)],
                            slopek[:, 16 * h:16 * (h + 1)],
                            mcols32[:, 4 * hh + qb:4 * hh + qb + 1], None,
                            op0=ALU.subtract)
                vvi.then_inc(C.sem("vv"), 1)
    C.c["vv"] = vv0 + 1

    iters = []
    for hh in range(8):
        for qb in range(NQB):
            kcs = _kcs_phase2(direction(hh), qb)
            geoms = [_tile_geom(direction(hh), qb, kc) for kc in kcs]
            iters.append((hh, qb, kcs, geoms))
    idx_last = {}
    for ii, (hh, qb, kcs, geoms) in enumerate(iters):
        idx_last[hh] = ii
    mm0 = C.c["mm"]; vv0 = C.c["vv"]; ev0 = C.c["ev"]; st0 = C.c["st"]
    ld0 = C.c["ld"]
    vtile = []
    viter = []
    stile = []
    siter = []
    vc = 0; sc = 0
    ntl = 0
    lrec_s = []
    iter_last = []
    for (hh, qb, kcs, geoms) in iters:
        for (lo, hi, tri0) in geoms:
            if tri0 is not None:
                vc += 1
            if (lo, hi) != (0, 512):
                vc += 1
            vtile.append(vc)
            sc += 1
            stile.append(sc)
            ntl += 1
        iter_last.append(ntl - 1)
        vc += 1
        viter.append(vc)
        sc += 1
        lrec_s.append(sc)
        sc += 1
        siter.append(sc)
    LOOK0 = 3
    mm_st = [0] * ntl
    mm_pv = [0] * ntl
    mm_bc = [0] * len(iters)
    last_set = set(iter_last)
    last_to_iter = {tgl: ii for ii, tgl in enumerate(iter_last)}
    mc = 0
    for tg in range(min(LOOK0, ntl)):
        mc += 1
        mm_st[tg] = mc
    for tg in range(ntl):
        if tg + LOOK0 < ntl:
            mc += 1
            mm_st[tg + LOOK0] = mc
        mc += 1
        mm_pv[tg] = mc
        if tg in last_set:
            mc += 1
            mm_bc[last_to_iter[tg]] = mc

    with nc.Block() as blk:
        @blk.sync
        def _(sync):
            if layer == 1:
                k = 0
                seen = set()
                for ii, (hh, qb, kcs, geoms) in enumerate(iters):
                    if hh not in seen:
                        seen.add(hh)
                        h = 8 * half + hh
                        if hh >= 2:
                            sync.wait_ge(C.sem("ev"), ev0 + siter[idx_last[hh - 2]])
                        sync.dma_start(out=(qbra if hh % 2 == 0 else qbrb)[0:1, :],
                                       in_=C.t["c_qbias_bf"][h:h + 1, :]
                                       ).then_inc(C.sem("ld"), 16)
                        k += 1
                        sync.wait_ge(C.sem("ld"), ld0 + 16 * k)

        tiles = []
        for ii, (hh, qb, kcs, geoms) in enumerate(iters):
            for i, kc in enumerate(kcs):
                tiles.append((ii, hh, qb, i, kc, len(kcs)))
        LOOK = 3

        @blk.tensor
        def _(t):
            def emit_st(tg):
                ii, hh, qb, i, kc, nk = tiles[tg]
                ch, r0 = hh // 2, 64 * (hh % 2)
                qbr = qbra if hh % 2 == 0 else qbrb
                if layer == 1 and qb == 0 and i == 0:
                    t.wait_ge(C.sem("ld"), ld0 + 16 * (hh + 1))
                par = tg % 3
                if tg >= 3:
                    t.wait_ge(C.sem("ev"), ev0 + stile[tg - 3])
                bank = psa[:, 512 * par:512 * (par + 1)]
                mmi = t.matmul(
                    bank,
                    kh[r0:r0 + 64, 2048 * ch + 128 * kc:2048 * ch + 128 * (kc + 1)],
                    qh[r0:r0 + 64, 2048 * ch + 512 * qb:2048 * ch + 512 * (qb + 1)],
                    start=True, stop=(layer == 0))
                if layer == 1:
                    mmi = t.matmul(bank, ones1b[0:1, :],
                                   qbr[0:1, 512 * qb:512 * (qb + 1)],
                                   start=False, stop=True)
                mmi.then_inc(C.sem("mm"), 1)

            def emit_pv(tg):
                ii, hh, qb, i, kc, nk = tiles[tg]
                par = tg % 3
                t.wait_ge(C.sem("ev"), ev0 + stile[tg])
                if ii >= 2 and i == 0:
                    t.wait_ge(C.sem("vv"), vv0 + viter[ii - 2])
                t.matmul(psb[0:65, 512 * (ii % 2):512 * (ii % 2) + 512],
                         vh[:, 520 * kc + 65 * hh:520 * kc + 65 * (hh + 1)],
                         pt4[:, 512 * par:512 * (par + 1)],
                         start=(i == 0), stop=(i == nk - 1)
                         ).then_inc(C.sem("mm"), 1)
                if i == nk - 1:
                    t.wait_ge(C.sem("ev"), ev0 + lrec_s[ii])
                    t.matmul(psa[0:64, 2048 - 512:2048], ones1f[0:1, 0:64],
                             lrec[0:1, :], start=True, stop=True
                             ).then_inc(C.sem("mm"), 1)

            for tg in range(min(LOOK, len(tiles))):
                emit_st(tg)
            for tg in range(len(tiles)):
                if tg + LOOK < len(tiles):
                    emit_st(tg + LOOK)
                emit_pv(tg)

        @blk.vector
        def _(v):
            tg = 0
            for ii, (hh, qb, kcs, geoms) in enumerate(iters):
                ch, r0 = hh // 2, 64 * (hh % 2)
                tri = C.t["tri_fwd"] if direction(hh) == "fwd" else C.t["tri_bwd"]
                for i, kc in enumerate(kcs):
                    par = tg % 3
                    lo, hi, tri0 = geoms[i]
                    if tri0 is not None:
                        v.wait_ge(C.sem("mm"), mm0 + mm_st[tg])
                        v.tensor_tensor(
                            psa[:, 512 * par + tri0:512 * par + tri0 + 128],
                            psa[:, 512 * par + tri0:512 * par + tri0 + 128],
                            tri[:, :], op=ALU.add).then_inc(C.sem("vv"), 1)
                    if (lo, hi) != (0, 512):
                        dead = (0, lo) if direction(hh) == "fwd" else (hi, 512)
                        if tg >= 3:
                            v.wait_ge(C.sem("mm"), mm0 + mm_pv[tg - 3])
                        v.memset(pt4[:, 512 * par + dead[0]:512 * par + dead[1]],
                                 0.0).then_inc(C.sem("vv"), 1)
                    tg += 1
                v.wait_ge(C.sem("ev"), ev0 + siter[ii])
                if hh % 2 == 0:
                    v.tensor_tensor(
                        oh[0:64, 2048 * ch + 512 * qb:2048 * ch + 512 * (qb + 1)],
                        psb[0:64, 512 * (ii % 2):512 * (ii % 2) + 512],
                        lb[:, :], op=ALU.mult).then_inc(C.sem("vv"), 1)
                else:
                    if ii >= 1:
                        v.wait_ge(C.sem("st"), st0 + 16 * _nodd(iters, ii))
                    v.tensor_tensor(
                        C.t["OTMP"], psb[0:64, 512 * (ii % 2):512 * (ii % 2) + 512],
                        lb[:, :], op=ALU.mult).then_inc(C.sem("vv"), 1)

        @blk.scalar
        def _(s):
            tg = 0
            nodd = 0
            for ii, (hh, qb, kcs, geoms) in enumerate(iters):
                ch = hh // 2
                for i, kc in enumerate(kcs):
                    par = tg % 3
                    lo, hi, tri0 = geoms[i]
                    s.wait_ge(C.sem("mm"), mm0 + mm_st[tg])
                    s.wait_ge(C.sem("vv"), vv0 + vtile[tg])
                    if tg >= 3:
                        s.wait_ge(C.sem("mm"), mm0 + mm_pv[tg - 3])
                    bias = biast[:, 4 * hh + qb:4 * hh + qb + 1] if layer == 0 \
                        else biast[:, 64 * hh + 16 * qb + kc:64 * hh + 16 * qb + kc + 1]
                    s.activation(out=pt4[:, 512 * par + lo:512 * par + hi],
                                 in_=psa[:, 512 * par + lo:512 * par + hi],
                                 func=AF.Exp, bias=bias).then_inc(C.sem("ev"), 1)
                    tg += 1
                s.wait_ge(C.sem("mm"), mm0 + mm_pv[tg - 1])
                s.drain()
                s.activation(out=lrec[0:1, :],
                             in_=psb[64:65, 512 * (ii % 2):512 * (ii % 2) + 512],
                             func=AF.Ln)
                s.drain()
                s.activation(out=lrec[0:1, :], in_=lrec[0:1, :], func=AF.Exp,
                             scale=-1.0).then_inc(C.sem("ev"), 1)
                s.wait_ge(C.sem("mm"), mm0 + mm_bc[ii])
                if ii >= 1:
                    s.wait_ge(C.sem("vv"), vv0 + viter[ii - 1])
                s.copy(lb[:, :], psa[0:64, 2048 - 512:2048]).then_inc(C.sem("ev"), 1)
                if hh % 2 == 1:
                    s.wait_ge(C.sem("vv"), vv0 + viter[ii])
                    s.dma_start(
                        out=oh[64:128, 2048 * ch + 512 * qb:2048 * ch + 512 * (qb + 1)],
                        in_=C.t["OTMP"]).then_inc(C.sem("st"), 16)
                    nodd += 1
    C.c["mm"] = mm0 + mm_bc[-1]
    C.c["vv"] = vv0 + viter[-1]
    C.c["ev"] = ev0 + siter[-1]
    C.c["st"] = st0 + 16 * sum(1 for (hh, qb, k, g) in iters if hh % 2 == 1)
    if layer == 1:
        C.c["ld"] = ld0 + 16 * 8


def _nodd(iters, ii):
    return sum(1 for (hh, qb, k, g) in iters[:ii] if hh % 2 == 1)


def _bfold(C, half, wso, wo_gd_need, bvb, bfc, bfold, bfold2):
    nc = C.nc
    psa = C.t["PSA"]
    mm0 = C.c["mm"]
    with nc.Block() as blk:
        @blk.tensor
        def _(t):
            t.wait_ge(C.sem("gd"), wo_gd_need)
            for mc in range(8):
                for rc in range(4):
                    mmi = t.matmul(psa[:, 1024 + mc:1024 + mc + 1],
                                   wso[:, 1024 * rc + 128 * mc:1024 * rc + 128 * (mc + 1)],
                                   bvb[:, 4 * half + rc:4 * half + rc + 1],
                                   start=(rc == 0), stop=(rc == 3))
            mmi.then_inc(C.sem("mm"), 1)

        @blk.vector
        def _(v):
            v.wait_ge(C.sem("mm"), mm0 + 1)
            if half == 0:
                v.tensor_tensor(bfold[:, :], psa[:, 1024:1032], bfc[:, :], op=ALU.add)
            else:
                v.tensor_copy(bfold2[:, :], psa[:, 1024:1032])
    C.c["mm"] = mm0 + 1


def _wo_proj(C, layer, half, wso, wo_gd_need, oh, scr, bfold, bfold2):
    nc = C.nc
    psa = C.t["PSA"]
    mm0 = C.c["mm"]; vv0 = C.c["vv"]; ev0 = C.c["ev"]
    use_scalar = (layer == 0 and half == 0)
    st_need = C.c["st"]
    with nc.Block() as blk:
        @blk.tensor
        def _(t):
            t.wait_ge(C.sem("gd"), wo_gd_need)
            t.wait_ge(C.sem("st"), st_need)
            idx = 0
            for qq in range(4):
                for mc in range(8):
                    if idx >= 2:
                        if use_scalar:
                            t.wait_ge(C.sem("ev"), ev0 + idx - 1)
                        else:
                            t.wait_ge(C.sem("vv"), vv0 + idx - 1)
                    for rc in range(4):
                        mmi = t.matmul(
                            psa[:, 512 * (idx % 2):512 * (idx % 2 + 1)],
                            wso[:, 1024 * rc + 128 * mc:1024 * rc + 128 * (mc + 1)],
                            oh[:, 2048 * rc + 512 * qq:2048 * rc + 512 * (qq + 1)],
                            start=(rc == 0), stop=(rc == 3))
                    mmi.then_inc(C.sem("mm"), 1)
                    idx += 1

        if use_scalar:
            @blk.scalar
            def _(s):
                idx = 0
                for qq in range(4):
                    for mc in range(8):
                        s.wait_ge(C.sem("mm"), mm0 + idx + 1)
                        s.copy(scr[:, 2048 * mc + 512 * qq:2048 * mc + 512 * (qq + 1)],
                               psa[:, 512 * (idx % 2):512 * (idx % 2 + 1)]
                               ).then_inc(C.sem("ev"), 1)
                        idx += 1
        else:
            @blk.vector
            def _(v):
                idx = 0
                for qq in range(4):
                    for mc in range(8):
                        v.wait_ge(C.sem("mm"), mm0 + idx + 1)
                        dst = scr[:, 2048 * mc + 512 * qq:2048 * mc + 512 * (qq + 1)]
                        src = psa[:, 512 * (idx % 2):512 * (idx % 2 + 1)]
                        if half == 0:
                            v.tensor_scalar(dst, src,
                                            bfold[:, mc:mc + 1], None, op0=ALU.add
                                            ).then_inc(C.sem("vv"), 1)
                        elif layer == 1:
                            v.tensor_tensor(dst, src, dst, op=ALU.add)
                            v.drain()
                            v.tensor_scalar(dst, dst, bfold2[:, mc:mc + 1],
                                            None, op0=ALU.add
                                            ).then_inc(C.sem("vv"), 1)
                        else:
                            v.tensor_tensor(dst, src, dst, op=ALU.add
                                            ).then_inc(C.sem("vv"), 1)
                        idx += 1
    C.c["mm"] = mm0 + 32
    if use_scalar:
        C.c["ev"] = ev0 + 32
    else:
        C.c["vv"] = vv0 + 32



_DRAM_SPECS = [
    ("b0_wq", [D, D], F32), ("b0_wk", [D, D], F32), ("b0_wv", [D, D], F32),
    ("b0_wo", [D, D], F32),
    ("b1_wgT", [D, 8], F32), ("b1_eb", [1, 8], F32),
    ("b1_sw1T", [D, D], F32), ("b1_sw3T", [D, D], F32), ("b1_sw2T", [D, D], F32),
    ("b1_w1", [8 * 1024, 512], F32), ("b1_w3", [8 * 1024, 512], F32),
    ("b1_w2", [8 * 512, 1024], F32),
    ("b2_wq", [D, D], F32), ("b2_wk", [D, D], F32), ("b2_wv", [D, D], F32),
    ("b2_wf", [D, D], F32),
    ("b2_bq", [128, 8], F32), ("b2_bk", [128, 8], F32), ("b2_bv", [128, 8], F32),
    ("b2_bf", [128, 8], F32),
    ("b3_gwT", [D, 8], F32),
    ("b3_wegT", [8 * 1024, 1024], F32), ("b3_weuT", [8 * 1024, 1024], F32),
    ("b3_wedT", [8 * 1024, 1024], F32),
    ("b3_wsgT", [D, 2048], F32), ("b3_wsuT", [D, 2048], F32),
    ("b3_wsdT", [2048, D], F32),
    ("c_id32", [128, 128], F32), ("c_idbf", [128, 128], BF16),
    ("c_tri_fwd", [128, 128], F32), ("c_tri_bwd", [128, 128], F32),
    ("c_cos", [128, 512], F32), ("c_sin", [128, 512], F32),
    ("c_slopek", [128, HEADS * 16], F32),
    ("c_ustrict", [128, 128], F32), ("c_u16", [16, 16], F32),
    ("c_qbias", [HEADS, T], F32),
    ("c_qbias_bf", [HEADS, T], BF16),
    ("c_pos", [128, 32], F32), ("c_maskcol", [8, 1], F32),
    ("c_coreflag", [128, 1], F32),
]

_SB_CONSTS = [
    ("id32", "c_id32", [128, 128], F32),
    ("idbf", "c_idbf", [128, 128], BF16),
    ("tri_fwd", "c_tri_fwd", [128, 128], F32),
    ("tri_bwd", "c_tri_bwd", [128, 128], F32),
    ("cos", "c_cos", [128, 512], F32),
    ("sin", "c_sin", [128, 512], F32),
    ("slopek", "c_slopek", [128, HEADS * 16], F32),
    ("ustrict", "c_ustrict", [128, 128], F32),
    ("u16", "c_u16", [16, 16], F32),
    ("pos", "c_pos", [128, 32], F32),
    ("maskcol", "c_maskcol", [8, 1], F32),
    ("coreflag", "c_coreflag", [128, 1], F32),
    ("ebrow", "b1_eb", [1, 8], F32),
]


def _emit_init(C):
    nc = C.nc
    C.sb("XN", [128, 16384], BF16)
    C.sb("SCR16", [128, 16384], BF16)
    C.sb("XST", [128, 2048], F32)
    C.sb("TMP1", [128, 2048], F32)
    C.sb("STT", [128, 64], F32)
    C.sb("MEAN", [128, 16], F32)
    C.sb("VAR", [128, 16], F32)
    C.sb("RSTD", [128, 16], F32)
    C.sb("BCOL", [128, 16], F32)
    C.sb("ones1f", [1, 128], F32)
    C.sb("ones128f", [128, 1], F32)
    C.sb("ones128b", [128, 1], BF16)
    C.sb("ones1b", [1, 128], BF16)
    for nm, dn, shape, dt in _SB_CONSTS:
        C.sb(nm, shape, dt)
    C.ps("PSA", [128, 2048], F32)
    C.ps("PSB", [128, 2048], F32)
    ld0 = C.c.get("ld", 0)
    n = len(_SB_CONSTS)
    with nc.Block() as blk:
        @blk.sync
        def _(sync):
            for nm, dn, shape, dt in _SB_CONSTS:
                sync.dma_start(out=C.t[nm][:, :], in_=C.t[dn][:, :]
                               ).then_inc(C.sem("ld"), 16)
            sync.wait_ge(C.sem("ld"), ld0 + 16 * n)

        @blk.gpsimd
        def _(g):
            g.memset(C.t["ones1f"][:, :], 1.0)
            g.memset(C.t["ones128f"][:, :], 1.0)
            g.memset(C.t["ones128b"][:, :], 1.0)
            g.memset(C.t["ones1b"][:, :], 1.0)
    C.c["ld"] = ld0 + 16 * n


def _emit_final_wait(C):
    nc = C.nc
    with nc.Block() as blk:
        @blk.sync
        def _(sync):
            sync.wait_ge(C.sem("st"), C.c["st"])
            sync.wait_ge(C.sem("ld"), C.c["ld"])
            sync.wait_ge(C.sem("gd"), C.c["gd"])
            sync.wait_ge(C.sem("sta"), C.c["sta"])
            sync.wait_ge(C.sem("stb"), C.c["stb"])
            sync.wait_ge(C.sem("lda"), C.c["lda"])
            sync.wait_ge(C.sem("ldb"), C.c["ldb"])
            if "swl" in C.c:
                sync.wait_ge(C.sem("swl"), C.c["swl"])


def _build_program(blocks):
    nc = bass.Bass("TRN2", target_bir_lowering=False)
    dram = {}
    for nm, shape, dt in _DRAM_SPECS:
        dram[nm] = nc.dram_tensor(nm, shape, dt, kind="ExternalInput")
    x_in = nc.dram_tensor("x_in", [D, T], F32, kind="ExternalInput")
    xi = [x_in]
    for i in range(len(blocks) - 1):
        xi.append(nc.dram_tensor(f"xmid{i}", [D, T], F32))
    last_tm = (blocks[-1] == "f1")
    if last_tm:
        x_out = nc.dram_tensor("x_out", [T, D], F32, kind="ExternalOutput")
    else:
        x_out = nc.dram_tensor("x_out", [D, T], F32, kind="ExternalOutput")
    xi.append(x_out)

    with ExitStack() as stack:
        C = Ctx(nc, stack)
        for nm in dram:
            C.t[nm] = dram[nm]
        for s in ("ld", "gd", "st", "mm", "ev", "vv", "gg", "cc", "lda", "ldb", "sta", "stb"):
            C.sem(s)
        _emit_init(C)
        for i, b in enumerate(blocks):
            with ExitStack() as bs:
                if b == "a0":
                    emit_attn(C, 0, xi[i], xi[i + 1], bs)
                elif b == "a1":
                    emit_attn(C, 1, xi[i], xi[i + 1], bs)
                elif b == "f0":
                    emit_ffn0(C, xi[i], xi[i + 1], bs)
                elif b == "f1":
                    emit_ffn1(C, xi[i], xi[i + 1], bs)
        if _DEBUG_DUMP:
            dbg = {}
            for nm in _DEBUG_DUMP:
                t_ = C.t[nm]
                dbg[nm] = nc.dram_tensor("dbg_" + nm, list(t_.shape),
                                         t_.dtype, kind="ExternalOutput")
            st0 = C.c["st"]
            with nc.Block() as blk:
                @blk.sync
                def _(sync):
                    for nm in _DEBUG_DUMP:
                        sync.dma_start(out=dbg[nm].ap(), in_=C.t[nm][:, :]
                                       ).then_inc(C.sem("st"), 16)
                    sync.wait_ge(C.sem("st"), st0 + 16 * len(_DEBUG_DUMP))
            C.c["st"] = st0 + 16 * len(_DEBUG_DUMP)
        _emit_final_wait(C)
    return nc


_BLOCK_ORDER = ["a0", "f0", "a1", "f1"]


def kernel(**inputs):
    maps = _host_inputs(inputs)
    if _MODE == "fused":
        nc = _build_program(_BLOCK_ORDER)
        res = run_bass_kernel_spmd(nc, maps, core_ids=list(range(NCORES)))
        return np.stack([np.asarray(res.results[c]["x_out"], np.float32)
                         for c in range(NCORES)], axis=0)
    cur = [m["x_in"] for m in maps]
    for b in _BLOCK_ORDER:
        nc = _build_program([b])
        in_maps = []
        for c in range(NCORES):
            m = dict(maps[c])
            m["x_in"] = np.ascontiguousarray(cur[c])
            in_maps.append(m)
        res = run_bass_kernel_spmd(nc, in_maps, core_ids=list(range(NCORES)))
        cur = [np.asarray(res.results[c]["x_out"], np.float32)
               for c in range(NCORES)]
    return np.stack(cur, axis=0)




def _emit_gate_softmax(C, bs, wgT_name, with_groups):
    nc = C.nc
    xn = C.t["XN"]
    psa, psb = C.t["PSA"], C.t["PSB"]
    id32 = C.t["id32"]
    ones1f = C.t["ones1f"]
    wgt = C.sb("WGT", [128, 64], BF16, bs)
    lg8 = C.t["TMP1"]
    s0 = C.sb("S0T", [128, 128], F32, bs)
    sb_ = C.sb("SBT", [128, 128], F32, bs)
    e1 = C.sb("E1T", [128, 128], F32, bs)
    e2 = C.sb("E2T", [128, 128], F32, bs)
    sm = C.sb("SMT", [128, 128], F32, bs)
    gsc = C.sb("GSC", [128, 64], F32, bs)
    sml = C.sb("SML", [128, 96], F32, bs)
    ebb = C.sb("EBB", [128, 8], F32, bs)

    def smlb(col, inner):
        return bass.AP(sml, col, [[96, 128], [1, 16], [0, inner]])

    def t3(ap_):
        return ap_.rearrange("p (t e) -> p t e", e=8)

    gd0 = C.c["gd"]
    with nc.Block() as blk:
        @blk.gpsimd
        def _(g):
            g.dma_start(out=wgt[:, :],
                        in_=_wview(C.t[wgT_name], 8, 8)
                        ).then_inc(C.sem("gd"), 16)
    C.c["gd"] = gd0 + 16

    mm0 = C.c["mm"]; ev0 = C.c["ev"]
    with nc.Block() as blk:
        @blk.tensor
        def _(t):
            t.wait_ge(C.sem("gd"), C.c["gd"])
            for dc in range(NDC):
                for s4 in range(4):
                    mmi = t.matmul(psa[0:8, 512 * s4:512 * (s4 + 1)],
                                   wgt[:, 8 * dc:8 * (dc + 1)],
                                   xn[:, 2048 * dc + 512 * s4:2048 * dc + 512 * (s4 + 1)],
                                   start=(dc == 0), stop=(dc == NDC - 1))
            mmi.then_inc(C.sem("mm"), 1)
            t.matmul(psb[:, 128:136], ones1f[:, :], C.t["ebrow"][0:1, :],
                     start=True, stop=True).then_inc(C.sem("mm"), 1)

        @blk.scalar
        def _(s):
            s.wait_ge(C.sem("mm"), mm0 + 1)
            s.copy(lg8[0:8, 0:2048], psa[0:8, 0:2048]).then_inc(C.sem("ev"), 1)
    C.c["mm"] = mm0 + 2
    C.c["ev"] = ev0 + 1

    mm0 = C.c["mm"]; vv0 = C.c["vv"]
    with nc.Block() as blk:
        @blk.tensor
        def _(t):
            tri_ = None
            for tl in range(NTC):
                tri_ = t.transpose(psb[:, 8 * tl:8 * (tl + 1)],
                                   lg8[0:8, 128 * tl:128 * (tl + 1)], id32[0:8, 0:8])
            tri_.then_inc(C.sem("mm"), 1)

        @blk.vector
        def _(v):
            v.wait_ge(C.sem("mm"), mm0 + 1)
            v.tensor_copy(s0[:, :], psb[:, 0:128])
            v.tensor_copy(ebb[:, :], psb[:, 128:136])
            v.drain()
            v.reduce_max(sml[:, 0:16], t3(s0[:, :]), axis=AX.X)
            v.drain()
            v.tensor_tensor(sb_[:, :], s0[:, :], smlb(0, 8), op=ALU.subtract
                            ).then_inc(C.sem("vv"), 1)
    C.c["mm"] = mm0 + 1
    C.c["vv"] = vv0 + 1

    vv0 = C.c["vv"]; ev0 = C.c["ev"]
    with nc.Block() as blk:
        @blk.scalar
        def _(s):
            s.activation(out=s0[:, :], in_=sb_[:, :], func=AF.Exp
                         ).then_inc(C.sem("ev"), 1)
    C.c["ev"] = ev0 + 1

    vv0 = C.c["vv"]
    with nc.Block() as blk:
        @blk.vector
        def _(v):
            v.reduce_sum(sml[:, 16:32], t3(s0[:, :]), axis=AX.X)
            v.drain()
            v.reciprocal(sml[:, 16:32], sml[:, 16:32])
            v.drain()
            v.tensor_tensor(s0[:, :], s0[:, :], smlb(16, 8), op=ALU.mult)
            v.drain()
            if with_groups:
                v.tensor_tensor(sb_[:, :], s0[:, :],
                                bass.AP(ebb, 0, [[8, 128], [0, 16], [1, 8]]),
                                op=ALU.add)
                v.drain()
                v.tensor_tensor(gsc[:, :],
                                bass.AP(sb_, 0, [[128, 128], [8, 16], [2, 4]]),
                                bass.AP(sb_, 1, [[128, 128], [8, 16], [2, 4]]),
                                op=ALU.add)
                v.drain()
                g3 = gsc.ap().rearrange("p (t g) -> p t g", g=4)
                v.reduce_max(sml[:, 32:48], g3, axis=AX.X)
                v.drain()
                v.tensor_tensor(sm[:, 0:64], gsc[:, :], smlb(32, 4), op=ALU.is_equal)
                v.drain()
                v.scalar_tensor_tensor(gsc[:, :], sm[:, 0:64], -1e9, gsc[:, :],
                                       op0=ALU.mult, op1=ALU.add)
                v.drain()
                v.reduce_max(sml[:, 48:64], g3, axis=AX.X)
                v.drain()
                v.tensor_tensor(sm[:, 64:128], gsc[:, :], smlb(48, 4),
                                op=ALU.is_equal)
                v.drain()
                v.tensor_tensor(sm[:, 0:64], sm[:, 0:64], sm[:, 64:128], op=ALU.add)
                v.drain()
                v.tensor_scalar(sm[:, 0:64], sm[:, 0:64], 1e9, -1e9,
                                op0=ALU.mult, op1=ALU.add)
                v.drain()
                v.tensor_tensor(sb_[:, :], sb_[:, :],
                                bass.AP(sm, 0, [[128, 128], [4, 16], [1, 4], [0, 2]]),
                                op=ALU.add)
                v.drain()
            else:
                v.tensor_copy(sb_[:, :], s0[:, :])
                v.drain()
            v.reduce_max(sml[:, 32:48], t3(sb_[:, :]), axis=AX.X)
            v.drain()
            v.tensor_tensor(e1[:, :], sb_[:, :], smlb(32, 8), op=ALU.is_equal)
            v.drain()
            v.tensor_tensor(sm[:, 0:128], e1[:, :], s0[:, :], op=ALU.mult)
            v.drain()
            v.reduce_sum(sml[:, 64:80], t3(sm[:, 0:128]), axis=AX.X)
            v.scalar_tensor_tensor(sb_[:, :], e1[:, :], -1e9, sb_[:, :],
                                   op0=ALU.mult, op1=ALU.add)
            v.drain()
            v.reduce_max(sml[:, 48:64], t3(sb_[:, :]), axis=AX.X)
            v.drain()
            v.tensor_tensor(e2[:, :], sb_[:, :], smlb(48, 8), op=ALU.is_equal)
            v.drain()
            v.tensor_tensor(sm[:, 0:128], e2[:, :], s0[:, :], op=ALU.mult)
            v.drain()
            v.reduce_sum(sml[:, 80:96], t3(sm[:, 0:128]), axis=AX.X
                         ).then_inc(C.sem("vv"), 1)
    C.c["vv"] = vv0 + 1
    return dict(e1=e1, e2=e2, w1=sml[:, 64:80], w2=sml[:, 80:96], s0=s0, sml=sml,
                sm=sm)


def emit_ffn0(C, x_in, x_out, bs):
    nc = C.nc
    xn = C.t["XN"]
    scr = C.t["SCR16"]
    psa, psb = C.t["PSA"], C.t["PSB"]
    id32, idbf = C.t["id32"], C.t["idbf"]
    ones1f, ones128f = C.t["ones1f"], C.t["ones128f"]
    tmp1 = C.t["TMP1"]
    pos = C.t["pos"]
    coreflag = C.t["coreflag"]

    agx_in = nc.dram_tensor("f0_agx_in", [128, 64], BF16)
    agx_out = nc.dram_tensor("f0_agx_out", [1024, 64], BF16)
    arc_in = nc.dram_tensor("f0_arc_in", [1, 8], F32)
    arc_out = nc.dram_tensor("f0_arc_out", [1, 8], F32)
    arp_in = nc.dram_tensor("f0_arp_in", [128, 64], F32)
    arp_out = nc.dram_tensor("f0_arp_out", [128, 64], F32)

    fxst2 = scr[:, 0:4096].bitcast(F32)
    fsq2 = scr[:, 4096:8192].bitcast(F32)
    emit_ln_stats_from_dram(C, x_in, xst2=fxst2, sq2=fsq2)
    emit_ln_tail(C, T, 0, "pre", x_dram=x_in, out_bf=xn, xst2=fxst2, sq2=fsq2,
                 acp=scr[:, 8192:12288].bitcast(F32),
                 bcp=scr[:, 12288:16384].bitcast(F32))

    pts = C.sb("PTS", [128, 64], F32, bs)
    ws1 = C.sb("WSH1", [128, 8192], BF16, bs)
    ws2 = C.sb("WSH2", [128, 8192], BF16, bs)
    ws3 = C.sb("WSH3", [128, 8192], BF16, bs)

    def load_sq(g, dram_name, slot):
        g.dma_start(out=slot[:, :],
                    in_=_wview(C.t[dram_name], 8, 1024)
                    ).then_inc(C.sem("swl"), 16)
        C.c["swl"] = C.c.get("swl", 0) + 16

    with nc.Block() as blk:
        @blk.gpsimd
        def _(g):
            load_sq(g, "b1_sw1T", ws1)
            load_sq(g, "b1_sw3T", ws2)
            load_sq(g, "b1_sw2T", ws3)
    sw_need = C.c["swl"]

    gd0 = C.c["gd"]; cc0 = C.c["cc"]
    with nc.Block() as blk:
        @blk.gpsimd
        def _(g):
            g.dma_start(out=agx_in[:, :],
                        in_=xn.ap().rearrange("p (c t) -> p c t", t=2048)[:, :, 0:8]
                        ).then_inc(C.sem("gd"), 16)
            g.wait_ge(C.sem("gd"), gd0 + 16)
            g.collective_compute(
                "AllGather", ALU.bypass, replica_groups=[list(range(NCORES))],
                ins=[agx_in.ap().opt()], outs=[agx_out.ap().opt()],
            ).then_inc(C.sem("cc"), 1)
    C.c["gd"] = gd0 + 16
    C.c["cc"] = cc0 + 1

    with ExitStack() as bs2:
        R = _emit_gate_softmax(C, bs2, "b1_wgT", with_groups=True)
        e1, e2, w1c, w2c = R["e1"], R["e2"], R["w1"], R["w2"]
        mbuf = C.sb("MBUF", [128, 2048], BF16, bs2)
        mwt = C.sb("MWT", [64, 4096], BF16, bs2)
        afl = C.sb("AFL", [64, 512], BF16, bs2)
        bfl = C.sb("BFL", [64, 512], BF16, bs2)
        ht = C.sb("HT", [128, 256], BF16, bs2)
        xfh = C.sb("XFH", [128, 64], BF16, bs2)
        ofs = C.sb("OFS", [128, 16], F32, bs2)
        cl8 = C.sb("CL8", [8, 1], F32, bs2)
        crow = C.sb("CROW", [1, 16], F32, bs2)
        hsb = C.sb("HSB", [64, 512], BF16, bs2)
        asb = C.sb("ASB", [8, 1024], BF16, bs2)
        mwsc = C.sb("MWSC", [128, 64], BF16, bs2)
        wss1 = C.sb("WSS1", [128, 4096], BF16, bs2)
        wss2 = C.sb("WSS2", [128, 4096], BF16, bs2)
        wss3 = C.sb("WSS3", [128, 4096], BF16, bs2)
        wss4 = C.sb("WSS4", [128, 4096], BF16, bs2)

        mm0 = C.c["mm"]; ev0 = C.c["ev"]
        with nc.Block() as blk:
            @blk.tensor
            def _(t):
                for tl in range(NTC):
                    t.matmul(psa[0:8, 0:1], e1[:, 8 * tl:8 * (tl + 1)],
                             ones128f[:, :], start=(tl == 0), stop=False)
                    mmi = t.matmul(psa[0:8, 0:1], e2[:, 8 * tl:8 * (tl + 1)],
                                   ones128f[:, :], start=False,
                                   stop=(tl == NTC - 1))
                mmi.then_inc(C.sem("mm"), 1)

            @blk.scalar
            def _(s):
                s.wait_ge(C.sem("mm"), mm0 + 1)
                s.copy(cl8[:, :], psa[0:8, 0:1]).then_inc(C.sem("ev"), 1)
        C.c["mm"] = mm0 + 1
        C.c["ev"] = ev0 + 1

        gd0 = C.c["gd"]; cc0 = C.c["cc"]
        with nc.Block() as blk:
            @blk.gpsimd
            def _(g):
                g.dma_start(out=bass.AP(arc_in, 0, [[1, 8], [1, 1]]),
                            in_=cl8[:, :]).then_inc(C.sem("gd"), 16)
                g.wait_ge(C.sem("gd"), gd0 + 16)
                g.collective_compute(
                    "AllReduce", ALU.add, replica_groups=[list(range(NCORES))],
                    ins=[arc_in.ap().opt()], outs=[arc_out.ap().opt()],
                ).then_inc(C.sem("cc"), 1)
                g.wait_ge(C.sem("cc"), cc0 + 1)
                g.dma_start(out=crow[:, 0:8], in_=arc_out[:, :]).then_inc(C.sem("gd"), 16)
                g.dma_start(out=xfh[:, :], in_=agx_out[0:128, :]).then_inc(C.sem("gd"), 16)
                g.wait_ge(C.sem("gd"), gd0 + 48)
        C.c["gd"] = gd0 + 48
        C.c["cc"] = cc0 + 1

        vv0 = C.c["vv"]; mm0 = C.c["mm"]
        with nc.Block() as blk:
            @blk.vector
            def _(v):
                v.tensor_tensor_scan(crow[0:1, 8:16],
                                     bass.AP(ones1f, 0, [[128, 1], [1, 8]]),
                                     crow[0:1, 0:8], 0.0,
                                     op0=ALU.mult, op1=ALU.add)
                v.drain()
                v.tensor_tensor(crow[0:1, 0:8], crow[0:1, 8:16], crow[0:1, 0:8],
                                op=ALU.subtract).then_inc(C.sem("vv"), 1)

            @blk.tensor
            def _(t):
                t.wait_ge(C.sem("vv"), vv0 + 1)
                t.matmul(psb[:, 256:264], ones1f[:, :], crow[0:1, 0:8],
                         start=True, stop=True)
                t.matmul(psb[:, 264:272], ones1f[:, :], crow[0:1, 8:16],
                         start=True, stop=True).then_inc(C.sem("mm"), 1)
        C.c["vv"] = vv0 + 1
        C.c["mm"] = mm0 + 1
        vv0 = C.c["vv"]; mm0 = C.c["mm"]
        with nc.Block() as blk:
            @blk.vector
            def _(v):
                v.tensor_copy(ofs[:, :], psb[:, 256:272]).then_inc(C.sem("vv"), 1)
        C.c["vv"] = vv0 + 1

        gd0 = C.c["gd"]
        mm0 = C.c["mm"]; ev0 = C.c["ev"]; st0 = C.c["st"]
        with nc.Block() as blk:
            @blk.gpsimd
            def _(g):
                for sseg in range(8):
                    s1 = wss1 if sseg % 2 == 0 else wss3
                    s2 = wss2 if sseg % 2 == 0 else wss4
                    if sseg >= 2:
                        g.wait_ge(C.sem("mm"), mm0 + sseg - 1)
                    g.dma_start(out=s1[:, :],
                                in_=_wview(C.t["b1_w1"], 8, 512, row0=1024 * sseg)
                                ).then_inc(C.sem("gd"), 16)
                    g.dma_start(out=s2[:, :],
                                in_=_wview(C.t["b1_w3"], 8, 512, row0=1024 * sseg)
                                ).then_inc(C.sem("gd"), 16)
                    g.wait_ge(C.sem("gd"), gd0 + 32 * (sseg + 1))

            @blk.tensor
            def _(t):
                for sseg in range(8):
                    s1 = wss1 if sseg % 2 == 0 else wss3
                    s2 = wss2 if sseg % 2 == 0 else wss4
                    t.wait_ge(C.sem("gd"), gd0 + 32 * (sseg + 1))
                    for dc in range(NDC):
                        t.matmul(psa[0:8, 0:512], xfh[:, 8 * dc:8 * (dc + 1)],
                                 s1[:, 512 * dc:512 * (dc + 1)],
                                 start=(dc == 0), stop=(dc == NDC - 1))
                    for dc in range(NDC):
                        mmi = t.matmul(psa[0:8, 512:1024], xfh[:, 8 * dc:8 * (dc + 1)],
                                       s2[:, 512 * dc:512 * (dc + 1)],
                                       start=(dc == 0), stop=(dc == NDC - 1))
                    mmi.then_inc(C.sem("mm"), 1)
                    t.wait_ge(C.sem("ev"), ev0 + sseg + 1)

            @blk.scalar
            def _(s):
                for sseg in range(8):
                    s.wait_ge(C.sem("mm"), mm0 + sseg + 1)
                    s.wait_ge(C.sem("st"), st0 + 32 * sseg)
                    s.copy(asb[:, 0:512], psa[0:8, 0:512])
                    s.copy(asb[:, 512:1024], psa[0:8, 512:1024]).then_inc(C.sem("ev"), 1)
                    s.drain()
                    s.dma_start(out=afl[8 * sseg:8 * (sseg + 1), :],
                                in_=asb[:, 0:512]).then_inc(C.sem("st"), 16)
                    s.dma_start(out=bfl[8 * sseg:8 * (sseg + 1), :],
                                in_=asb[:, 512:1024]).then_inc(C.sem("st"), 16)
        C.c["gd"] = gd0 + 32 * 8
        C.c["mm"] = mm0 + 8
        C.c["ev"] = ev0 + 8
        C.c["st"] = st0 + 32 * 8

        vv0 = C.c["vv"]; mm0 = C.c["mm"]; ev0 = C.c["ev"]; st0 = C.c["st"]
        with nc.Block() as blk:
            @blk.vector
            def _(v):
                for sc in range(32):
                    tl, k = sc // 2, sc % 2
                    ek = e1 if k == 0 else e2
                    wc = w1c if k == 0 else w2c
                    pc = pos[:, 2 * tl + k:2 * tl + k + 1]
                    v.tensor_scalar(R["sm"][:, 0:8], ofs[:, 0:8], pc, None,
                                    op0=ALU.is_le)
                    v.tensor_scalar(R["sm"][:, 8:16], ofs[:, 8:16], pc, None,
                                    op0=ALU.is_gt)
                    v.drain()
                    v.tensor_tensor(R["sm"][:, 0:8], R["sm"][:, 0:8],
                                    R["sm"][:, 8:16], op=ALU.mult)
                    v.drain()
                    v.tensor_tensor(mbuf[:, 64 * sc:64 * (sc + 1)],
                                    bass.AP(ek.tensor if hasattr(ek, "tensor") else ek,
                                            8 * tl, [[128, 128], [0, 8], [1, 8]]),
                                    bass.AP(R["sm"], 0, [[128, 128], [1, 8], [0, 8]]),
                                    op=ALU.mult)
                    v.drain()
                    if sc >= 1:
                        v.wait_ge(C.sem("mm"), mm0 + sc)
                    v.tensor_scalar(mwsc[:, :], mbuf[:, 64 * sc:64 * (sc + 1)],
                                    wc[:, tl:tl + 1], None, op0=ALU.mult
                                    ).then_inc(C.sem("vv"), 1)

            @blk.tensor
            def _(t):
                for sc in range(32):
                    t.wait_ge(C.sem("vv"), vv0 + sc + 1)
                    tri_ = t.transpose(psb[:, :].bitcast(BF16)[0:64, 3584:3712],
                                       mwsc[:, :], idbf[:, :])
                    tri_.then_inc(C.sem("mm"), 1)
                    t.wait_ge(C.sem("ev"), ev0 + sc + 1)

            @blk.scalar
            def _(s):
                for sc in range(32):
                    s.wait_ge(C.sem("mm"), mm0 + sc + 1)
                    s.copy(mwt[0:64, 128 * sc:128 * (sc + 1)],
                           psb[:, :].bitcast(BF16)[0:64, 3584:3712]
                           ).then_inc(C.sem("ev"), 1)
        C.c["vv"] = vv0 + 32
        C.c["mm"] = mm0 + 32
        C.c["ev"] = ev0 + 32

        mm0 = C.c["mm"]; ev0 = C.c["ev"]; vv0 = C.c["vv"]
        st_need = C.c["st"]
        with nc.Block() as blk:
            @blk.tensor
            def _(t):
                t.wait_ge(C.sem("st"), st_need)
                for sc in range(32):
                    par = sc % 2
                    if sc >= 2:
                        t.wait_ge(C.sem("vv"), vv0 + sc - 1)
                    t.matmul(psa[:, 1024 * par:1024 * par + 512],
                             mwt[:, 128 * sc:128 * (sc + 1)], afl[:, :],
                             start=True, stop=True).then_inc(C.sem("mm"), 1)
                    t.matmul(psa[:, 1024 * par + 512:1024 * par + 1024],
                             mwt[:, 128 * sc:128 * (sc + 1)], bfl[:, :],
                             start=True, stop=True).then_inc(C.sem("mm"), 1)
                    t.wait_ge(C.sem("vv"), vv0 + sc + 1)
                    t.matmul(psb[0:64, 0:512], mbuf[:, 64 * sc:64 * (sc + 1)],
                             scr[:, 512 * par:512 * (par + 1)],
                             start=(sc == 0), stop=(sc == 31)).then_inc(C.sem("mm"), 1)

            @blk.scalar
            def _(s):
                for sc in range(32):
                    par = sc % 2
                    s.wait_ge(C.sem("mm"), mm0 + 3 * sc + 1)
                    if sc >= 1:
                        s.wait_ge(C.sem("vv"), vv0 + sc)
                    s.activation(out=tmp1[:, 0:512], in_=psa[:, 1024 * par:1024 * par + 512],
                                 func=AF.Silu).then_inc(C.sem("ev"), 1)

            @blk.vector
            def _(v):
                for sc in range(32):
                    par = sc % 2
                    v.wait_ge(C.sem("ev"), ev0 + sc + 1)
                    v.wait_ge(C.sem("mm"), mm0 + 3 * sc + 2)
                    v.tensor_tensor(scr[:, 512 * par:512 * (par + 1)],
                                    psa[:, 1024 * par + 512:1024 * par + 1024],
                                    tmp1[:, 0:512], op=ALU.mult).then_inc(C.sem("vv"), 1)
        C.c["mm"] = mm0 + 3 * 32
        C.c["ev"] = ev0 + 32
        C.c["vv"] = vv0 + 32

        mm0 = C.c["mm"]; ev0 = C.c["ev"]; vv0 = C.c["vv"]
        with nc.Block() as blk:
            @blk.scalar
            def _(s):
                s.copy(hsb[:, :], psb[0:64, 0:512]).then_inc(C.sem("ev"), 1)

            @blk.tensor
            def _(t):
                t.wait_ge(C.sem("ev"), ev0 + 1)
                tri_ = None
                for hc in range(4):
                    tri_ = t.transpose(
                        psb[:, :].bitcast(BF16)[:, 1024 + 64 * hc:1024 + 64 * (hc + 1)],
                        hsb[:, 128 * hc:128 * (hc + 1)], idbf[0:64, 0:64])
                tri_.then_inc(C.sem("mm"), 1)

            @blk.vector
            def _(v):
                v.wait_ge(C.sem("mm"), mm0 + 1)
                v.tensor_copy(ht[:, :], psb[:, :].bitcast(BF16)[:, 1024:1280]
                              ).then_inc(C.sem("vv"), 1)
        C.c["ev"] = ev0 + 1
        C.c["mm"] = mm0 + 1
        C.c["vv"] = vv0 + 1

        gd0 = C.c["gd"]; mm0 = C.c["mm"]; ev0 = C.c["ev"]
        with nc.Block() as blk:
            @blk.gpsimd
            def _(g):
                for sseg in range(8):
                    if sseg > 0:
                        g.wait_ge(C.sem("mm"), mm0 + sseg)
                    g.dma_start(out=wss1[:, :],
                                in_=_wview(C.t["b1_w2"], 4, 1024, row0=512 * sseg)
                                ).then_inc(C.sem("gd"), 16)

            @blk.tensor
            def _(t):
                for sseg in range(8):
                    t.wait_ge(C.sem("gd"), gd0 + 16 * (sseg + 1))
                    for dm in range(8):
                        for hc in range(4):
                            mmi = t.matmul(
                                psb[:, 512 + 8 * dm:512 + 8 * (dm + 1)],
                                wss1[:, 1024 * hc + 128 * dm:1024 * hc + 128 * (dm + 1)],
                                ht[:, 64 * hc + 8 * sseg:64 * hc + 8 * (sseg + 1)],
                                start=(sseg == 0 and dm == 0 and hc == 0),
                                stop=(sseg == 7 and dm == 7 and hc == 3))
                    mmi.then_inc(C.sem("mm"), 1)

            @blk.scalar
            def _(s):
                s.wait_ge(C.sem("mm"), mm0 + 8)
                s.copy(pts[:, :], psb[:, 512:576]).then_inc(C.sem("ev"), 1)
        C.c["gd"] = gd0 + 16 * 8
        C.c["mm"] = mm0 + 8
        C.c["ev"] = ev0 + 1

        gd0 = C.c["gd"]; cc0 = C.c["cc"]
        with nc.Block() as blk:
            @blk.gpsimd
            def _(g):
                g.dma_start(out=arp_in[:, :], in_=pts[:, :]).then_inc(C.sem("gd"), 16)
                g.wait_ge(C.sem("gd"), gd0 + 16)
                g.collective_compute(
                    "AllReduce", ALU.add, replica_groups=[list(range(NCORES))],
                    ins=[arp_in.ap().opt()], outs=[arp_out.ap().opt()],
                ).then_inc(C.sem("cc"), 1)
                g.wait_ge(C.sem("cc"), cc0 + 1)
                g.dma_start(out=pts[:, :], in_=arp_out[:, :]).then_inc(C.sem("gd"), 16)
                g.wait_ge(C.sem("gd"), gd0 + 32)
        C.c["gd"] = gd0 + 32
        C.c["cc"] = cc0 + 1

    pts = C.t["PTS"]
    yh = C.sb("YH", [128, 8192], F32, bs)

    mm0 = C.c["mm"]; ev0 = C.c["ev"]; vv0 = C.c["vv"]
    with nc.Block() as blk:
        @blk.tensor
        def _(t):
            t.wait_ge(C.sem("swl"), sw_need)
            idx = 0
            for mc in range(8):
                for s4 in range(4):
                    if idx >= 2:
                        t.wait_ge(C.sem("vv"), vv0 + idx - 1)
                    for dc in range(NDC):
                        t.matmul(psa[:, 1024 * (idx % 2):1024 * (idx % 2) + 512],
                                 ws1[:, 1024 * dc + 128 * mc:1024 * dc + 128 * (mc + 1)],
                                 xn[:, 2048 * dc + 512 * s4:2048 * dc + 512 * (s4 + 1)],
                                 start=(dc == 0), stop=(dc == NDC - 1))
                    for dc in range(NDC):
                        mmi = t.matmul(
                            psa[:, 1024 * (idx % 2) + 512:1024 * (idx % 2) + 1024],
                            ws2[:, 1024 * dc + 128 * mc:1024 * dc + 128 * (mc + 1)],
                            xn[:, 2048 * dc + 512 * s4:2048 * dc + 512 * (s4 + 1)],
                            start=(dc == 0), stop=(dc == NDC - 1))
                    mmi.then_inc(C.sem("mm"), 1)
                    idx += 1

        @blk.scalar
        def _(s):
            idx = 0
            for mc in range(8):
                for s4 in range(4):
                    s.wait_ge(C.sem("mm"), mm0 + idx + 1)
                    if idx >= 1:
                        s.wait_ge(C.sem("vv"), vv0 + idx)
                    s.activation(out=tmp1[:, 0:512],
                                 in_=psa[:, 1024 * (idx % 2):1024 * (idx % 2) + 512],
                                 func=AF.Silu).then_inc(C.sem("ev"), 1)
                    idx += 1

        @blk.vector
        def _(v):
            idx = 0
            for mc in range(8):
                for s4 in range(4):
                    v.wait_ge(C.sem("ev"), ev0 + idx + 1)
                    v.tensor_tensor(scr[:, 2048 * mc + 512 * s4:2048 * mc + 512 * (s4 + 1)],
                                    psa[:, 1024 * (idx % 2) + 512:1024 * (idx % 2) + 1024],
                                    tmp1[:, 0:512], op=ALU.mult).then_inc(C.sem("vv"), 1)
                    idx += 1
    C.c["mm"] = mm0 + 32
    C.c["ev"] = ev0 + 32
    C.c["vv"] = vv0 + 32

    for th in range(2):
        mm0 = C.c["mm"]; vv0 = C.c["vv"]
        with nc.Block() as blk:
            @blk.tensor
            def _(t):
                idx = 0
                for mc in range(8):
                    for s2 in range(2):
                        if idx >= 2:
                            t.wait_ge(C.sem("vv"), vv0 + idx - 1)
                        for dc in range(NDC):
                            mmi = t.matmul(
                                psa[:, 512 * (idx % 2):512 * (idx % 2) + 512],
                                ws3[:, 1024 * dc + 128 * mc:1024 * dc + 128 * (mc + 1)],
                                scr[:, 2048 * dc + 1024 * th + 512 * s2:
                                    2048 * dc + 1024 * th + 512 * (s2 + 1)],
                                start=(dc == 0), stop=(dc == NDC - 1))
                        mmi.then_inc(C.sem("mm"), 1)
                        idx += 1

            @blk.vector
            def _(v):
                idx = 0
                for mc in range(8):
                    for s2 in range(2):
                        v.wait_ge(C.sem("mm"), mm0 + idx + 1)
                        v.tensor_copy(yh[:, 1024 * mc + 512 * s2:1024 * mc + 512 * (s2 + 1)],
                                      psa[:, 512 * (idx % 2):512 * (idx % 2) + 512]
                                      ).then_inc(C.sem("vv"), 1)
                        idx += 1
        C.c["mm"] = mm0 + 16
        C.c["vv"] = vv0 + 16

        if th == 0:
            vv0 = C.c["vv"]
            with nc.Block() as blk:
                @blk.vector
                def _(v):
                    for mc in range(8):
                        v.drain()
                        v.scalar_tensor_tensor(
                            yh[:, 1024 * mc:1024 * mc + 8],
                            pts[:, 8 * mc:8 * (mc + 1)], coreflag[:, 0:1],
                            yh[:, 1024 * mc:1024 * mc + 8],
                            op0=ALU.mult, op1=ALU.add)
                    v.engine_nop().then_inc(C.sem("vv"), 1)
            C.c["vv"] = vv0 + 1

        def ysrc(c, _th=th):
            return yh[:, 1024 * c:1024 * (c + 1)]
        emit_ln_stats_mid(C, ysrc, 1024, src_dtype=F32)
        emit_ln_tail(C, 1024, 1024 * th, "post", x_dram=x_in, ysrc=ysrc,
                     store=x_out, xst2=ws1[:, 0:2048].bitcast(F32),
                     sq2=ws1[:, 2048:4096].bitcast(F32),
                     acp=ws1[:, 6144:8192].bitcast(F32),
                     bcp=ws2[:, 0:2048].bitcast(F32))


def emit_ffn1(C, x_in, x_out, bs):
    nc = C.nc
    xn = C.t["XN"]
    scr = C.t["SCR16"]
    psa, psb = C.t["PSA"], C.t["PSB"]
    id32 = C.t["id32"]
    ones1f, ones128f = C.t["ones1f"], C.t["ones128f"]
    ustrict = C.t["ustrict"]
    maskcol = C.t["maskcol"]
    tmp1 = C.t["TMP1"]

    agc_in = nc.dram_tensor("f1_agc_in", [1, 8], F32)
    agc_out = nc.dram_tensor("f1_agc_out", [8, 8], F32)
    rt16 = nc.dram_tensor("f1_rt16", [1, 128], F32)

    f1xst2 = scr[:, 0:4096].bitcast(F32)
    f1sq2 = scr[:, 4096:8192].bitcast(F32)
    emit_ln_stats_from_dram(C, x_in, xst2=f1xst2, sq2=f1sq2)
    emit_ln_tail(C, T, 0, "pre", x_dram=x_in, out_bf=xn, xst2=f1xst2, sq2=f1sq2,
                 acp=scr[:, 8192:12288].bitcast(F32),
                 bcp=scr[:, 12288:16384].bitcast(F32))

    yh = C.sb("YF", [128, 16384], F32, bs)
    comb = C.sb("COMB", [128, 128], F32, bs)

    with ExitStack() as bs2:
        R = _emit_gate_softmax(C, bs2, "b3_gwT", with_groups=False)
        e1, e2, w1c, w2c, sml, sm = (R["e1"], R["e2"], R["w1"], R["w2"],
                                     R["sml"], R["sm"])
        te = C.sb("TE", [128, 128], F32, bs2)
        tss = C.sb("TSS", [8, 16], F32, bs2)
        stb = C.sb("STB", [16, 8], F32, bs2)
        clrow = C.sb("CLROW", [1, 8], F32, bs2)
        agc = C.sb("AGC", [8, 8], F32, bs2)
        pfx = C.sb("PFX", [8, 1], F32, bs2)
        pbt = C.sb("PBT", [8, 16], F32, bs2)
        pbf = C.sb("PBF", [16, 8], F32, bs2)
        pbrow = C.sb("PBROW", [1, 128], F32, bs2)
        km = C.sb("KM", [128, 128], F32, bs2)
        wn = C.sb("WN", [128, 64], F32, bs2)

        vv0 = C.c["vv"]
        with nc.Block() as blk:
            @blk.vector
            def _(v):
                v.tensor_tensor(wn[:, 0:16], w1c, w2c, op=ALU.add)
                v.drain()
                v.tensor_scalar_add(wn[:, 0:16], wn[:, 0:16], 1e-20)
                v.drain()
                v.reciprocal(wn[:, 16:32], wn[:, 0:16])
                v.drain()
                v.tensor_tensor(wn[:, 32:48], w1c, wn[:, 16:32], op=ALU.mult)
                v.tensor_tensor(wn[:, 48:64], w2c, wn[:, 16:32], op=ALU.mult)
                v.drain()
                v.tensor_tensor(te[:, :], e1[:, :], e2[:, :], op=ALU.add
                                ).then_inc(C.sem("vv"), 1)
        C.c["vv"] = vv0 + 1

        mm0 = C.c["mm"]; ev0 = C.c["ev"]; vv0 = C.c["vv"]
        with nc.Block() as blk:
            @blk.tensor
            def _(t):
                mmi = None
                for tl in range(NTC):
                    mmi = t.matmul(psa[0:8, tl:tl + 1], te[:, 8 * tl:8 * (tl + 1)],
                                   ones128f[:, :], start=True, stop=True)
                mmi.then_inc(C.sem("mm"), 1)

            @blk.scalar
            def _(s):
                s.wait_ge(C.sem("mm"), mm0 + 1)
                s.copy(tss[:, :], psa[0:8, 0:16]).then_inc(C.sem("ev"), 1)
        C.c["mm"] = mm0 + 1
        C.c["ev"] = ev0 + 1

        mm0 = C.c["mm"]; vv0 = C.c["vv"]; ev0 = C.c["ev"]
        with nc.Block() as blk:
            @blk.tensor
            def _(t):
                t.transpose(psa[0:16, 16:24], tss[:, :], id32[0:8, 0:8]
                            ).then_inc(C.sem("mm"), 1)

            @blk.vector
            def _(v):
                v.wait_ge(C.sem("mm"), mm0 + 1)
                v.tensor_copy(stb[:, :], psa[0:16, 16:24]).then_inc(C.sem("vv"), 1)
        C.c["mm"] = mm0 + 1
        C.c["vv"] = vv0 + 1

        mm0 = C.c["mm"]; ev0 = C.c["ev"]
        with nc.Block() as blk:
            @blk.tensor
            def _(t):
                t.matmul(psa[0:1, 24:32], ones128f[0:16, :], stb[:, :],
                         start=True, stop=True).then_inc(C.sem("mm"), 1)

            @blk.scalar
            def _(s):
                s.wait_ge(C.sem("mm"), mm0 + 1)
                s.copy(clrow[:, :], psa[0:1, 24:32]).then_inc(C.sem("ev"), 1)
        C.c["mm"] = mm0 + 1
        C.c["ev"] = ev0 + 1

        gd0 = C.c["gd"]; cc0 = C.c["cc"]
        with nc.Block() as blk:
            @blk.gpsimd
            def _(g):
                g.dma_start(out=agc_in[:, :], in_=clrow[:, :]).then_inc(C.sem("gd"), 16)
                g.wait_ge(C.sem("gd"), gd0 + 16)
                g.collective_compute(
                    "AllGather", ALU.bypass, replica_groups=[list(range(NCORES))],
                    ins=[agc_in.ap().opt()], outs=[agc_out.ap().opt()],
                ).then_inc(C.sem("cc"), 1)
                g.wait_ge(C.sem("cc"), cc0 + 1)
                g.dma_start(out=agc[:, :], in_=agc_out[:, :]).then_inc(C.sem("gd"), 16)
                g.wait_ge(C.sem("gd"), gd0 + 32)
        C.c["gd"] = gd0 + 32
        C.c["cc"] = cc0 + 1

        mm0 = C.c["mm"]; vv0 = C.c["vv"]; ev0 = C.c["ev"]
        with nc.Block() as blk:
            @blk.tensor
            def _(t):
                t.matmul(psa[0:8, 32:33], agc[:, :], maskcol[:, :],
                         start=True, stop=True).then_inc(C.sem("mm"), 1)

            @blk.vector
            def _(v):
                v.wait_ge(C.sem("mm"), mm0 + 1)
                v.tensor_copy(pfx[:, :], psa[0:8, 32:33])
                v.drain()
                v.tensor_tensor_scan(pbt[:, :],
                                     bass.AP(ones128f, 0, [[1, 8], [0, 16]]),
                                     tss[:, :], 0.0, op0=ALU.mult, op1=ALU.add)
                v.drain()
                v.tensor_tensor(pbt[:, :], pbt[:, :], tss[:, :], op=ALU.subtract)
                v.drain()
                v.tensor_scalar(pbt[:, :], pbt[:, :], pfx[:, 0:1], None,
                                op0=ALU.add).then_inc(C.sem("vv"), 1)
        C.c["mm"] = mm0 + 1
        C.c["vv"] = vv0 + 1

        mm0 = C.c["mm"]; vv0 = C.c["vv"]
        with nc.Block() as blk:
            @blk.tensor
            def _(t):
                t.transpose(psa[0:16, 40:48], pbt[:, :], id32[0:8, 0:8]
                            ).then_inc(C.sem("mm"), 1)

            @blk.vector
            def _(v):
                v.wait_ge(C.sem("mm"), mm0 + 1)
                v.tensor_copy(pbf[:, :], psa[0:16, 40:48]).then_inc(C.sem("vv"), 1)
        C.c["mm"] = mm0 + 1
        C.c["vv"] = vv0 + 1

        gd0 = C.c["gd"]
        with nc.Block() as blk:
            @blk.gpsimd
            def _(g):
                g.dma_start(out=bass.AP(rt16, 0, [[8, 16], [1, 8]]),
                            in_=pbf[:, :]).then_inc(C.sem("gd"), 16)
                g.wait_ge(C.sem("gd"), gd0 + 16)
                g.dma_start(out=pbrow[:, :], in_=rt16[:, :]).then_inc(C.sem("gd"), 16)
                g.wait_ge(C.sem("gd"), gd0 + 32)
        C.c["gd"] = gd0 + 32

        mm0 = C.c["mm"]; vv0 = C.c["vv"]
        with nc.Block() as blk:
            @blk.tensor
            def _(t):
                mmi = None
                for tl in range(NTC):
                    t.matmul(psb[:, 8 * tl:8 * (tl + 1)], ustrict[:, :],
                             te[:, 8 * tl:8 * (tl + 1)],
                             start=(tl == 0), stop=False)
                    mmi = t.matmul(psb[:, 8 * tl:8 * (tl + 1)], ones1f[0:1, :],
                                   pbrow[0:1, 8 * tl:8 * (tl + 1)],
                                   start=False, stop=(tl == NTC - 1))
                mmi.then_inc(C.sem("mm"), 1)

            @blk.vector
            def _(v):
                v.wait_ge(C.sem("mm"), mm0 + 1)
                v.tensor_scalar(km[:, :], psb[:, 0:128], float(CAPACITY), None,
                                op0=ALU.is_lt)
                v.drain()
                v.tensor_tensor(sm[:, 0:128], km[:, :], e1[:, :], op=ALU.mult)
                v.drain()
                v.reduce_sum(wn[:, 0:16], sm[:, 0:128].rearrange(
                    "p (t e) -> p t e", e=8), axis=AX.X)
                v.drain()
                v.tensor_tensor(sm[:, 0:128], km[:, :], e2[:, :], op=ALU.mult)
                v.drain()
                v.reduce_sum(wn[:, 16:32], sm[:, 0:128].rearrange(
                    "p (t e) -> p t e", e=8), axis=AX.X)
                v.drain()
                v.tensor_tensor(wn[:, 32:48], wn[:, 32:48], wn[:, 0:16], op=ALU.mult)
                v.tensor_tensor(wn[:, 48:64], wn[:, 48:64], wn[:, 16:32], op=ALU.mult)
                v.drain()
                for tl in range(NTC):
                    v.tensor_scalar(sm[:, 8 * tl:8 * (tl + 1)],
                                    e1[:, 8 * tl:8 * (tl + 1)],
                                    wn[:, 32 + tl:33 + tl], None, op0=ALU.mult)
                v.drain()
                for tl in range(NTC):
                    v.scalar_tensor_tensor(comb[:, 8 * tl:8 * (tl + 1)],
                                           e2[:, 8 * tl:8 * (tl + 1)],
                                           wn[:, 48 + tl:49 + tl],
                                           sm[:, 8 * tl:8 * (tl + 1)],
                                           op0=ALU.mult, op1=ALU.add)
                v.engine_nop().then_inc(C.sem("vv"), 1)
        C.c["mm"] = mm0 + 1
        C.c["vv"] = vv0 + 1

    wse1 = C.sb("WSE1", [128, 8192], BF16, bs)
    wse2 = C.sb("WSE2", [128, 8192], BF16, bs)
    wse3 = C.sb("WSE3", [128, 8192], BF16, bs)

    def load_we(g, dram_name, e, slot):
        g.dma_start(out=slot[:, :],
                    in_=_wview(C.t[dram_name], 8, 1024, row0=1024 * e)
                    ).then_inc(C.sem("gd"), 16)
        C.c["gd"] += 16

    def load_sh(g, dram_name, sh, slot):
        if dram_name == "b3_wsdT":
            ap = _wview(C.t[dram_name], 8, 1024, row0=1024 * sh)
        else:
            ap = _wview(C.t[dram_name], 8, 1024, col0=1024 * sh)
        g.dma_start(out=slot[:, :], in_=ap).then_inc(C.sem("gd"), 16)
        C.c["gd"] += 16

    with nc.Block() as blk:
        @blk.gpsimd
        def _(g):
            load_we(g, "b3_wegT", 0, wse1)
            load_we(g, "b3_weuT", 0, wse2)
            g.wait_ge(C.sem("gd"), C.c["gd"])
    gu_need = C.c["gd"]
    dn_need = 0

    for e in range(8):
        mm0 = C.c["mm"]; ev0 = C.c["ev"]; vv0 = C.c["vv"]
        with nc.Block() as blk:
            @blk.gpsimd
            def _(g):
                load_we(g, "b3_wedT", e, wse3)
                g.wait_ge(C.sem("gd"), C.c["gd"])
            dn_need = C.c["gd"]

            @blk.tensor
            def _(t):
                t.wait_ge(C.sem("gd"), gu_need)
                idx = 0
                for ic in range(8):
                    for s4 in range(4):
                        if idx >= 2:
                            t.wait_ge(C.sem("vv"), vv0 + idx - 1)
                        for dc in range(NDC):
                            t.matmul(psa[:, 1024 * (idx % 2):1024 * (idx % 2) + 512],
                                     wse1[:, 1024 * dc + 128 * ic:1024 * dc + 128 * (ic + 1)],
                                     xn[:, 2048 * dc + 512 * s4:2048 * dc + 512 * (s4 + 1)],
                                     start=(dc == 0), stop=(dc == NDC - 1))
                        for dc in range(NDC):
                            mmi = t.matmul(
                                psa[:, 1024 * (idx % 2) + 512:1024 * (idx % 2) + 1024],
                                wse2[:, 1024 * dc + 128 * ic:1024 * dc + 128 * (ic + 1)],
                                xn[:, 2048 * dc + 512 * s4:2048 * dc + 512 * (s4 + 1)],
                                start=(dc == 0), stop=(dc == NDC - 1))
                        mmi.then_inc(C.sem("mm"), 1)
                        idx += 1

            @blk.scalar
            def _(s):
                idx = 0
                for ic in range(8):
                    for s4 in range(4):
                        s.wait_ge(C.sem("mm"), mm0 + idx + 1)
                        if idx >= 1:
                            s.wait_ge(C.sem("vv"), vv0 + idx)
                        s.activation(out=tmp1[:, 0:512],
                                     in_=psa[:, 1024 * (idx % 2):1024 * (idx % 2) + 512],
                                     func=AF.Silu).then_inc(C.sem("ev"), 1)
                        idx += 1

            @blk.vector
            def _(v):
                idx = 0
                for ic in range(8):
                    for s4 in range(4):
                        v.wait_ge(C.sem("ev"), ev0 + idx + 1)
                        v.tensor_tensor(
                            scr[:, 2048 * ic + 512 * s4:2048 * ic + 512 * (s4 + 1)],
                            psa[:, 1024 * (idx % 2) + 512:1024 * (idx % 2) + 1024],
                            tmp1[:, 0:512], op=ALU.mult).then_inc(C.sem("vv"), 1)
                        idx += 1
        C.c["mm"] = mm0 + 32
        C.c["ev"] = ev0 + 32
        C.c["vv"] = vv0 + 32

        mm0 = C.c["mm"]; vv0 = C.c["vv"]
        with nc.Block() as blk:
            @blk.gpsimd
            def _(g):
                if e < 7:
                    load_we(g, "b3_wegT", e + 1, wse1)
                    load_we(g, "b3_weuT", e + 1, wse2)
                else:
                    load_sh(g, "b3_wsgT", 0, wse1)
                    load_sh(g, "b3_wsuT", 0, wse2)
                g.wait_ge(C.sem("gd"), C.c["gd"])
            gu_need = C.c["gd"]

            @blk.tensor
            def _(t):
                t.wait_ge(C.sem("gd"), dn_need)
                for tc in range(NTC):
                    if tc >= 2:
                        t.wait_ge(C.sem("vv"), vv0 + tc - 1)
                    for ic in range(8):
                        for nh in range(2):
                            mmi = t.matmul(
                                psa[:, 1024 * (tc % 2) + 512 * nh:
                                    1024 * (tc % 2) + 512 * (nh + 1)],
                                scr[:, 2048 * ic + 128 * tc:2048 * ic + 128 * (tc + 1)],
                                wse3[:, 1024 * ic + 512 * nh:1024 * ic + 512 * (nh + 1)],
                                start=(ic == 0), stop=(ic == 7))
                    mmi.then_inc(C.sem("mm"), 1)

            @blk.vector
            def _(v):
                for tc in range(NTC):
                    v.wait_ge(C.sem("mm"), mm0 + tc + 1)
                    v.drain()
                    if e == 0:
                        v.tensor_scalar(yh[:, 1024 * tc:1024 * (tc + 1)],
                                        psa[:, 1024 * (tc % 2):1024 * (tc % 2) + 1024],
                                        comb[:, 8 * tc + e:8 * tc + e + 1], None,
                                        op0=ALU.mult).then_inc(C.sem("vv"), 1)
                    else:
                        v.scalar_tensor_tensor(yh[:, 1024 * tc:1024 * (tc + 1)],
                                               psa[:, 1024 * (tc % 2):1024 * (tc % 2) + 1024],
                                               comb[:, 8 * tc + e:8 * tc + e + 1],
                                               yh[:, 1024 * tc:1024 * (tc + 1)],
                                               op0=ALU.mult, op1=ALU.add
                                               ).then_inc(C.sem("vv"), 1)
        C.c["mm"] = mm0 + NTC
        C.c["vv"] = vv0 + NTC

    for sh in range(2):
        mm0 = C.c["mm"]; ev0 = C.c["ev"]; vv0 = C.c["vv"]
        with nc.Block() as blk:
            @blk.gpsimd
            def _(g):
                load_sh(g, "b3_wsdT", sh, wse3)
                g.wait_ge(C.sem("gd"), C.c["gd"])
            dn_need = C.c["gd"]

            @blk.tensor
            def _(t):
                t.wait_ge(C.sem("gd"), gu_need)
                idx = 0
                for ic in range(8):
                    for s4 in range(4):
                        if idx >= 2:
                            t.wait_ge(C.sem("vv"), vv0 + idx - 1)
                        for dc in range(NDC):
                            t.matmul(psa[:, 1024 * (idx % 2):1024 * (idx % 2) + 512],
                                     wse1[:, 1024 * dc + 128 * ic:1024 * dc + 128 * (ic + 1)],
                                     xn[:, 2048 * dc + 512 * s4:2048 * dc + 512 * (s4 + 1)],
                                     start=(dc == 0), stop=(dc == NDC - 1))
                        for dc in range(NDC):
                            mmi = t.matmul(
                                psa[:, 1024 * (idx % 2) + 512:1024 * (idx % 2) + 1024],
                                wse2[:, 1024 * dc + 128 * ic:1024 * dc + 128 * (ic + 1)],
                                xn[:, 2048 * dc + 512 * s4:2048 * dc + 512 * (s4 + 1)],
                                start=(dc == 0), stop=(dc == NDC - 1))
                        mmi.then_inc(C.sem("mm"), 1)
                        idx += 1

            @blk.scalar
            def _(s):
                idx = 0
                for ic in range(8):
                    for s4 in range(4):
                        s.wait_ge(C.sem("mm"), mm0 + idx + 1)
                        if idx >= 1:
                            s.wait_ge(C.sem("vv"), vv0 + idx)
                        s.activation(out=tmp1[:, 0:512],
                                     in_=psa[:, 1024 * (idx % 2):1024 * (idx % 2) + 512],
                                     func=AF.Silu).then_inc(C.sem("ev"), 1)
                        idx += 1

            @blk.vector
            def _(v):
                idx = 0
                for ic in range(8):
                    for s4 in range(4):
                        v.wait_ge(C.sem("ev"), ev0 + idx + 1)
                        v.tensor_tensor(
                            scr[:, 2048 * ic + 512 * s4:2048 * ic + 512 * (s4 + 1)],
                            psa[:, 1024 * (idx % 2) + 512:1024 * (idx % 2) + 1024],
                            tmp1[:, 0:512], op=ALU.mult).then_inc(C.sem("vv"), 1)
                        idx += 1
        C.c["mm"] = mm0 + 32
        C.c["ev"] = ev0 + 32
        C.c["vv"] = vv0 + 32

        mm0 = C.c["mm"]; vv0 = C.c["vv"]
        with nc.Block() as blk:
            @blk.gpsimd
            def _(g):
                if sh == 0:
                    load_sh(g, "b3_wsgT", 1, wse1)
                    load_sh(g, "b3_wsuT", 1, wse2)
                g.wait_ge(C.sem("gd"), C.c["gd"])
            gu_need = C.c["gd"]

            @blk.tensor
            def _(t):
                t.wait_ge(C.sem("gd"), dn_need)
                for tc in range(NTC):
                    if tc >= 2:
                        t.wait_ge(C.sem("vv"), vv0 + tc - 1)
                    for ic in range(8):
                        for nh in range(2):
                            mmi = t.matmul(
                                psa[:, 1024 * (tc % 2) + 512 * nh:
                                    1024 * (tc % 2) + 512 * (nh + 1)],
                                scr[:, 2048 * ic + 128 * tc:2048 * ic + 128 * (tc + 1)],
                                wse3[:, 1024 * ic + 512 * nh:1024 * ic + 512 * (nh + 1)],
                                start=(ic == 0), stop=(ic == 7))
                    mmi.then_inc(C.sem("mm"), 1)

            @blk.vector
            def _(v):
                for tc in range(NTC):
                    v.wait_ge(C.sem("mm"), mm0 + tc + 1)
                    v.drain()
                    v.tensor_tensor(yh[:, 1024 * tc:1024 * (tc + 1)],
                                    psa[:, 1024 * (tc % 2):1024 * (tc % 2) + 1024],
                                    yh[:, 1024 * tc:1024 * (tc + 1)], op=ALU.add
                                    ).then_inc(C.sem("vv"), 1)
        C.c["mm"] = mm0 + NTC
        C.c["vv"] = vv0 + NTC

    sqs = C.sb("SQS", [128, 16], F32, bs)
    sms = C.sb("SMS", [128, 16], F32, bs)
    rst2 = C.sb("RST2", [128, 16], F32, bs)
    bc2 = C.sb("BC2", [128, 16], F32, bs)
    sqscr = C.sb("SQSCR", [128, 1024], F32, bs)

    ev0 = C.c["ev"]; vv0 = C.c["vv"]
    with nc.Block() as blk:
        @blk.scalar
        def _(s):
            for tc in range(NTC):
                s.drain()
                s.activation(out=sqscr[:, :], in_=yh[:, 1024 * tc:1024 * (tc + 1)],
                             func=AF.Square, accum_out=sqs[:, tc:tc + 1]
                             ).then_inc(C.sem("ev"), 1)

        @blk.vector
        def _(v):
            for tc in range(NTC):
                v.drain()
                v.reduce_sum(sms[:, tc:tc + 1], yh[:, 1024 * tc:1024 * (tc + 1)],
                             axis=AX.X).then_inc(C.sem("vv"), 1)
    C.c["ev"] = ev0 + NTC
    C.c["vv"] = vv0 + NTC

    ev0 = C.c["ev"]; vv0 = C.c["vv"]
    with nc.Block() as blk:
        @blk.vector
        def _(v):
            v.tensor_scalar_mul(sms[:, :], sms[:, :], 1.0 / D)
            v.drain()
            v.tensor_scalar_mul(sqs[:, :], sqs[:, :], 1.0 / D)
            v.drain()
            v.tensor_tensor(rst2[:, :], sms[:, :], sms[:, :], op=ALU.mult)
            v.drain()
            v.tensor_tensor(sqs[:, :], sqs[:, :], rst2[:, :], op=ALU.subtract)
            v.drain()
            v.tensor_scalar_add(sqs[:, :], sqs[:, :], EPS).then_inc(C.sem("vv"), 1)
    C.c["vv"] = vv0 + 1

    ev0 = C.c["ev"]; vv0 = C.c["vv"]
    with nc.Block() as blk:
        @blk.scalar
        def _(s):
            s.activation(out=rst2[:, :], in_=sqs[:, :], func=AF.Ln)
            s.drain()
            s.activation(out=rst2[:, :], in_=rst2[:, :], func=AF.Exp, scale=-0.5
                         ).then_inc(C.sem("ev"), 1)

        @blk.vector
        def _(v):
            v.wait_ge(C.sem("ev"), ev0 + 1)
            v.scalar_tensor_tensor(bc2[:, :], sms[:, :], -1.0, rst2[:, :],
                                   op0=ALU.mult, op1=ALU.mult).then_inc(C.sem("vv"), 1)
    C.c["ev"] = ev0 + 1
    C.c["vv"] = vv0 + 1

    ld0 = C.c["ld"]; mm0 = C.c["mm"]; vv0 = C.c["vv"]; st0 = C.c["st"]
    xst = C.t["XST"]
    with nc.Block() as blk:
        @blk.sync
        def _(sync):
            for tc in range(NTC):
                if tc > 0:
                    sync.wait_ge(C.sem("mm"), mm0 + tc)
                for dc in range(NDC):
                    sync.dma_start(out=xst[:, 128 * dc:128 * (dc + 1)],
                                   in_=x_in[128 * dc:128 * (dc + 1),
                                            128 * tc:128 * (tc + 1)]
                                   ).then_inc(C.sem("ld"), 16)

        @blk.tensor
        def _(t):
            for tc in range(NTC):
                t.wait_ge(C.sem("ld"), ld0 + 128 * (tc + 1))
                if tc >= 2:
                    t.wait_ge(C.sem("vv"), vv0 + 2 * (tc - 1))
                tri_ = None
                for dc in range(NDC):
                    tri_ = t.transpose(
                        psb[:, 1024 * (tc % 2) + 128 * dc:1024 * (tc % 2) + 128 * (dc + 1)],
                        xst[:, 128 * dc:128 * (dc + 1)], id32[:, :])
                tri_.then_inc(C.sem("mm"), 1)

        @blk.vector
        def _(v):
            for tc in range(NTC):
                v.wait_ge(C.sem("mm"), mm0 + tc + 1)
                if tc > 0:
                    v.wait_ge(C.sem("st"), st0 + 16 * tc)
                v.scalar_tensor_tensor(tmp1[:, 0:1024],
                                       yh[:, 1024 * tc:1024 * (tc + 1)],
                                       rst2[:, tc:tc + 1],
                                       psb[:, 1024 * (tc % 2):1024 * (tc % 2) + 1024],
                                       op0=ALU.mult, op1=ALU.add
                                       ).then_inc(C.sem("vv"), 1)
                v.drain()
                v.tensor_scalar(tmp1[:, 0:1024], tmp1[:, 0:1024],
                                bc2[:, tc:tc + 1], None, op0=ALU.add
                                ).then_inc(C.sem("vv"), 1)
                v.drain()

        @blk.scalar
        def _(s):
            for tc in range(NTC):
                s.wait_ge(C.sem("vv"), vv0 + 2 * (tc + 1))
                s.dma_start(out=x_out[128 * tc:128 * (tc + 1), :],
                            in_=tmp1[:, 0:1024]).then_inc(C.sem("st"), 16)
            s.wait_ge(C.sem("st"), st0 + 16 * NTC)
    C.c["ld"] = ld0 + 128 * NTC
    C.c["mm"] = mm0 + NTC
    C.c["vv"] = vv0 + 2 * NTC
    C.c["st"] = st0 + 16 * NTC


# version 16
# speedup vs baseline: 1.0209x; 1.0054x over previous
import math
from contextlib import ExitStack

import numpy as np
import ml_dtypes
import concourse.bass as bass
import concourse.mybir as mybir
from concourse.bass_utils import run_bass_kernel_spmd

F32 = mybir.dt.float32
BF16 = mybir.dt.bfloat16
AF = mybir.ActivationFunctionType
ALU = mybir.AluOpType
AX = mybir.AxisListType
bfdt = ml_dtypes.bfloat16

D = 1024
T = 2048
NCORES = 8
NDC = 8
NTC = 16
VSP = 1408
NQB = 4
HEADS = 16
HD = 64
CAPACITY = 5120
EPS = 1e-5
NEG = -1.0e9

_MODE = "fused"
_DEBUG_DUMP = []



def _host_shared(inputs):
    g = {}
    f32 = np.float32
    g["b0_wq"] = np.ascontiguousarray(inputs["L0_attn_wq"], f32)
    g["b0_wk"] = np.ascontiguousarray(inputs["L0_attn_wk"], f32)
    g["b0_wv"] = np.ascontiguousarray(inputs["L0_attn_wv"], f32)
    g["b0_wo"] = np.ascontiguousarray(inputs["L0_attn_wo"], f32)
    g["b1_wgT"] = np.ascontiguousarray(inputs["L0_ffn_w_gate"].T, f32)
    g["b1_eb"] = np.ascontiguousarray(inputs["L0_ffn_expert_bias"].reshape(1, 8), f32)
    g["b1_sw1T"] = np.ascontiguousarray(inputs["L0_ffn_sw1"].T, f32)
    g["b1_sw3T"] = np.ascontiguousarray(inputs["L0_ffn_sw3"].T, f32)
    g["b1_sw2T"] = np.ascontiguousarray(inputs["L0_ffn_sw2"].T, f32)
    g["b1_w1"] = np.ascontiguousarray(inputs["L0_ffn_w1"].reshape(8 * 1024, 512), f32)
    g["b1_w3"] = np.ascontiguousarray(inputs["L0_ffn_w3"].reshape(8 * 1024, 512), f32)
    g["b1_w2"] = np.ascontiguousarray(inputs["L0_ffn_w2"].reshape(8 * 512, 1024), f32)
    g["b2_wq"] = np.ascontiguousarray(inputs["L1_attn_Wq"], f32)
    g["b2_wk"] = np.ascontiguousarray(inputs["L1_attn_Wk"], f32)
    g["b2_wv"] = np.ascontiguousarray(inputs["L1_attn_Wv"], f32)
    g["b2_wf"] = np.ascontiguousarray(inputs["L1_attn_Wf"], f32)
    for nm, key in (("bq", "L1_attn_bq"), ("bk", "L1_attn_bk"),
                    ("bv", "L1_attn_bv"), ("bf", "L1_attn_bf")):
        g["b2_" + nm] = np.ascontiguousarray(
            inputs[key].reshape(NDC, 128).T, f32)
    g["b3_gwT"] = np.ascontiguousarray(inputs["L1_ffn_gate_w"].T, f32)
    g["b3_wegT"] = np.ascontiguousarray(
        inputs["L1_ffn_we_gate"].transpose(0, 2, 1).reshape(8 * 1024, 1024), f32)
    g["b3_weuT"] = np.ascontiguousarray(
        inputs["L1_ffn_we_up"].transpose(0, 2, 1).reshape(8 * 1024, 1024), f32)
    g["b3_wedT"] = np.ascontiguousarray(
        inputs["L1_ffn_we_down"].transpose(0, 2, 1).reshape(8 * 1024, 1024), f32)
    g["b3_wsgT"] = np.ascontiguousarray(inputs["L1_ffn_ws_gate"].T, f32)
    g["b3_wsuT"] = np.ascontiguousarray(inputs["L1_ffn_ws_up"].T, f32)
    g["b3_wsdT"] = np.ascontiguousarray(inputs["L1_ffn_ws_down"].T, f32)

    g["c_id32"] = np.eye(128, dtype=f32)
    g["c_idbf"] = np.eye(128, dtype=f32).astype(bfdt)
    jj = np.arange(128)[None, :]
    pp = np.arange(128)[:, None]
    g["c_tri_fwd"] = np.where(pp <= jj, 0.0, NEG).astype(f32)
    g["c_tri_bwd"] = np.where(pp >= jj, 0.0, NEG).astype(f32)
    inv_freq = 1.0 / (10000.0 ** (np.arange(0, HD, 2, dtype=np.float64) / HD))
    ang = np.concatenate(
        [(128 * tc + np.arange(128, dtype=np.float64))[:, None] * inv_freq[None, :]
         for tc in range(NTC)], axis=1)
    g["c_cos"] = np.cos(ang).astype(f32)
    g["c_sin"] = np.sin(ang).astype(f32)
    half = HEADS // 2
    base = 24.0 ** (1.0 / half)
    slopes = 1.0 / base ** np.arange(1, half + 1, dtype=np.float64)
    tab = np.zeros((128, HEADS * 16), dtype=f32)
    for h in range(HEADS):
        s = slopes[h % half] * (1.0 if h < half else -1.0)
        for kc in range(16):
            tab[:, 16 * h + kc] = (s * (np.arange(128) + 128 * kc)).astype(f32)
    g["c_slopek"] = tab
    g["c_ustrict"] = np.triu(np.ones((128, 128), dtype=f32), 1)
    g["c_u16"] = np.triu(np.ones((16, 16), dtype=f32), 1)
    qb = np.zeros((HEADS, T), dtype=f32)
    for h in range(HEADS):
        s = slopes[h % half] * (1.0 if h < half else -1.0)
        qb[h] = (-s * np.arange(T)).astype(f32)
    g["c_qbias"] = qb
    g["c_qbias_bf"] = qb.astype(bfdt)
    return g


def _host_core(core):
    f32 = np.float32
    g = {}
    pos = np.zeros((128, 32), dtype=f32)
    for tc in range(NTC):
        for k in range(2):
            pos[:, 2 * tc + k] = 4096 * core + 256 * tc + 2 * np.arange(128) + k
    g["c_pos"] = pos
    g["c_maskcol"] = (np.arange(8) < core).astype(f32).reshape(8, 1)
    g["c_coreflag"] = np.full((128, 1), 1.0 if core == 0 else 0.0, dtype=f32)
    return g


def _host_inputs(inputs):
    shared = _host_shared(inputs)
    x = inputs["L0_attn_x"]
    maps = []
    for c in range(NCORES):
        m = dict(shared)
        m.update(_host_core(c))
        m["x_in"] = np.ascontiguousarray(x[c].T.astype(np.float32))
        maps.append(m)
    return maps



class Ctx:
    def __init__(self, nc, stack):
        self.nc = nc
        self.stack = stack
        self.t = {}
        self._sems = {}
        self.c = {}

    def sem(self, name):
        if name not in self._sems:
            self._sems[name] = self.stack.enter_context(self.nc.semaphore(name))
            self.c[name] = 0
        return self._sems[name]

    def bump(self, name, n=1):
        self.sem(name)
        self.c[name] += n
        return self.c[name]

    def sb(self, name, shape, dtype, stack=None):
        self._n = getattr(self, "_n", {})
        k = self._n.get(name, 0)
        self._n[name] = k + 1
        uname = name if k == 0 else f"{name}_{k}"
        h = (stack or self.stack).enter_context(
            self.nc.sbuf_tensor(uname, shape, dtype))
        self.t[name] = h
        return h

    def ps(self, name, shape, dtype=F32, stack=None):
        h = (stack or self.stack).enter_context(
            self.nc.psum_tensor(name, shape, dtype))
        self.t[name] = h
        return h




def _wview(dram, nch, ncols, row0=0, col0=0):
    rs = dram.shape[1]
    return bass.AP(dram, row0 * rs + col0, [[rs, 128], [128 * rs, nch], [1, ncols]])



def emit_ln_stats_from_dram(C, x_dram, xst2=None, sq2=None):
    nc = C.nc
    xst, tmp1 = C.t["XST"], C.t["TMP1"]
    ones128f = C.t["ones128f"]
    psa, psb = C.t["PSA"], C.t["PSB"]
    ld0 = C.c["ld"]; mm0 = C.c["mm"]; ev0 = C.c["ev"]; vv_0 = C.c["vv"]
    gg_0 = C.c["gg"]
    depth = 2 if xst2 is not None else 1

    def stg(c):
        return xst[:, 0:T] if (xst2 is None or c % 2 == 0) else xst2

    def sqb(c):
        return tmp1[:, 0:T] if (sq2 is None or c % 2 == 0) else sq2

    la0 = C.c["lda"] if "lda" in C.c else C.bump("lda", 0)
    lb0 = C.c["ldb"] if "ldb" in C.c else C.bump("ldb", 0)

    def ldsem(c):
        return C.sem("lda") if c % 2 == 0 else C.sem("ldb")

    def ldval(c):
        return (la0 if c % 2 == 0 else lb0) + 16 * (c // 2 + 1)

    with nc.Block() as blk:
        @blk.sync
        def _(sync):
            for c in range(NDC):
                if c >= depth:
                    sync.wait_ge(C.sem("mm"), mm0 + c - depth + 1)
                sync.dma_start(out=stg(c), in_=x_dram[128 * c:128 * (c + 1), :]
                               ).then_inc(ldsem(c), 16)

        @blk.vector
        def _(v):
            for c in range(NDC):
                v.wait_ge(ldsem(c), ldval(c))
                if c >= depth:
                    v.wait_ge(C.sem("mm"), mm0 + c - depth + 1)
                v.drain()
                v.tensor_tensor(sqb(c)[:, 0:VSP], stg(c)[:, 0:VSP],
                                stg(c)[:, 0:VSP], op=ALU.mult
                                ).then_inc(C.sem("vv"), 1)

        @blk.gpsimd
        def _(g):
            for c in range(NDC):
                g.wait_ge(ldsem(c), ldval(c))
                if c >= depth:
                    g.wait_ge(C.sem("mm"), mm0 + c - depth + 1)
                g.drain()
                g.tensor_tensor(sqb(c)[:, VSP:T], stg(c)[:, VSP:T],
                                stg(c)[:, VSP:T], op=ALU.mult
                                ).then_inc(C.sem("gg"), 1)

        @blk.tensor
        def _(t):
            for c in range(NDC):
                t.wait_ge(C.sem("vv"), vv_0 + c + 1)
                t.wait_ge(C.sem("gg"), gg_0 + c + 1)
                for s4 in range(4):
                    t.matmul(psa[0:1, 512 * s4:512 * (s4 + 1)], ones128f[:, :],
                             stg(c)[:, 512 * s4:512 * (s4 + 1)],
                             start=(c == 0), stop=(c == NDC - 1))
                for s4 in range(4):
                    mmi = t.matmul(psb[0:1, 512 * s4:512 * (s4 + 1)], ones128f[:, :],
                                   sqb(c)[:, 512 * s4:512 * (s4 + 1)],
                                   start=(c == 0), stop=(c == NDC - 1))
                mmi.then_inc(C.sem("mm"), 1)
    C.c["lda"] = la0 + 16 * (NDC // 2)
    C.c["ldb"] = lb0 + 16 * (NDC // 2)
    C.c["vv"] = vv_0 + NDC
    C.c["gg"] = gg_0 + NDC
    C.c["mm"] = mm0 + NDC


def emit_ln_tail(C, width, col0, mode, x_dram=None, out_bf=None,
                 ysrc=None, store=None, extra_store_wait=True,
                 xst2=None, sq2=None, acp=None, bcp=None,
                 fold_stats=False, stats_row=0):
    nc = C.nc
    xst, tmp1 = C.t["XST"], C.t["TMP1"]
    id32 = C.t["id32"]
    ones1f = C.t["ones1f"]
    psa, psb = C.t["PSA"], C.t["PSB"]
    stt, mean, var, rstd, bcol = (C.t["STT"], C.t["MEAN"], C.t["VAR"],
                                  C.t["RSTD"], C.t["BCOL"])
    arow = C.t["XST"]
    brow = C.t["TMP1"]
    ones128f = C.t["ones128f"]
    ntile = width // 128
    nslice = width // 512
    srow = stats_row

    ev0 = C.c["ev"]; mm0 = C.c["mm"]; vv0 = C.c["vv"]
    with nc.Block() as blk:
        @blk.scalar
        def _(s):
            s.copy(arow[0:1, 0:width], psa[srow:srow + 1, 0:width])
            s.copy(brow[0:1, 0:width], psb[srow:srow + 1, 0:width]
                   ).then_inc(C.sem("ev"), 1)

        @blk.tensor
        def _(t):
            t.wait_ge(C.sem("ev"), ev0 + 1)
            tri_ = None
            for q, rsrc in ((0, arow), (1, brow)):
                for tl in range(ntile):
                    tri_ = t.transpose(psa[:, ntile * q + tl:ntile * q + tl + 1],
                                       rsrc[0:1, 128 * tl:128 * (tl + 1)],
                                       id32[0:1, 0:1])
            tri_.then_inc(C.sem("mm"), 1)

        @blk.vector
        def _(v):
            v.wait_ge(C.sem("mm"), mm0 + 1)
            v.tensor_copy(stt[:, 0:2 * ntile], psa[:, 0:2 * ntile])
            v.drain()
            v.tensor_scalar_mul(mean[:, 0:ntile], stt[:, 0:ntile], 1.0 / D)
            v.tensor_scalar_mul(var[:, 0:ntile], stt[:, ntile:2 * ntile], 1.0 / D)
            v.drain()
            v.tensor_tensor(stt[:, 48:48 + ntile], mean[:, 0:ntile], mean[:, 0:ntile],
                            op=ALU.mult)
            v.drain()
            v.tensor_tensor(var[:, 0:ntile], var[:, 0:ntile], stt[:, 48:48 + ntile],
                            op=ALU.subtract)
            v.drain()
            v.tensor_scalar_add(var[:, 0:ntile], var[:, 0:ntile], EPS
                                ).then_inc(C.sem("vv"), 1)
    C.c["ev"] = ev0 + 1
    C.c["mm"] = mm0 + 1
    C.c["vv"] = vv0 + 1

    ev0 = C.c["ev"]; mm0 = C.c["mm"]; vv0 = C.c["vv"]
    with nc.Block() as blk:
        @blk.scalar
        def _(s):
            s.activation(out=rstd[:, 0:ntile], in_=var[:, 0:ntile], func=AF.Ln)
            s.drain()
            s.activation(out=rstd[:, 0:ntile], in_=rstd[:, 0:ntile], func=AF.Exp,
                         scale=-0.5).then_inc(C.sem("ev"), 1)

        @blk.vector
        def _(v):
            v.wait_ge(C.sem("ev"), ev0 + 1)
            v.scalar_tensor_tensor(bcol[:, 0:ntile], mean[:, 0:ntile], -1.0,
                                   rstd[:, 0:ntile], op0=ALU.mult, op1=ALU.mult
                                   ).then_inc(C.sem("vv"), 1)

        @blk.tensor
        def _(t):
            t.wait_ge(C.sem("vv"), vv0 + 1)
            for tl in range(ntile):
                t.transpose(psa[0:1, 128 * tl:128 * (tl + 1)],
                            rstd[:, tl:tl + 1], id32[:, :])
            tri_ = None
            for tl in range(ntile):
                tri_ = t.transpose(psb[0:1, 128 * tl:128 * (tl + 1)],
                                   bcol[:, tl:tl + 1], id32[:, :])
            tri_.then_inc(C.sem("mm"), 1)
    C.c["ev"] = ev0 + 1
    C.c["vv"] = vv0 + 1
    C.c["mm"] = mm0 + 1

    ev0 = C.c["ev"]; mm0 = C.c["mm"]
    wsp = (width * 11 // 16) // 64 * 64
    with nc.Block() as blk:
        @blk.scalar
        def _(s):
            s.copy(arow[0:1, 0:width], psa[0:1, 0:width])
            s.copy(brow[0:1, 0:width], psb[0:1, 0:width]).then_inc(C.sem("ev"), 1)
            s.wait_ge(C.sem("mm"), mm0 + 1)
            s.copy(acp[:, 0:width], psa[:, col0:col0 + width])
            s.copy(bcp[:, 0:width], psb[:, col0:col0 + width]
                   ).then_inc(C.sem("ev"), 1)

        @blk.tensor
        def _(t):
            t.wait_ge(C.sem("ev"), ev0 + 1)
            for s4 in range(nslice):
                t.matmul(psa[:, col0 + 512 * s4:col0 + 512 * (s4 + 1)], ones1f[:, :],
                         arow[0:1, 512 * s4:512 * (s4 + 1)], start=True, stop=True)
            mmi = None
            for s4 in range(nslice):
                mmi = t.matmul(psb[:, col0 + 512 * s4:col0 + 512 * (s4 + 1)], ones1f[:, :],
                               brow[0:1, 512 * s4:512 * (s4 + 1)], start=True, stop=True)
            mmi.then_inc(C.sem("mm"), 1)
    C.c["ev"] = ev0 + 2
    C.c["mm"] = mm0 + 1
    cp_done = C.c["ev"]

    ld0 = C.c["ld"]; vv0 = C.c["vv"]; mm_done = C.c["mm"]; st0 = C.c["st"]
    depth = 2 if xst2 is not None else 1

    def stg(c):
        return (xst[:, 0:width] if (xst2 is None or c % 2 == 0)
                else xst2[:, 0:width])

    def sqb(c):
        return (tmp1[:, 0:width] if (sq2 is None or c % 2 == 0)
                else sq2[:, 0:width])

    la0 = C.c["lda"]
    lb0 = C.c["ldb"]
    sa0 = C.c["sta"] if "sta" in C.c else C.bump("sta", 0)
    sb0 = C.c["stb"] if "stb" in C.c else C.bump("stb", 0)

    def ldsem(c):
        return C.sem("lda") if c % 2 == 0 else C.sem("ldb")

    def ldval(c):
        return (la0 if c % 2 == 0 else lb0) + 16 * (c // 2 + 1)

    def stsem(c):
        return C.sem("sta") if c % 2 == 0 else C.sem("stb")

    def stval(c):
        return (sa0 if c % 2 == 0 else sb0) + 16 * (c // 2 + 1)

    gg0 = C.c["gg"]
    ipc = 3 if fold_stats else 2
    mmf0 = C.c["mm"]
    assert not (fold_stats and mode == "pre")

    with nc.Block() as blk:
        @blk.sync
        def _(sync):
            for c in range(NDC):
                if mode == "post" and c >= depth:
                    sync.wait_ge(stsem(c - depth), stval(c - depth))
                    if fold_stats:
                        sync.wait_ge(C.sem("mm"), mmf0 + c - depth + 1)
                elif c >= depth:
                    sync.wait_ge(C.sem("vv"), vv0 + ipc * (c - depth + 1))
                    sync.wait_ge(C.sem("gg"), gg0 + ipc * (c - depth + 1))
                sync.dma_start(out=stg(c),
                               in_=x_dram[128 * c:128 * (c + 1), col0:col0 + width]
                               ).then_inc(ldsem(c), 16)

        def apply_chunks(eng, lo, hi, sem_name, atile, btile):
            for c in range(NDC):
                eng.wait_ge(ldsem(c), ldval(c))
                if mode == "post" and depth == 2 and c >= depth:
                    eng.wait_ge(stsem(c - depth), stval(c - depth))
                    if fold_stats:
                        eng.wait_ge(C.sem("mm"), mmf0 + c - depth + 1)
                eng.drain()
                if mode == "pre":
                    eng.tensor_tensor(sqb(c)[:, lo:hi], stg(c)[:, lo:hi], atile,
                                      op=ALU.mult)
                    eng.drain()
                    eng.tensor_tensor(
                        out_bf[:, T * c + col0 + lo:T * c + col0 + hi],
                        sqb(c)[:, lo:hi], btile, op=ALU.add
                        ).then_inc(C.sem(sem_name), 2)
                else:
                    eng.tensor_tensor(sqb(c)[:, lo:hi], ysrc(c)[:, lo:hi], atile,
                                      op=ALU.mult)
                    eng.drain()
                    eng.tensor_tensor(sqb(c)[:, lo:hi], sqb(c)[:, lo:hi], btile,
                                      op=ALU.add)
                    eng.drain()
                    eng.tensor_tensor(stg(c)[:, lo:hi], sqb(c)[:, lo:hi],
                                      stg(c)[:, lo:hi], op=ALU.add
                                      ).then_inc(C.sem(sem_name), 2)
                    if fold_stats:
                        eng.drain()
                        eng.tensor_tensor(sqb(c)[:, lo:hi], stg(c)[:, lo:hi],
                                          stg(c)[:, lo:hi], op=ALU.mult
                                          ).then_inc(C.sem(sem_name), 1)

        @blk.vector
        def _(v):
            v.wait_ge(C.sem("ev"), cp_done)
            apply_chunks(v, 0, wsp, "vv", acp[:, 0:wsp], bcp[:, 0:wsp])

        @blk.gpsimd
        def _(g):
            g.wait_ge(C.sem("ev"), cp_done)
            apply_chunks(g, wsp, width, "gg",
                         acp[:, wsp:width], bcp[:, wsp:width])

        if fold_stats:
            @blk.tensor
            def _(t):
                for c in range(NDC):
                    t.wait_ge(C.sem("vv"), vv0 + 3 * (c + 1))
                    t.wait_ge(C.sem("gg"), gg0 + 3 * (c + 1))
                    for s4 in range(nslice):
                        t.matmul(psa[32:33, col0 + 512 * s4:col0 + 512 * (s4 + 1)],
                                 ones128f[:, :],
                                 stg(c)[:, 512 * s4:512 * (s4 + 1)],
                                 start=(c == 0), stop=(c == NDC - 1))
                    mmi = None
                    for s4 in range(nslice):
                        mmi = t.matmul(
                            psb[32:33, col0 + 512 * s4:col0 + 512 * (s4 + 1)],
                            ones128f[:, :], sqb(c)[:, 512 * s4:512 * (s4 + 1)],
                            start=(c == 0), stop=(c == NDC - 1))
                    mmi.then_inc(C.sem("mm"), 1)

        if mode == "post":
            @blk.scalar
            def _(s):
                for c in range(NDC):
                    s.wait_ge(C.sem("vv"), vv0 + ipc * c + 2)
                    s.wait_ge(C.sem("gg"), gg0 + ipc * c + 2)
                    s.dma_start(out=store[128 * c:128 * (c + 1), col0:col0 + width],
                                in_=stg(c)).then_inc(stsem(c), 16)
                if extra_store_wait:
                    s.wait_ge(C.sem("sta"), sa0 + 16 * (NDC // 2))
                    s.wait_ge(C.sem("stb"), sb0 + 16 * (NDC // 2))
    C.c["lda"] = la0 + 16 * (NDC // 2)
    C.c["ldb"] = lb0 + 16 * (NDC // 2)
    C.c["vv"] = vv0 + ipc * NDC
    C.c["gg"] = gg0 + ipc * NDC
    if fold_stats:
        C.c["mm"] = mmf0 + NDC
    if mode == "post":
        C.c["sta"] = sa0 + 16 * (NDC // 2)
        C.c["stb"] = sb0 + 16 * (NDC // 2)



def _kbs_phase1(direction, qc):
    return list(range(0, qc // 4 + 1)) if direction == "fwd" else \
        list(range(qc // 4, 4))


def _kcs_phase2(direction, qb):
    return list(range(0, 4 * qb + 4)) if direction == "fwd" else \
        list(range(4 * qb, 16))


def emit_attn(C, layer, x_in, x_out, bs, fold_in=False, fold_out=False):
    nc = C.nc
    xn = C.t["XN"]
    scr = C.t["SCR16"]
    psa, psb = C.t["PSA"], C.t["PSB"]
    id32, idbf = C.t["id32"], C.t["idbf"]
    ones1f = C.t["ones1f"]
    tmp1 = C.t["TMP1"]

    qh = C.sb("QH", [128, 8192], BF16, bs)
    kh = C.sb("KH", [128, 8192], BF16, bs)
    vh = C.sb("VH", [128, 8320], BF16, bs)
    oh = C.sb("OH", [128, 8192], BF16, bs)
    pt4 = C.t["XST"][:, :].bitcast(BF16)
    C.t["OTMP"] = C.t["TMP1"][0:64, 1024:1280].bitcast(BF16)
    ptm = C.t["TMP1"][:, 0:512]
    ktm = C.t["TMP1"][:, 512:1024]
    roq = C.sb("ROQ", [128, 1024], BF16, bs)
    rok = C.sb("ROK", [128, 1024], BF16, bs)
    mhall = C.sb("MHALL", [128, 128], F32, bs)
    mqc = C.sb("MQC", [16, 8], F32, bs)
    m8 = C.sb("M8", [8, 16], F32, bs)
    mrows = C.sb("MROWS", [1, 160], F32, bs)
    mcols32 = C.sb("MCOLS32", [128, 32], F32, bs)
    biast = C.sb("BIAST", [128, 512], F32, bs)
    lrec = C.t["TMP1"][0:1, 0:512]
    lb = C.t["TMP1"][0:64, 512:1024]
    wsa = C.sb("WSA", [128, 4096], BF16, bs)
    wsb = C.sb("WSB", [128, 4096], BF16, bs)
    wsc = C.sb("WSC", [128, 4096], BF16, bs)
    qbra = qbrb = None
    if layer == 1:
        bqc = C.sb("BQC", [128, 8], F32, bs)
        bkc = C.sb("BKC", [128, 8], F32, bs)
        bvc = C.sb("BVC", [128, 8], F32, bs)
        bfc = C.sb("BFC", [128, 8], F32, bs)
        bvb = C.sb("BVB", [128, 8], BF16, bs)
        qbra = C.sb("QBRA", [1, 2048], BF16, bs)
        qbrb = C.sb("QBRB", [1, 2048], BF16, bs)
        bfold = C.sb("BFOLD", [128, 8], F32, bs)
        bfold2 = C.sb("BFOLD2", [128, 8], F32, bs)
    wq_d = C.t["b0_wq" if layer == 0 else "b2_wq"]
    wk_d = C.t["b0_wk" if layer == 0 else "b2_wk"]
    wv_d = C.t["b0_wv" if layer == 0 else "b2_wv"]
    wo_d = C.t["b0_wo" if layer == 0 else "b2_wf"]

    def load_w_half(g, dram, slot, half):
        for dc in range(NDC):
            g.dma_start(out=slot[:, 512 * dc:512 * (dc + 1)],
                        in_=dram[128 * dc:128 * (dc + 1), 512 * half:512 * (half + 1)]
                        ).then_inc(C.sem("gd"), 16)
        C.c["gd"] += 16 * 8

    def load_wo_half(g, slot, half):
        for rc in range(4):
            g.dma_start(out=slot[:, 1024 * rc:1024 * (rc + 1)],
                        in_=wo_d[512 * half + 128 * rc:512 * half + 128 * (rc + 1), :]
                        ).then_inc(C.sem("gd"), 16)
        C.c["gd"] += 16 * 4

    with nc.Block() as blk:
        @blk.gpsimd
        def _(g):
            load_w_half(g, wq_d, wsa, 0)
            load_w_half(g, wk_d, wsb, 0)
            load_w_half(g, wv_d, wsc, 0)
        if layer == 1:
            ld0 = C.c["ld"]

            @blk.sync
            def _(sync):
                for dst, nm in ((bqc, "b2_bq"), (bkc, "b2_bk"),
                                (bvc, "b2_bv"), (bfc, "b2_bf")):
                    sync.dma_start(out=dst[:, :], in_=C.t[nm][:, :]
                                   ).then_inc(C.sem("ld"), 16)

            @blk.vector
            def _(v):
                v.wait_ge(C.sem("ld"), ld0 + 64)
                v.tensor_scalar_mul(bqc[:, :], bqc[:, :], 0.125)
                v.tensor_copy(bvb[:, :], bvc[:, :])
            C.c["ld"] = ld0 + 64

    axst2 = qh[:, 0:4096].bitcast(F32)
    asq2 = kh[:, 0:4096].bitcast(F32)
    if not fold_in:
        emit_ln_stats_from_dram(C, x_in, xst2=axst2, sq2=asq2)
    emit_ln_tail(C, T, 0, "pre", x_dram=x_in, out_bf=xn, xst2=axst2, sq2=asq2,
                 acp=vh[:, 0:4096].bitcast(F32), bcp=vh[:, 4096:8192].bitcast(F32),
                 stats_row=32 if fold_in else 0)

    for half in range(2):
        direction_of = lambda hh: ("fwd" if (8 * half + hh) < 8 else "bwd") \
            if layer == 1 else "fwd"

        gd_need = C.c["gd"]
        if layer == 0:
            _qkv_l0(C, half, gd_need, qh, kh, vh, ptm, ktm, roq, rok,
                    wsa, wsb, wsc)
        else:
            _qkv_l1(C, half, gd_need, qh, kh, vh, ptm, wsa, wsb, wsc, bqc, bkc)

        woslot = wsc if half == 0 else wsa
        with nc.Block() as blk:
            @blk.gpsimd
            def _(g):
                if half == 0:
                    load_w_half(g, wq_d, wsa, 1)
                    load_w_half(g, wk_d, wsb, 1)
                load_wo_half(g, woslot, half)
        wo_gd_need = C.c["gd"]

        with nc.Block() as blk:
            @blk.vector
            def _(v):
                ap = vh.ap().rearrange("p (tc h j) -> p tc h j", h=8, j=65)
                v.memset(ap[:, :, :, 64:65], 1.0)

        _attn_half(C, layer, half, qh, kh, vh, oh, pt4,
                   mhall, mqc, m8, mrows, mcols32, biast, lrec, lb, qbra, qbrb)

        if layer == 1:
            _bfold(C, half, woslot, C.c["gd"], bvb, bfc, bfold, bfold2)

        _wo_proj(C, layer, half, woslot, C.c["gd"], oh, scr,
                 bfold if layer == 1 else None,
                 bfold2 if layer == 1 else None)

        if half == 0:
            with nc.Block() as blk:
                @blk.gpsimd
                def _(g):
                    load_w_half(g, wv_d, wsc, 1)

    def ysrc(c):
        return scr[:, T * c:T * (c + 1)]
    emit_ln_stats_mid(C, ysrc, T, src_dtype=BF16)
    emit_ln_tail(C, T, 0, "post", x_dram=x_in, ysrc=ysrc, store=x_out,
                 xst2=axst2, sq2=asq2,
                 acp=vh[:, 0:4096].bitcast(F32), bcp=vh[:, 4096:8192].bitcast(F32),
                 fold_stats=fold_out)


def emit_ln_stats_mid(C, src_ap_fn, width, src_dtype=F32, nchunk=NDC):
    nc = C.nc
    tmp1 = C.t["TMP1"]
    ones128f, ones128b = C.t["ones128f"], C.t["ones128b"]
    psa, psb = C.t["PSA"], C.t["PSB"]
    mm0 = C.c["mm"]; ev0 = C.c["ev"]
    nslice = width // 512
    ones = ones128f if src_dtype == F32 else ones128b
    vv_0 = C.c["vv"]; gg_0 = C.c["gg"]
    wsp = (width * 11 // 16) // 64 * 64
    with nc.Block() as blk:
        @blk.vector
        def _(v):
            for c in range(nchunk):
                if c > 0:
                    v.wait_ge(C.sem("mm"), mm0 + c)
                v.drain()
                v.tensor_tensor(tmp1[:, 0:wsp], src_ap_fn(c)[:, 0:wsp],
                                src_ap_fn(c)[:, 0:wsp],
                                op=ALU.mult).then_inc(C.sem("vv"), 1)

        @blk.gpsimd
        def _(g):
            for c in range(nchunk):
                if c > 0:
                    g.wait_ge(C.sem("mm"), mm0 + c)
                g.drain()
                g.tensor_tensor(tmp1[:, wsp:width], src_ap_fn(c)[:, wsp:width],
                                src_ap_fn(c)[:, wsp:width],
                                op=ALU.mult).then_inc(C.sem("gg"), 1)

        @blk.tensor
        def _(t):
            for c in range(nchunk):
                t.wait_ge(C.sem("vv"), vv_0 + c + 1)
                t.wait_ge(C.sem("gg"), gg_0 + c + 1)
                for s4 in range(nslice):
                    t.matmul(psa[0:1, 512 * s4:512 * (s4 + 1)], ones[:, :],
                             src_ap_fn(c)[:, 512 * s4:512 * (s4 + 1)],
                             start=(c == 0), stop=(c == nchunk - 1))
                for s4 in range(nslice):
                    mmi = t.matmul(psb[0:1, 512 * s4:512 * (s4 + 1)], ones128f[:, :],
                                   tmp1[:, 512 * s4:512 * (s4 + 1)],
                                   start=(c == 0), stop=(c == nchunk - 1))
                mmi.then_inc(C.sem("mm"), 1)
    C.c["vv"] = vv_0 + nchunk
    C.c["gg"] = gg_0 + nchunk
    C.c["mm"] = mm0 + nchunk


def _qkv_l0(C, half, gd_need, qh, kh, vh, ptm, ktm, roq, rok,
            wsa, wsb, wsc):
    nc = C.nc
    xn = C.t["XN"]
    psa, psb = C.t["PSA"], C.t["PSB"]
    idbf = C.t["idbf"]
    cos, sin = C.t["cos"], C.t["sin"]
    tmp1 = C.t["TMP1"]
    mm0 = C.c["mm"]; ev0 = C.c["ev"]; vv0 = C.c["vv"]; gg0 = C.c["gg"]

    xstf = C.t["XST"]

    def pp(tc):
        return psa if tc % 2 == 0 else psb

    def tp(tc):
        return (psb if tc % 2 == 0 else psa)[:, :].bitcast(BF16)

    def ptmf(tc):
        return tmp1[:, 0:512] if tc % 2 == 0 else tmp1[:, 1024:1536]

    def ktmf(tc):
        return tmp1[:, 512:1024] if tc % 2 == 0 else tmp1[:, 1536:2048]

    def rq(tc):
        return roq[:, 512 * (tc % 2):512 * (tc % 2) + 512]

    def rk(tc):
        return rok[:, 512 * (tc % 2):512 * (tc % 2) + 512]

    def rope(eng, src, dst, tc, toff, cnt_sem):
        si = src.rearrange("p (h f two) -> p h f two", two=2, f=32)
        do = dst.rearrange("p (h f two) -> p h f two", two=2, f=32)
        ev_i, od_i = si[:, :, :, 0], si[:, :, :, 1]
        ev_o, od_o = do[:, :, :, 0], do[:, :, :, 1]
        sl = [xstf[:, toff + 256 * k:toff + 256 * (k + 1)].rearrange(
            "p (h f) -> p h f", f=32) for k in range(4)]
        cosb = bass.AP(cos, 32 * tc, [[512, 128], [0, 8], [1, 32]])
        sinb = bass.AP(sin, 32 * tc, [[512, 128], [0, 8], [1, 32]])
        eng.drain()
        eng.tensor_tensor(sl[0], ev_i, cosb, op=ALU.mult)
        eng.tensor_tensor(sl[1], od_i, sinb, op=ALU.mult)
        eng.tensor_tensor(sl[2], ev_i, sinb, op=ALU.mult)
        eng.tensor_tensor(sl[3], od_i, cosb, op=ALU.mult)
        eng.drain()
        eng.tensor_tensor(ev_o, sl[0], sl[1], op=ALU.subtract)
        eng.tensor_tensor(od_o, sl[2], sl[3], op=ALU.add
                          ).then_inc(C.sem(cnt_sem), 1)

    def emit_transp(t, tc):
        t.wait_ge(C.sem("vv"), vv0 + tc + 1)
        t.wait_ge(C.sem("gg"), gg0 + tc + 1)
        if tc >= 2:
            t.wait_ge(C.sem("ev"), ev0 + 8 * (tc - 2) + 8)
        for i in range(4):
            t.transpose(tp(tc)[:, 3072 + 128 * i:3072 + 128 * (i + 1)],
                        rq(tc)[:, 128 * i:128 * (i + 1)], idbf[:, :])
        tri_ = None
        for i in range(4):
            tri_ = t.transpose(tp(tc)[:, 3584 + 128 * i:3584 + 128 * (i + 1)],
                               rk(tc)[:, 128 * i:128 * (i + 1)], idbf[:, :])
        tri_.then_inc(C.sem("mm"), 1)

    with nc.Block() as blk:
        @blk.tensor
        def _(t):
            t.wait_ge(C.sem("gd"), gd_need)
            for tc in range(NTC):
                if tc >= 2:
                    t.wait_ge(C.sem("ev"), ev0 + 8 * tc - 12)
                for dc in range(NDC):
                    t.matmul(pp(tc)[:, 0:512],
                             xn[:, 2048 * dc + 128 * tc:2048 * dc + 128 * (tc + 1)],
                             wsa[:, 512 * dc:512 * (dc + 1)],
                             start=(dc == 0), stop=(dc == NDC - 1))
                for dc in range(NDC):
                    t.matmul(pp(tc)[:, 512:1024],
                             xn[:, 2048 * dc + 128 * tc:2048 * dc + 128 * (tc + 1)],
                             wsb[:, 512 * dc:512 * (dc + 1)],
                             start=(dc == 0), stop=(dc == NDC - 1))
                for dc in range(NDC):
                    mmi = t.matmul(pp(tc)[:, 1024:1536],
                                   xn[:, 2048 * dc + 128 * tc:2048 * dc + 128 * (tc + 1)],
                                   wsc[:, 512 * dc:512 * (dc + 1)],
                                   start=(dc == 0), stop=(dc == NDC - 1))
                mmi.then_inc(C.sem("mm"), 1)
                if tc > 0:
                    emit_transp(t, tc - 1)
            emit_transp(t, NTC - 1)

        @blk.scalar
        def _(s):
            for tc in range(NTC):
                s.wait_ge(C.sem("mm"), mm0 + (1 if tc == 0 else 2 * tc))
                if tc >= 2:
                    s.wait_ge(C.sem("vv"), vv0 + tc - 1)
                    s.wait_ge(C.sem("gg"), gg0 + tc - 1)
                s.activation(out=ptmf(tc), in_=pp(tc)[:, 0:512], func=AF.Copy,
                             scale=0.125)
                s.activation(out=ktmf(tc), in_=pp(tc)[:, 512:1024], func=AF.Copy)
                pvh = vh.ap().rearrange("p (tc h j) -> p tc h j", h=8, j=65)
                psv = pp(tc).ap().rearrange("p (b h j) -> p b h j", b=4, j=64)[:, 2]
                s.activation(out=pvh[:, tc, :, 0:64], in_=psv, func=AF.Copy
                             ).then_inc(C.sem("ev"), 4)
                s.wait_ge(C.sem("mm"),
                          mm0 + (2 * tc + 3 if tc < NTC - 1 else 2 * NTC))
                s.copy(bass.AP(qh, 128 * tc, [[8192, 128], [2048, 4], [1, 128]]),
                       tp(tc)[:, 3072:3584])
                s.copy(bass.AP(kh, 128 * tc, [[8192, 128], [2048, 4], [1, 128]]),
                       tp(tc)[:, 3584:4096]).then_inc(C.sem("ev"), 4)

        @blk.vector
        def _(v):
            for tc in range(NTC):
                v.wait_ge(C.sem("ev"), ev0 + 8 * tc + 4)
                rope(v, ptmf(tc), rq(tc), tc, 0, "vv")

        @blk.gpsimd
        def _(g):
            for tc in range(NTC):
                g.wait_ge(C.sem("ev"), ev0 + 8 * tc + 4)
                rope(g, ktmf(tc), rk(tc), tc, 1024, "gg")
    C.c["mm"] = mm0 + 2 * NTC
    C.c["ev"] = ev0 + 8 * NTC
    C.c["vv"] = vv0 + NTC
    C.c["gg"] = gg0 + NTC


def _qkv_l1(C, half, gd_need, qh, kh, vh, ptm, wsa, wsb, wsc, bqc, bkc):
    nc = C.nc
    xn = C.t["XN"]
    psa = C.t["PSA"]
    mm0 = C.c["mm"]; vv0 = C.c["vv"]; ev0 = C.c["ev"]

    with nc.Block() as blk:
        @blk.tensor
        def _(t):
            t.wait_ge(C.sem("gd"), gd_need)
            idx = 0
            for proj, ws in ((0, wsa), (1, wsb)):
                for mc4 in range(4):
                    for ts4 in range(4):
                        if idx >= 4:
                            t.wait_ge(C.sem("vv"), vv0 + idx - 3)
                        for dc in range(NDC):
                            mmi = t.matmul(
                                psa[:, 512 * (idx % 4):512 * (idx % 4 + 1)],
                                wsa[:, 512 * dc + 128 * mc4:512 * dc + 128 * (mc4 + 1)]
                                if proj == 0 else
                                wsb[:, 512 * dc + 128 * mc4:512 * dc + 128 * (mc4 + 1)],
                                xn[:, 2048 * dc + 512 * ts4:2048 * dc + 512 * (ts4 + 1)],
                                start=(dc == 0), stop=(dc == NDC - 1))
                        mmi.then_inc(C.sem("mm"), 1)
                        idx += 1

        @blk.vector
        def _(v):
            idx = 0
            for proj in range(2):
                for mc4 in range(4):
                    for ts4 in range(4):
                        v.wait_ge(C.sem("mm"), mm0 + idx + 1)
                        dst = qh if proj == 0 else kh
                        bias = bqc if proj == 0 else bkc
                        if proj == 0:
                            v.tensor_scalar(
                                dst[:, 2048 * mc4 + 512 * ts4:2048 * mc4 + 512 * (ts4 + 1)],
                                psa[:, 512 * (idx % 4):512 * (idx % 4 + 1)],
                                0.125, bias[:, 4 * half + mc4:4 * half + mc4 + 1],
                                op0=ALU.mult, op1=ALU.add).then_inc(C.sem("vv"), 1)
                        else:
                            v.tensor_scalar(
                                dst[:, 2048 * mc4 + 512 * ts4:2048 * mc4 + 512 * (ts4 + 1)],
                                psa[:, 512 * (idx % 4):512 * (idx % 4 + 1)],
                                bias[:, 4 * half + mc4:4 * half + mc4 + 1], None,
                                op0=ALU.add).then_inc(C.sem("vv"), 1)
                        idx += 1
    C.c["mm"] = mm0 + 32
    C.c["vv"] = vv0 + 32

    mm0 = C.c["mm"]; ev0 = C.c["ev"]
    with nc.Block() as blk:
        @blk.tensor
        def _(t):
            for tc in range(NTC):
                if tc > 1:
                    t.wait_ge(C.sem("ev"), ev0 + tc - 1)
                for dc in range(NDC):
                    mmi = t.matmul(
                        psa[:, 512 * (tc % 2):512 * (tc % 2 + 1)],
                        xn[:, 2048 * dc + 128 * tc:2048 * dc + 128 * (tc + 1)],
                        wsc[:, 512 * dc:512 * (dc + 1)],
                        start=(dc == 0), stop=(dc == NDC - 1))
                mmi.then_inc(C.sem("mm"), 1)

        @blk.scalar
        def _(s):
            pvh = vh.ap().rearrange("p (tc h j) -> p tc h j", h=8, j=65)
            for tc in range(NTC):
                s.wait_ge(C.sem("mm"), mm0 + tc + 1)
                psv = psa.ap().rearrange("p (b h j) -> p b h j", b=4, j=64)[:, tc % 2]
                s.activation(out=pvh[:, tc, :, 0:64], in_=psv, func=AF.Copy
                             ).then_inc(C.sem("ev"), 1)
    C.c["mm"] = mm0 + NTC
    C.c["ev"] = ev0 + NTC


def _tile_geom(direction, qb, kc):
    m = kc - 4 * qb
    if direction == "fwd":
        if m < 0:
            return 0, 512, None
        return 128 * m, 512, 128 * m
    else:
        if m > 3:
            return 0, 512, None
        return 0, 128 * m + 128, 128 * m


def _attn_half(C, layer, half, qh, kh, vh, oh, pt4,
               mhall, mqc, m8, mrows, mcols32, biast, lrec, lb, qbra, qbrb):
    nc = C.nc
    psa, psb = C.t["PSA"], C.t["PSB"]
    id32 = C.t["id32"]
    ones1f = C.t["ones1f"]
    ones1b = C.t["ones1b"]

    def direction(hh):
        return ("fwd" if (8 * half + hh) < 8 else "bwd") if layer == 1 else "fwd"

    qsq = pt4[:, 0:2048]
    ksq = pt4[:, 2048:4096]
    mm0 = C.c["mm"]; vv0 = C.c["vv"]; ev0 = C.c["ev"]
    with nc.Block() as blk:
        @blk.scalar
        def _(s):
            for ch in range(4):
                if ch > 0:
                    s.wait_ge(C.sem("mm"), mm0 + 2 * ch)
                    s.drain()
                s.activation(out=qsq, in_=qh[:, 2048 * ch:2048 * (ch + 1)],
                             func=AF.Square)
                s.activation(out=ksq, in_=kh[:, 2048 * ch:2048 * (ch + 1)],
                             func=AF.Square).then_inc(C.sem("ev"), 1)

        @blk.tensor
        def _(t):
            ones128b = C.t["ones128b"]
            for hh in range(8):
                hb = hh % 2
                t.wait_ge(C.sem("ev"), ev0 + hh // 2 + 1)
                if hh >= 1:
                    t.wait_ge(C.sem("vv"), vv0 + hh)
                for s4 in range(4):
                    t.matmul(psa[0:1, 512 * s4:512 * (s4 + 1)],
                             ones128b[64 * hb:64 * (hb + 1), :],
                             qsq[64 * hb:64 * (hb + 1), 512 * s4:512 * (s4 + 1)],
                             start=True, stop=True)
                mmi = None
                for s4 in range(4):
                    mmi = t.matmul(psb[0:1, 512 * s4:512 * (s4 + 1)],
                                   ones128b[64 * hb:64 * (hb + 1), :],
                                   ksq[64 * hb:64 * (hb + 1), 512 * s4:512 * (s4 + 1)],
                                   start=True, stop=True)
                mmi.then_inc(C.sem("mm"), 1)

        @blk.vector
        def _(v):
            for hh in range(8):
                v.wait_ge(C.sem("mm"), mm0 + hh + 1)
                v.reduce_max(mrows[0:1, 4 * hh:4 * (hh + 1)],
                             psa[0:1, 0:2048].rearrange(
                                 "p (a b) -> p a b", b=512), axis=AX.X)
                v.reduce_max(mrows[0:1, 32 + hh:33 + hh],
                             psb[0:1, 0:2048].rearrange(
                                 "p (a b) -> p a b", b=2048), axis=AX.X
                             ).then_inc(C.sem("vv"), 1)
    C.c["mm"] = mm0 + 8
    C.c["vv"] = vv0 + 8
    C.c["ev"] = ev0 + 4

    vv0 = C.c["vv"]; ev0 = C.c["ev"]; mm0 = C.c["mm"]
    with nc.Block() as blk:
        @blk.vector
        def _(v):
            v.tensor_tensor(mrows[0:1, 64:96], mrows[0:1, 0:32],
                            bass.AP(mrows, 32, [[160, 1], [1, 8], [0, 4]]),
                            op=ALU.mult).then_inc(C.sem("vv"), 1)

        @blk.scalar
        def _(s):
            s.wait_ge(C.sem("vv"), vv0 + 1)
            s.activation(out=mrows[0:1, 96:128], in_=mrows[0:1, 64:96], func=AF.Ln)
            s.drain()
            s.activation(out=mrows[0:1, 128:160], in_=mrows[0:1, 96:128],
                         func=AF.Exp, scale=0.5).then_inc(C.sem("ev"), 1)
    C.c["vv"] = vv0 + 1
    C.c["ev"] = ev0 + 1

    vv0 = C.c["vv"]; ev0 = C.c["ev"]; mm0 = C.c["mm"]
    with nc.Block() as blk:
        @blk.vector
        def _(v):
            v.wait_ge(C.sem("ev"), ev0)
            v.tensor_scalar_add(mrows[0:1, 128:160], mrows[0:1, 128:160], 1.0
                                ).then_inc(C.sem("vv"), 1)

        @blk.tensor
        def _(t):
            t.wait_ge(C.sem("vv"), vv0 + 1)
            t.matmul(psb[:, 256:288], ones1f[:, :], mrows[0:1, 128:160],
                     start=True, stop=True).then_inc(C.sem("mm"), 1)
    C.c["mm"] = mm0 + 1
    C.c["vv"] = vv0 + 1

    mm0 = C.c["mm"]; vv0 = C.c["vv"]
    with nc.Block() as blk:
        @blk.vector
        def _(v):
            v.wait_ge(C.sem("mm"), mm0)
            v.tensor_copy(mcols32[:, :], psb[:, 256:288])
            v.drain()
            if layer == 0:
                v.tensor_scalar_mul(biast[:, 0:32], mcols32[:, :], -1.0
                                    ).then_inc(C.sem("vv"), 1)
            else:
                slopek = C.t["slopek"]
                vvi = None
                for hh in range(8):
                    h = 8 * half + hh
                    for qb in range(NQB):
                        vvi = v.tensor_scalar(
                            biast[:, 64 * hh + 16 * qb:64 * hh + 16 * (qb + 1)],
                            slopek[:, 16 * h:16 * (h + 1)],
                            mcols32[:, 4 * hh + qb:4 * hh + qb + 1], None,
                            op0=ALU.subtract)
                vvi.then_inc(C.sem("vv"), 1)
    C.c["vv"] = vv0 + 1

    iters = []
    for hh in range(8):
        for qb in range(NQB):
            kcs = _kcs_phase2(direction(hh), qb)
            geoms = [_tile_geom(direction(hh), qb, kc) for kc in kcs]
            iters.append((hh, qb, kcs, geoms))
    idx_last = {}
    for ii, (hh, qb, kcs, geoms) in enumerate(iters):
        idx_last[hh] = ii
    mm0 = C.c["mm"]; vv0 = C.c["vv"]; ev0 = C.c["ev"]; st0 = C.c["st"]
    ld0 = C.c["ld"]
    vtile = []
    viter = []
    stile = []
    siter = []
    vc = 0; sc = 0
    ntl = 0
    lrec_s = []
    iter_last = []
    for (hh, qb, kcs, geoms) in iters:
        for (lo, hi, tri0) in geoms:
            if tri0 is not None:
                vc += 1
            if (lo, hi) != (0, 512):
                vc += 1
            vtile.append(vc)
            sc += 1
            stile.append(sc)
            ntl += 1
        iter_last.append(ntl - 1)
        vc += 1
        viter.append(vc)
        sc += 1
        lrec_s.append(sc)
        sc += 1
        siter.append(sc)
    LOOK0 = 3
    mm_st = [0] * ntl
    mm_pv = [0] * ntl
    mm_bc = [0] * len(iters)
    last_set = set(iter_last)
    last_to_iter = {tgl: ii for ii, tgl in enumerate(iter_last)}
    mc = 0
    for tg in range(min(LOOK0, ntl)):
        mc += 1
        mm_st[tg] = mc
    for tg in range(ntl):
        if tg + LOOK0 < ntl:
            mc += 1
            mm_st[tg + LOOK0] = mc
        mc += 1
        mm_pv[tg] = mc
        if tg in last_set:
            mc += 1
            mm_bc[last_to_iter[tg]] = mc

    with nc.Block() as blk:
        @blk.sync
        def _(sync):
            if layer == 1:
                k = 0
                seen = set()
                for ii, (hh, qb, kcs, geoms) in enumerate(iters):
                    if hh not in seen:
                        seen.add(hh)
                        h = 8 * half + hh
                        if hh >= 2:
                            sync.wait_ge(C.sem("ev"), ev0 + siter[idx_last[hh - 2]])
                        sync.dma_start(out=(qbra if hh % 2 == 0 else qbrb)[0:1, :],
                                       in_=C.t["c_qbias_bf"][h:h + 1, :]
                                       ).then_inc(C.sem("ld"), 16)
                        k += 1
                        sync.wait_ge(C.sem("ld"), ld0 + 16 * k)

        tiles = []
        for ii, (hh, qb, kcs, geoms) in enumerate(iters):
            for i, kc in enumerate(kcs):
                tiles.append((ii, hh, qb, i, kc, len(kcs)))
        LOOK = 3

        @blk.tensor
        def _(t):
            def emit_st(tg):
                ii, hh, qb, i, kc, nk = tiles[tg]
                ch, r0 = hh // 2, 64 * (hh % 2)
                qbr = qbra if hh % 2 == 0 else qbrb
                if layer == 1 and qb == 0 and i == 0:
                    t.wait_ge(C.sem("ld"), ld0 + 16 * (hh + 1))
                par = tg % 3
                if tg >= 3:
                    t.wait_ge(C.sem("ev"), ev0 + stile[tg - 3])
                bank = psa[:, 512 * par:512 * (par + 1)]
                mmi = t.matmul(
                    bank,
                    kh[r0:r0 + 64, 2048 * ch + 128 * kc:2048 * ch + 128 * (kc + 1)],
                    qh[r0:r0 + 64, 2048 * ch + 512 * qb:2048 * ch + 512 * (qb + 1)],
                    start=True, stop=(layer == 0))
                if layer == 1:
                    mmi = t.matmul(bank, ones1b[0:1, :],
                                   qbr[0:1, 512 * qb:512 * (qb + 1)],
                                   start=False, stop=True)
                mmi.then_inc(C.sem("mm"), 1)

            def emit_pv(tg):
                ii, hh, qb, i, kc, nk = tiles[tg]
                par = tg % 3
                t.wait_ge(C.sem("ev"), ev0 + stile[tg])
                if ii >= 2 and i == 0:
                    t.wait_ge(C.sem("vv"), vv0 + viter[ii - 2])
                t.matmul(psb[0:65, 512 * (ii % 2):512 * (ii % 2) + 512],
                         vh[:, 520 * kc + 65 * hh:520 * kc + 65 * (hh + 1)],
                         pt4[:, 512 * par:512 * (par + 1)],
                         start=(i == 0), stop=(i == nk - 1)
                         ).then_inc(C.sem("mm"), 1)
                if i == nk - 1:
                    t.wait_ge(C.sem("ev"), ev0 + lrec_s[ii])
                    t.matmul(psa[0:64, 2048 - 512:2048], ones1f[0:1, 0:64],
                             lrec[0:1, :], start=True, stop=True
                             ).then_inc(C.sem("mm"), 1)

            for tg in range(min(LOOK, len(tiles))):
                emit_st(tg)
            for tg in range(len(tiles)):
                if tg + LOOK < len(tiles):
                    emit_st(tg + LOOK)
                emit_pv(tg)

        @blk.vector
        def _(v):
            tg = 0
            for ii, (hh, qb, kcs, geoms) in enumerate(iters):
                ch, r0 = hh // 2, 64 * (hh % 2)
                tri = C.t["tri_fwd"] if direction(hh) == "fwd" else C.t["tri_bwd"]
                for i, kc in enumerate(kcs):
                    par = tg % 3
                    lo, hi, tri0 = geoms[i]
                    if tri0 is not None:
                        v.wait_ge(C.sem("mm"), mm0 + mm_st[tg])
                        v.tensor_tensor(
                            psa[:, 512 * par + tri0:512 * par + tri0 + 128],
                            psa[:, 512 * par + tri0:512 * par + tri0 + 128],
                            tri[:, :], op=ALU.add).then_inc(C.sem("vv"), 1)
                    if (lo, hi) != (0, 512):
                        dead = (0, lo) if direction(hh) == "fwd" else (hi, 512)
                        if tg >= 3:
                            v.wait_ge(C.sem("mm"), mm0 + mm_pv[tg - 3])
                        v.memset(pt4[:, 512 * par + dead[0]:512 * par + dead[1]],
                                 0.0).then_inc(C.sem("vv"), 1)
                    tg += 1
                v.wait_ge(C.sem("ev"), ev0 + siter[ii])
                if hh % 2 == 0:
                    v.tensor_tensor(
                        oh[0:64, 2048 * ch + 512 * qb:2048 * ch + 512 * (qb + 1)],
                        psb[0:64, 512 * (ii % 2):512 * (ii % 2) + 512],
                        lb[:, :], op=ALU.mult).then_inc(C.sem("vv"), 1)
                else:
                    if ii >= 1:
                        v.wait_ge(C.sem("st"), st0 + 16 * _nodd(iters, ii))
                    v.tensor_tensor(
                        C.t["OTMP"], psb[0:64, 512 * (ii % 2):512 * (ii % 2) + 512],
                        lb[:, :], op=ALU.mult).then_inc(C.sem("vv"), 1)

        @blk.scalar
        def _(s):
            tg = 0
            nodd = 0
            for ii, (hh, qb, kcs, geoms) in enumerate(iters):
                ch = hh // 2
                for i, kc in enumerate(kcs):
                    par = tg % 3
                    lo, hi, tri0 = geoms[i]
                    s.wait_ge(C.sem("mm"), mm0 + mm_st[tg])
                    s.wait_ge(C.sem("vv"), vv0 + vtile[tg])
                    if tg >= 3:
                        s.wait_ge(C.sem("mm"), mm0 + mm_pv[tg - 3])
                    bias = biast[:, 4 * hh + qb:4 * hh + qb + 1] if layer == 0 \
                        else biast[:, 64 * hh + 16 * qb + kc:64 * hh + 16 * qb + kc + 1]
                    s.activation(out=pt4[:, 512 * par + lo:512 * par + hi],
                                 in_=psa[:, 512 * par + lo:512 * par + hi],
                                 func=AF.Exp, bias=bias).then_inc(C.sem("ev"), 1)
                    tg += 1
                s.wait_ge(C.sem("mm"), mm0 + mm_pv[tg - 1])
                s.drain()
                s.activation(out=lrec[0:1, :],
                             in_=psb[64:65, 512 * (ii % 2):512 * (ii % 2) + 512],
                             func=AF.Ln)
                s.drain()
                s.activation(out=lrec[0:1, :], in_=lrec[0:1, :], func=AF.Exp,
                             scale=-1.0).then_inc(C.sem("ev"), 1)
                s.wait_ge(C.sem("mm"), mm0 + mm_bc[ii])
                if ii >= 1:
                    s.wait_ge(C.sem("vv"), vv0 + viter[ii - 1])
                s.copy(lb[:, :], psa[0:64, 2048 - 512:2048]).then_inc(C.sem("ev"), 1)
                if hh % 2 == 1:
                    s.wait_ge(C.sem("vv"), vv0 + viter[ii])
                    s.dma_start(
                        out=oh[64:128, 2048 * ch + 512 * qb:2048 * ch + 512 * (qb + 1)],
                        in_=C.t["OTMP"]).then_inc(C.sem("st"), 16)
                    nodd += 1
    C.c["mm"] = mm0 + mm_bc[-1]
    C.c["vv"] = vv0 + viter[-1]
    C.c["ev"] = ev0 + siter[-1]
    C.c["st"] = st0 + 16 * sum(1 for (hh, qb, k, g) in iters if hh % 2 == 1)
    if layer == 1:
        C.c["ld"] = ld0 + 16 * 8


def _nodd(iters, ii):
    return sum(1 for (hh, qb, k, g) in iters[:ii] if hh % 2 == 1)


def _bfold(C, half, wso, wo_gd_need, bvb, bfc, bfold, bfold2):
    nc = C.nc
    psa = C.t["PSA"]
    mm0 = C.c["mm"]
    with nc.Block() as blk:
        @blk.tensor
        def _(t):
            t.wait_ge(C.sem("gd"), wo_gd_need)
            for mc in range(8):
                for rc in range(4):
                    mmi = t.matmul(psa[:, 1024 + mc:1024 + mc + 1],
                                   wso[:, 1024 * rc + 128 * mc:1024 * rc + 128 * (mc + 1)],
                                   bvb[:, 4 * half + rc:4 * half + rc + 1],
                                   start=(rc == 0), stop=(rc == 3))
            mmi.then_inc(C.sem("mm"), 1)

        @blk.vector
        def _(v):
            v.wait_ge(C.sem("mm"), mm0 + 1)
            if half == 0:
                v.tensor_tensor(bfold[:, :], psa[:, 1024:1032], bfc[:, :], op=ALU.add)
            else:
                v.tensor_copy(bfold2[:, :], psa[:, 1024:1032])
    C.c["mm"] = mm0 + 1


def _wo_proj(C, layer, half, wso, wo_gd_need, oh, scr, bfold, bfold2):
    nc = C.nc
    psa = C.t["PSA"]
    mm0 = C.c["mm"]; vv0 = C.c["vv"]; ev0 = C.c["ev"]
    use_scalar = (layer == 0 and half == 0)
    st_need = C.c["st"]
    with nc.Block() as blk:
        @blk.tensor
        def _(t):
            t.wait_ge(C.sem("gd"), wo_gd_need)
            t.wait_ge(C.sem("st"), st_need)
            idx = 0
            for qq in range(4):
                for mc in range(8):
                    if idx >= 2:
                        if use_scalar:
                            t.wait_ge(C.sem("ev"), ev0 + idx - 1)
                        else:
                            t.wait_ge(C.sem("vv"), vv0 + idx - 1)
                    for rc in range(4):
                        mmi = t.matmul(
                            psa[:, 512 * (idx % 2):512 * (idx % 2 + 1)],
                            wso[:, 1024 * rc + 128 * mc:1024 * rc + 128 * (mc + 1)],
                            oh[:, 2048 * rc + 512 * qq:2048 * rc + 512 * (qq + 1)],
                            start=(rc == 0), stop=(rc == 3))
                    mmi.then_inc(C.sem("mm"), 1)
                    idx += 1

        if use_scalar:
            @blk.scalar
            def _(s):
                idx = 0
                for qq in range(4):
                    for mc in range(8):
                        s.wait_ge(C.sem("mm"), mm0 + idx + 1)
                        s.copy(scr[:, 2048 * mc + 512 * qq:2048 * mc + 512 * (qq + 1)],
                               psa[:, 512 * (idx % 2):512 * (idx % 2 + 1)]
                               ).then_inc(C.sem("ev"), 1)
                        idx += 1
        else:
            @blk.vector
            def _(v):
                idx = 0
                for qq in range(4):
                    for mc in range(8):
                        v.wait_ge(C.sem("mm"), mm0 + idx + 1)
                        dst = scr[:, 2048 * mc + 512 * qq:2048 * mc + 512 * (qq + 1)]
                        src = psa[:, 512 * (idx % 2):512 * (idx % 2 + 1)]
                        if half == 0:
                            v.tensor_scalar(dst, src,
                                            bfold[:, mc:mc + 1], None, op0=ALU.add
                                            ).then_inc(C.sem("vv"), 1)
                        elif layer == 1:
                            v.tensor_tensor(dst, src, dst, op=ALU.add)
                            v.drain()
                            v.tensor_scalar(dst, dst, bfold2[:, mc:mc + 1],
                                            None, op0=ALU.add
                                            ).then_inc(C.sem("vv"), 1)
                        else:
                            v.tensor_tensor(dst, src, dst, op=ALU.add
                                            ).then_inc(C.sem("vv"), 1)
                        idx += 1
    C.c["mm"] = mm0 + 32
    if use_scalar:
        C.c["ev"] = ev0 + 32
    else:
        C.c["vv"] = vv0 + 32



_DRAM_SPECS = [
    ("b0_wq", [D, D], F32), ("b0_wk", [D, D], F32), ("b0_wv", [D, D], F32),
    ("b0_wo", [D, D], F32),
    ("b1_wgT", [D, 8], F32), ("b1_eb", [1, 8], F32),
    ("b1_sw1T", [D, D], F32), ("b1_sw3T", [D, D], F32), ("b1_sw2T", [D, D], F32),
    ("b1_w1", [8 * 1024, 512], F32), ("b1_w3", [8 * 1024, 512], F32),
    ("b1_w2", [8 * 512, 1024], F32),
    ("b2_wq", [D, D], F32), ("b2_wk", [D, D], F32), ("b2_wv", [D, D], F32),
    ("b2_wf", [D, D], F32),
    ("b2_bq", [128, 8], F32), ("b2_bk", [128, 8], F32), ("b2_bv", [128, 8], F32),
    ("b2_bf", [128, 8], F32),
    ("b3_gwT", [D, 8], F32),
    ("b3_wegT", [8 * 1024, 1024], F32), ("b3_weuT", [8 * 1024, 1024], F32),
    ("b3_wedT", [8 * 1024, 1024], F32),
    ("b3_wsgT", [D, 2048], F32), ("b3_wsuT", [D, 2048], F32),
    ("b3_wsdT", [2048, D], F32),
    ("c_id32", [128, 128], F32), ("c_idbf", [128, 128], BF16),
    ("c_tri_fwd", [128, 128], F32), ("c_tri_bwd", [128, 128], F32),
    ("c_cos", [128, 512], F32), ("c_sin", [128, 512], F32),
    ("c_slopek", [128, HEADS * 16], F32),
    ("c_ustrict", [128, 128], F32), ("c_u16", [16, 16], F32),
    ("c_qbias", [HEADS, T], F32),
    ("c_qbias_bf", [HEADS, T], BF16),
    ("c_pos", [128, 32], F32), ("c_maskcol", [8, 1], F32),
    ("c_coreflag", [128, 1], F32),
]

_SB_CONSTS = [
    ("id32", "c_id32", [128, 128], F32),
    ("idbf", "c_idbf", [128, 128], BF16),
    ("tri_fwd", "c_tri_fwd", [128, 128], F32),
    ("tri_bwd", "c_tri_bwd", [128, 128], F32),
    ("cos", "c_cos", [128, 512], F32),
    ("sin", "c_sin", [128, 512], F32),
    ("slopek", "c_slopek", [128, HEADS * 16], F32),
    ("ustrict", "c_ustrict", [128, 128], F32),
    ("u16", "c_u16", [16, 16], F32),
    ("pos", "c_pos", [128, 32], F32),
    ("maskcol", "c_maskcol", [8, 1], F32),
    ("coreflag", "c_coreflag", [128, 1], F32),
    ("ebrow", "b1_eb", [1, 8], F32),
]


def _emit_init(C):
    nc = C.nc
    C.sb("XN", [128, 16384], BF16)
    C.sb("SCR16", [128, 16384], BF16)
    C.sb("XST", [128, 2048], F32)
    C.sb("TMP1", [128, 2048], F32)
    C.sb("STT", [128, 64], F32)
    C.sb("MEAN", [128, 16], F32)
    C.sb("VAR", [128, 16], F32)
    C.sb("RSTD", [128, 16], F32)
    C.sb("BCOL", [128, 16], F32)
    C.sb("ones1f", [1, 128], F32)
    C.sb("ones128f", [128, 1], F32)
    C.sb("ones128b", [128, 1], BF16)
    C.sb("ones1b", [1, 128], BF16)
    for nm, dn, shape, dt in _SB_CONSTS:
        C.sb(nm, shape, dt)
    C.ps("PSA", [128, 2048], F32)
    C.ps("PSB", [128, 2048], F32)
    ld0 = C.c.get("ld", 0)
    n = len(_SB_CONSTS)
    with nc.Block() as blk:
        @blk.sync
        def _(sync):
            for nm, dn, shape, dt in _SB_CONSTS:
                sync.dma_start(out=C.t[nm][:, :], in_=C.t[dn][:, :]
                               ).then_inc(C.sem("ld"), 16)
            sync.wait_ge(C.sem("ld"), ld0 + 16 * n)

        @blk.gpsimd
        def _(g):
            g.memset(C.t["ones1f"][:, :], 1.0)
            g.memset(C.t["ones128f"][:, :], 1.0)
            g.memset(C.t["ones128b"][:, :], 1.0)
            g.memset(C.t["ones1b"][:, :], 1.0)
    C.c["ld"] = ld0 + 16 * n


def _emit_final_wait(C):
    nc = C.nc
    with nc.Block() as blk:
        @blk.sync
        def _(sync):
            sync.wait_ge(C.sem("st"), C.c["st"])
            sync.wait_ge(C.sem("ld"), C.c["ld"])
            sync.wait_ge(C.sem("gd"), C.c["gd"])
            sync.wait_ge(C.sem("sta"), C.c["sta"])
            sync.wait_ge(C.sem("stb"), C.c["stb"])
            sync.wait_ge(C.sem("lda"), C.c["lda"])
            sync.wait_ge(C.sem("ldb"), C.c["ldb"])
            if "swl" in C.c:
                sync.wait_ge(C.sem("swl"), C.c["swl"])


def _build_program(blocks):
    nc = bass.Bass("TRN2", target_bir_lowering=False)
    dram = {}
    for nm, shape, dt in _DRAM_SPECS:
        dram[nm] = nc.dram_tensor(nm, shape, dt, kind="ExternalInput")
    x_in = nc.dram_tensor("x_in", [D, T], F32, kind="ExternalInput")
    xi = [x_in]
    for i in range(len(blocks) - 1):
        xi.append(nc.dram_tensor(f"xmid{i}", [D, T], F32))
    last_tm = (blocks[-1] == "f1")
    if last_tm:
        x_out = nc.dram_tensor("x_out", [T, D], F32, kind="ExternalOutput")
    else:
        x_out = nc.dram_tensor("x_out", [D, T], F32, kind="ExternalOutput")
    xi.append(x_out)

    with ExitStack() as stack:
        C = Ctx(nc, stack)
        for nm in dram:
            C.t[nm] = dram[nm]
        for s in ("ld", "gd", "st", "mm", "ev", "vv", "gg", "cc", "lda", "ldb", "sta", "stb"):
            C.sem(s)
        _emit_init(C)
        for i, b in enumerate(blocks):
            fin = i > 0
            fout = i < len(blocks) - 1
            with ExitStack() as bs:
                if b == "a0":
                    emit_attn(C, 0, xi[i], xi[i + 1], bs,
                              fold_in=fin, fold_out=fout)
                elif b == "a1":
                    emit_attn(C, 1, xi[i], xi[i + 1], bs,
                              fold_in=fin, fold_out=fout)
                elif b == "f0":
                    emit_ffn0(C, xi[i], xi[i + 1], bs,
                              fold_in=fin, fold_out=fout)
                elif b == "f1":
                    emit_ffn1(C, xi[i], xi[i + 1], bs, fold_in=fin)
        if _DEBUG_DUMP:
            dbg = {}
            for nm in _DEBUG_DUMP:
                t_ = C.t[nm]
                dbg[nm] = nc.dram_tensor("dbg_" + nm, list(t_.shape),
                                         t_.dtype, kind="ExternalOutput")
            st0 = C.c["st"]
            with nc.Block() as blk:
                @blk.sync
                def _(sync):
                    for nm in _DEBUG_DUMP:
                        sync.dma_start(out=dbg[nm].ap(), in_=C.t[nm][:, :]
                                       ).then_inc(C.sem("st"), 16)
                    sync.wait_ge(C.sem("st"), st0 + 16 * len(_DEBUG_DUMP))
            C.c["st"] = st0 + 16 * len(_DEBUG_DUMP)
        _emit_final_wait(C)
    return nc


_BLOCK_ORDER = ["a0", "f0", "a1", "f1"]


def kernel(**inputs):
    maps = _host_inputs(inputs)
    if _MODE == "fused":
        nc = _build_program(_BLOCK_ORDER)
        res = run_bass_kernel_spmd(nc, maps, core_ids=list(range(NCORES)))
        return np.stack([np.asarray(res.results[c]["x_out"], np.float32)
                         for c in range(NCORES)], axis=0)
    cur = [m["x_in"] for m in maps]
    for b in _BLOCK_ORDER:
        nc = _build_program([b])
        in_maps = []
        for c in range(NCORES):
            m = dict(maps[c])
            m["x_in"] = np.ascontiguousarray(cur[c])
            in_maps.append(m)
        res = run_bass_kernel_spmd(nc, in_maps, core_ids=list(range(NCORES)))
        cur = [np.asarray(res.results[c]["x_out"], np.float32)
               for c in range(NCORES)]
    return np.stack(cur, axis=0)




def _emit_gate_softmax(C, bs, wgT_name, with_groups):
    nc = C.nc
    xn = C.t["XN"]
    psa, psb = C.t["PSA"], C.t["PSB"]
    id32 = C.t["id32"]
    ones1f = C.t["ones1f"]
    wgt = C.sb("WGT", [128, 64], BF16, bs)
    lg8 = C.t["TMP1"]
    s0 = C.sb("S0T", [128, 128], F32, bs)
    sb_ = C.sb("SBT", [128, 128], F32, bs)
    e1 = C.sb("E1T", [128, 128], F32, bs)
    e2 = C.sb("E2T", [128, 128], F32, bs)
    sm = C.sb("SMT", [128, 128], F32, bs)
    gsc = C.sb("GSC", [128, 64], F32, bs)
    sml = C.sb("SML", [128, 96], F32, bs)
    ebb = C.sb("EBB", [128, 8], F32, bs)

    def smlb(col, inner):
        return bass.AP(sml, col, [[96, 128], [1, 16], [0, inner]])

    def t3(ap_):
        return ap_.rearrange("p (t e) -> p t e", e=8)

    gd0 = C.c["gd"]
    with nc.Block() as blk:
        @blk.gpsimd
        def _(g):
            g.dma_start(out=wgt[:, :],
                        in_=_wview(C.t[wgT_name], 8, 8)
                        ).then_inc(C.sem("gd"), 16)
    C.c["gd"] = gd0 + 16

    mm0 = C.c["mm"]; ev0 = C.c["ev"]
    with nc.Block() as blk:
        @blk.tensor
        def _(t):
            t.wait_ge(C.sem("gd"), C.c["gd"])
            for dc in range(NDC):
                for s4 in range(4):
                    mmi = t.matmul(psa[0:8, 512 * s4:512 * (s4 + 1)],
                                   wgt[:, 8 * dc:8 * (dc + 1)],
                                   xn[:, 2048 * dc + 512 * s4:2048 * dc + 512 * (s4 + 1)],
                                   start=(dc == 0), stop=(dc == NDC - 1))
            mmi.then_inc(C.sem("mm"), 1)
            t.matmul(psb[:, 128:136], ones1f[:, :], C.t["ebrow"][0:1, :],
                     start=True, stop=True).then_inc(C.sem("mm"), 1)

        @blk.scalar
        def _(s):
            s.wait_ge(C.sem("mm"), mm0 + 1)
            s.copy(lg8[0:8, 0:2048], psa[0:8, 0:2048]).then_inc(C.sem("ev"), 1)
    C.c["mm"] = mm0 + 2
    C.c["ev"] = ev0 + 1

    mm0 = C.c["mm"]; vv0 = C.c["vv"]
    with nc.Block() as blk:
        @blk.tensor
        def _(t):
            tri_ = None
            for tl in range(NTC):
                tri_ = t.transpose(psb[:, 8 * tl:8 * (tl + 1)],
                                   lg8[0:8, 128 * tl:128 * (tl + 1)], id32[0:8, 0:8])
            tri_.then_inc(C.sem("mm"), 1)

        @blk.vector
        def _(v):
            v.wait_ge(C.sem("mm"), mm0 + 1)
            v.tensor_copy(s0[:, :], psb[:, 0:128])
            v.tensor_copy(ebb[:, :], psb[:, 128:136])
            v.drain()
            v.reduce_max(sml[:, 0:16], t3(s0[:, :]), axis=AX.X)
            v.drain()
            v.tensor_tensor(sb_[:, :], s0[:, :], smlb(0, 8), op=ALU.subtract
                            ).then_inc(C.sem("vv"), 1)
    C.c["mm"] = mm0 + 1
    C.c["vv"] = vv0 + 1

    vv0 = C.c["vv"]; ev0 = C.c["ev"]
    with nc.Block() as blk:
        @blk.scalar
        def _(s):
            s.activation(out=s0[:, :], in_=sb_[:, :], func=AF.Exp
                         ).then_inc(C.sem("ev"), 1)
    C.c["ev"] = ev0 + 1

    vv0 = C.c["vv"]
    with nc.Block() as blk:
        @blk.vector
        def _(v):
            v.reduce_sum(sml[:, 16:32], t3(s0[:, :]), axis=AX.X)
            v.drain()
            v.reciprocal(sml[:, 16:32], sml[:, 16:32])
            v.drain()
            v.tensor_tensor(s0[:, :], s0[:, :], smlb(16, 8), op=ALU.mult)
            v.drain()
            if with_groups:
                v.tensor_tensor(sb_[:, :], s0[:, :],
                                bass.AP(ebb, 0, [[8, 128], [0, 16], [1, 8]]),
                                op=ALU.add)
                v.drain()
                v.tensor_tensor(gsc[:, :],
                                bass.AP(sb_, 0, [[128, 128], [8, 16], [2, 4]]),
                                bass.AP(sb_, 1, [[128, 128], [8, 16], [2, 4]]),
                                op=ALU.add)
                v.drain()
                g3 = gsc.ap().rearrange("p (t g) -> p t g", g=4)
                v.reduce_max(sml[:, 32:48], g3, axis=AX.X)
                v.drain()
                v.tensor_tensor(sm[:, 0:64], gsc[:, :], smlb(32, 4), op=ALU.is_equal)
                v.drain()
                v.scalar_tensor_tensor(gsc[:, :], sm[:, 0:64], -1e9, gsc[:, :],
                                       op0=ALU.mult, op1=ALU.add)
                v.drain()
                v.reduce_max(sml[:, 48:64], g3, axis=AX.X)
                v.drain()
                v.tensor_tensor(sm[:, 64:128], gsc[:, :], smlb(48, 4),
                                op=ALU.is_equal)
                v.drain()
                v.tensor_tensor(sm[:, 0:64], sm[:, 0:64], sm[:, 64:128], op=ALU.add)
                v.drain()
                v.tensor_scalar(sm[:, 0:64], sm[:, 0:64], 1e9, -1e9,
                                op0=ALU.mult, op1=ALU.add)
                v.drain()
                v.tensor_tensor(sb_[:, :], sb_[:, :],
                                bass.AP(sm, 0, [[128, 128], [4, 16], [1, 4], [0, 2]]),
                                op=ALU.add)
                v.drain()
            else:
                v.tensor_copy(sb_[:, :], s0[:, :])
                v.drain()
            v.reduce_max(sml[:, 32:48], t3(sb_[:, :]), axis=AX.X)
            v.drain()
            v.tensor_tensor(e1[:, :], sb_[:, :], smlb(32, 8), op=ALU.is_equal)
            v.drain()
            v.tensor_tensor(sm[:, 0:128], e1[:, :], s0[:, :], op=ALU.mult)
            v.drain()
            v.reduce_sum(sml[:, 64:80], t3(sm[:, 0:128]), axis=AX.X)
            v.scalar_tensor_tensor(sb_[:, :], e1[:, :], -1e9, sb_[:, :],
                                   op0=ALU.mult, op1=ALU.add)
            v.drain()
            v.reduce_max(sml[:, 48:64], t3(sb_[:, :]), axis=AX.X)
            v.drain()
            v.tensor_tensor(e2[:, :], sb_[:, :], smlb(48, 8), op=ALU.is_equal)
            v.drain()
            v.tensor_tensor(sm[:, 0:128], e2[:, :], s0[:, :], op=ALU.mult)
            v.drain()
            v.reduce_sum(sml[:, 80:96], t3(sm[:, 0:128]), axis=AX.X
                         ).then_inc(C.sem("vv"), 1)
    C.c["vv"] = vv0 + 1
    return dict(e1=e1, e2=e2, w1=sml[:, 64:80], w2=sml[:, 80:96], s0=s0, sml=sml,
                sm=sm)


def emit_ffn0(C, x_in, x_out, bs, fold_in=False, fold_out=False):
    nc = C.nc
    xn = C.t["XN"]
    scr = C.t["SCR16"]
    psa, psb = C.t["PSA"], C.t["PSB"]
    id32, idbf = C.t["id32"], C.t["idbf"]
    ones1f, ones128f = C.t["ones1f"], C.t["ones128f"]
    tmp1 = C.t["TMP1"]
    pos = C.t["pos"]
    coreflag = C.t["coreflag"]

    agx_in = nc.dram_tensor("f0_agx_in", [128, 64], BF16)
    agx_out = nc.dram_tensor("f0_agx_out", [1024, 64], BF16)
    arc_in = nc.dram_tensor("f0_arc_in", [1, 8], F32)
    arc_out = nc.dram_tensor("f0_arc_out", [1, 8], F32)
    arp_in = nc.dram_tensor("f0_arp_in", [128, 64], F32)
    arp_out = nc.dram_tensor("f0_arp_out", [128, 64], F32)

    fxst2 = scr[:, 0:4096].bitcast(F32)
    fsq2 = scr[:, 4096:8192].bitcast(F32)
    if not fold_in:
        emit_ln_stats_from_dram(C, x_in, xst2=fxst2, sq2=fsq2)
    emit_ln_tail(C, T, 0, "pre", x_dram=x_in, out_bf=xn, xst2=fxst2, sq2=fsq2,
                 acp=scr[:, 8192:12288].bitcast(F32),
                 bcp=scr[:, 12288:16384].bitcast(F32),
                 stats_row=32 if fold_in else 0)

    pts = C.sb("PTS", [128, 64], F32, bs)
    ws1 = C.sb("WSH1", [128, 8192], BF16, bs)
    ws2 = C.sb("WSH2", [128, 8192], BF16, bs)
    ws3 = C.sb("WSH3", [128, 8192], BF16, bs)

    def load_sq(g, dram_name, slot):
        g.dma_start(out=slot[:, :],
                    in_=_wview(C.t[dram_name], 8, 1024)
                    ).then_inc(C.sem("swl"), 16)
        C.c["swl"] = C.c.get("swl", 0) + 16

    with nc.Block() as blk:
        @blk.gpsimd
        def _(g):
            load_sq(g, "b1_sw1T", ws1)
            load_sq(g, "b1_sw3T", ws2)
            load_sq(g, "b1_sw2T", ws3)
    sw_need = C.c["swl"]

    gd0 = C.c["gd"]; cc0 = C.c["cc"]
    with nc.Block() as blk:
        @blk.gpsimd
        def _(g):
            g.dma_start(out=agx_in[:, :],
                        in_=xn.ap().rearrange("p (c t) -> p c t", t=2048)[:, :, 0:8]
                        ).then_inc(C.sem("gd"), 16)
            g.wait_ge(C.sem("gd"), gd0 + 16)
            g.collective_compute(
                "AllGather", ALU.bypass, replica_groups=[list(range(NCORES))],
                ins=[agx_in.ap().opt()], outs=[agx_out.ap().opt()],
            ).then_inc(C.sem("cc"), 1)
    C.c["gd"] = gd0 + 16
    C.c["cc"] = cc0 + 1

    with ExitStack() as bs2:
        R = _emit_gate_softmax(C, bs2, "b1_wgT", with_groups=True)
        e1, e2, w1c, w2c = R["e1"], R["e2"], R["w1"], R["w2"]
        mbuf = C.sb("MBUF", [128, 2048], BF16, bs2)
        mwt = C.sb("MWT", [64, 4096], BF16, bs2)
        afl = C.sb("AFL", [64, 512], BF16, bs2)
        bfl = C.sb("BFL", [64, 512], BF16, bs2)
        ht = C.sb("HT", [128, 256], BF16, bs2)
        xfh = C.sb("XFH", [128, 64], BF16, bs2)
        ofs = C.sb("OFS", [128, 16], F32, bs2)
        cl8 = C.sb("CL8", [8, 1], F32, bs2)
        crow = C.sb("CROW", [1, 16], F32, bs2)
        hsb = C.sb("HSB", [64, 512], BF16, bs2)
        asb = C.sb("ASB", [8, 1024], BF16, bs2)
        mwsc = C.sb("MWSC", [128, 64], BF16, bs2)
        wss1 = C.sb("WSS1", [128, 4096], BF16, bs2)
        wss2 = C.sb("WSS2", [128, 4096], BF16, bs2)
        wss3 = C.sb("WSS3", [128, 4096], BF16, bs2)
        wss4 = C.sb("WSS4", [128, 4096], BF16, bs2)

        mm0 = C.c["mm"]; ev0 = C.c["ev"]
        with nc.Block() as blk:
            @blk.tensor
            def _(t):
                for tl in range(NTC):
                    t.matmul(psa[0:8, 0:1], e1[:, 8 * tl:8 * (tl + 1)],
                             ones128f[:, :], start=(tl == 0), stop=False)
                    mmi = t.matmul(psa[0:8, 0:1], e2[:, 8 * tl:8 * (tl + 1)],
                                   ones128f[:, :], start=False,
                                   stop=(tl == NTC - 1))
                mmi.then_inc(C.sem("mm"), 1)

            @blk.scalar
            def _(s):
                s.wait_ge(C.sem("mm"), mm0 + 1)
                s.copy(cl8[:, :], psa[0:8, 0:1]).then_inc(C.sem("ev"), 1)
        C.c["mm"] = mm0 + 1
        C.c["ev"] = ev0 + 1

        gd0 = C.c["gd"]; cc0 = C.c["cc"]
        with nc.Block() as blk:
            @blk.gpsimd
            def _(g):
                g.dma_start(out=bass.AP(arc_in, 0, [[1, 8], [1, 1]]),
                            in_=cl8[:, :]).then_inc(C.sem("gd"), 16)
                g.wait_ge(C.sem("gd"), gd0 + 16)
                g.collective_compute(
                    "AllReduce", ALU.add, replica_groups=[list(range(NCORES))],
                    ins=[arc_in.ap().opt()], outs=[arc_out.ap().opt()],
                ).then_inc(C.sem("cc"), 1)
                g.wait_ge(C.sem("cc"), cc0 + 1)
                g.dma_start(out=crow[:, 0:8], in_=arc_out[:, :]).then_inc(C.sem("gd"), 16)
                g.dma_start(out=xfh[:, :], in_=agx_out[0:128, :]).then_inc(C.sem("gd"), 16)
                g.wait_ge(C.sem("gd"), gd0 + 48)
        C.c["gd"] = gd0 + 48
        C.c["cc"] = cc0 + 1

        vv0 = C.c["vv"]; mm0 = C.c["mm"]
        with nc.Block() as blk:
            @blk.vector
            def _(v):
                v.tensor_tensor_scan(crow[0:1, 8:16],
                                     bass.AP(ones1f, 0, [[128, 1], [1, 8]]),
                                     crow[0:1, 0:8], 0.0,
                                     op0=ALU.mult, op1=ALU.add)
                v.drain()
                v.tensor_tensor(crow[0:1, 0:8], crow[0:1, 8:16], crow[0:1, 0:8],
                                op=ALU.subtract).then_inc(C.sem("vv"), 1)

            @blk.tensor
            def _(t):
                t.wait_ge(C.sem("vv"), vv0 + 1)
                t.matmul(psb[:, 256:264], ones1f[:, :], crow[0:1, 0:8],
                         start=True, stop=True)
                t.matmul(psb[:, 264:272], ones1f[:, :], crow[0:1, 8:16],
                         start=True, stop=True).then_inc(C.sem("mm"), 1)
        C.c["vv"] = vv0 + 1
        C.c["mm"] = mm0 + 1
        vv0 = C.c["vv"]; mm0 = C.c["mm"]
        with nc.Block() as blk:
            @blk.vector
            def _(v):
                v.tensor_copy(ofs[:, :], psb[:, 256:272]).then_inc(C.sem("vv"), 1)
        C.c["vv"] = vv0 + 1

        gd0 = C.c["gd"]
        mm0 = C.c["mm"]; ev0 = C.c["ev"]; st0 = C.c["st"]
        with nc.Block() as blk:
            @blk.gpsimd
            def _(g):
                for sseg in range(8):
                    s1 = wss1 if sseg % 2 == 0 else wss3
                    s2 = wss2 if sseg % 2 == 0 else wss4
                    if sseg >= 2:
                        g.wait_ge(C.sem("mm"), mm0 + sseg - 1)
                    g.dma_start(out=s1[:, :],
                                in_=_wview(C.t["b1_w1"], 8, 512, row0=1024 * sseg)
                                ).then_inc(C.sem("gd"), 16)
                    g.dma_start(out=s2[:, :],
                                in_=_wview(C.t["b1_w3"], 8, 512, row0=1024 * sseg)
                                ).then_inc(C.sem("gd"), 16)
                    g.wait_ge(C.sem("gd"), gd0 + 32 * (sseg + 1))

            @blk.tensor
            def _(t):
                for sseg in range(8):
                    s1 = wss1 if sseg % 2 == 0 else wss3
                    s2 = wss2 if sseg % 2 == 0 else wss4
                    t.wait_ge(C.sem("gd"), gd0 + 32 * (sseg + 1))
                    for dc in range(NDC):
                        t.matmul(psa[0:8, 0:512], xfh[:, 8 * dc:8 * (dc + 1)],
                                 s1[:, 512 * dc:512 * (dc + 1)],
                                 start=(dc == 0), stop=(dc == NDC - 1))
                    for dc in range(NDC):
                        mmi = t.matmul(psa[0:8, 512:1024], xfh[:, 8 * dc:8 * (dc + 1)],
                                       s2[:, 512 * dc:512 * (dc + 1)],
                                       start=(dc == 0), stop=(dc == NDC - 1))
                    mmi.then_inc(C.sem("mm"), 1)
                    t.wait_ge(C.sem("ev"), ev0 + sseg + 1)

            @blk.scalar
            def _(s):
                for sseg in range(8):
                    s.wait_ge(C.sem("mm"), mm0 + sseg + 1)
                    s.wait_ge(C.sem("st"), st0 + 32 * sseg)
                    s.copy(asb[:, 0:512], psa[0:8, 0:512])
                    s.copy(asb[:, 512:1024], psa[0:8, 512:1024]).then_inc(C.sem("ev"), 1)
                    s.drain()
                    s.dma_start(out=afl[8 * sseg:8 * (sseg + 1), :],
                                in_=asb[:, 0:512]).then_inc(C.sem("st"), 16)
                    s.dma_start(out=bfl[8 * sseg:8 * (sseg + 1), :],
                                in_=asb[:, 512:1024]).then_inc(C.sem("st"), 16)
        C.c["gd"] = gd0 + 32 * 8
        C.c["mm"] = mm0 + 8
        C.c["ev"] = ev0 + 8
        C.c["st"] = st0 + 32 * 8

        vv0 = C.c["vv"]; mm0 = C.c["mm"]; ev0 = C.c["ev"]; st0 = C.c["st"]
        with nc.Block() as blk:
            @blk.vector
            def _(v):
                for sc in range(32):
                    tl, k = sc // 2, sc % 2
                    ek = e1 if k == 0 else e2
                    wc = w1c if k == 0 else w2c
                    pc = pos[:, 2 * tl + k:2 * tl + k + 1]
                    v.tensor_scalar(R["sm"][:, 0:8], ofs[:, 0:8], pc, None,
                                    op0=ALU.is_le)
                    v.tensor_scalar(R["sm"][:, 8:16], ofs[:, 8:16], pc, None,
                                    op0=ALU.is_gt)
                    v.drain()
                    v.tensor_tensor(R["sm"][:, 0:8], R["sm"][:, 0:8],
                                    R["sm"][:, 8:16], op=ALU.mult)
                    v.drain()
                    v.tensor_tensor(mbuf[:, 64 * sc:64 * (sc + 1)],
                                    bass.AP(ek.tensor if hasattr(ek, "tensor") else ek,
                                            8 * tl, [[128, 128], [0, 8], [1, 8]]),
                                    bass.AP(R["sm"], 0, [[128, 128], [1, 8], [0, 8]]),
                                    op=ALU.mult)
                    v.drain()
                    if sc >= 1:
                        v.wait_ge(C.sem("mm"), mm0 + sc)
                    v.tensor_scalar(mwsc[:, :], mbuf[:, 64 * sc:64 * (sc + 1)],
                                    wc[:, tl:tl + 1], None, op0=ALU.mult
                                    ).then_inc(C.sem("vv"), 1)

            @blk.tensor
            def _(t):
                for sc in range(32):
                    t.wait_ge(C.sem("vv"), vv0 + sc + 1)
                    tri_ = t.transpose(psb[:, :].bitcast(BF16)[0:64, 3584:3712],
                                       mwsc[:, :], idbf[:, :])
                    tri_.then_inc(C.sem("mm"), 1)
                    t.wait_ge(C.sem("ev"), ev0 + sc + 1)

            @blk.scalar
            def _(s):
                for sc in range(32):
                    s.wait_ge(C.sem("mm"), mm0 + sc + 1)
                    s.copy(mwt[0:64, 128 * sc:128 * (sc + 1)],
                           psb[:, :].bitcast(BF16)[0:64, 3584:3712]
                           ).then_inc(C.sem("ev"), 1)
        C.c["vv"] = vv0 + 32
        C.c["mm"] = mm0 + 32
        C.c["ev"] = ev0 + 32

        mm0 = C.c["mm"]; ev0 = C.c["ev"]; vv0 = C.c["vv"]
        st_need = C.c["st"]
        with nc.Block() as blk:
            @blk.tensor
            def _(t):
                t.wait_ge(C.sem("st"), st_need)
                for sc in range(32):
                    par = sc % 2
                    if sc >= 2:
                        t.wait_ge(C.sem("vv"), vv0 + sc - 1)
                    t.matmul(psa[:, 1024 * par:1024 * par + 512],
                             mwt[:, 128 * sc:128 * (sc + 1)], afl[:, :],
                             start=True, stop=True).then_inc(C.sem("mm"), 1)
                    t.matmul(psa[:, 1024 * par + 512:1024 * par + 1024],
                             mwt[:, 128 * sc:128 * (sc + 1)], bfl[:, :],
                             start=True, stop=True).then_inc(C.sem("mm"), 1)
                    t.wait_ge(C.sem("vv"), vv0 + sc + 1)
                    t.matmul(psb[0:64, 0:512], mbuf[:, 64 * sc:64 * (sc + 1)],
                             scr[:, 512 * par:512 * (par + 1)],
                             start=(sc == 0), stop=(sc == 31)).then_inc(C.sem("mm"), 1)

            @blk.scalar
            def _(s):
                for sc in range(32):
                    par = sc % 2
                    s.wait_ge(C.sem("mm"), mm0 + 3 * sc + 1)
                    if sc >= 1:
                        s.wait_ge(C.sem("vv"), vv0 + sc)
                    s.activation(out=tmp1[:, 0:512], in_=psa[:, 1024 * par:1024 * par + 512],
                                 func=AF.Silu).then_inc(C.sem("ev"), 1)

            @blk.vector
            def _(v):
                for sc in range(32):
                    par = sc % 2
                    v.wait_ge(C.sem("ev"), ev0 + sc + 1)
                    v.wait_ge(C.sem("mm"), mm0 + 3 * sc + 2)
                    v.tensor_tensor(scr[:, 512 * par:512 * (par + 1)],
                                    psa[:, 1024 * par + 512:1024 * par + 1024],
                                    tmp1[:, 0:512], op=ALU.mult).then_inc(C.sem("vv"), 1)
        C.c["mm"] = mm0 + 3 * 32
        C.c["ev"] = ev0 + 32
        C.c["vv"] = vv0 + 32

        mm0 = C.c["mm"]; ev0 = C.c["ev"]; vv0 = C.c["vv"]
        with nc.Block() as blk:
            @blk.scalar
            def _(s):
                s.copy(hsb[:, :], psb[0:64, 0:512]).then_inc(C.sem("ev"), 1)

            @blk.tensor
            def _(t):
                t.wait_ge(C.sem("ev"), ev0 + 1)
                tri_ = None
                for hc in range(4):
                    tri_ = t.transpose(
                        psb[:, :].bitcast(BF16)[:, 1024 + 64 * hc:1024 + 64 * (hc + 1)],
                        hsb[:, 128 * hc:128 * (hc + 1)], idbf[0:64, 0:64])
                tri_.then_inc(C.sem("mm"), 1)

            @blk.vector
            def _(v):
                v.wait_ge(C.sem("mm"), mm0 + 1)
                v.tensor_copy(ht[:, :], psb[:, :].bitcast(BF16)[:, 1024:1280]
                              ).then_inc(C.sem("vv"), 1)
        C.c["ev"] = ev0 + 1
        C.c["mm"] = mm0 + 1
        C.c["vv"] = vv0 + 1

        gd0 = C.c["gd"]; mm0 = C.c["mm"]; ev0 = C.c["ev"]
        with nc.Block() as blk:
            @blk.gpsimd
            def _(g):
                for sseg in range(8):
                    if sseg > 0:
                        g.wait_ge(C.sem("mm"), mm0 + sseg)
                    g.dma_start(out=wss1[:, :],
                                in_=_wview(C.t["b1_w2"], 4, 1024, row0=512 * sseg)
                                ).then_inc(C.sem("gd"), 16)

            @blk.tensor
            def _(t):
                for sseg in range(8):
                    t.wait_ge(C.sem("gd"), gd0 + 16 * (sseg + 1))
                    for dm in range(8):
                        for hc in range(4):
                            mmi = t.matmul(
                                psb[:, 512 + 8 * dm:512 + 8 * (dm + 1)],
                                wss1[:, 1024 * hc + 128 * dm:1024 * hc + 128 * (dm + 1)],
                                ht[:, 64 * hc + 8 * sseg:64 * hc + 8 * (sseg + 1)],
                                start=(sseg == 0 and dm == 0 and hc == 0),
                                stop=(sseg == 7 and dm == 7 and hc == 3))
                    mmi.then_inc(C.sem("mm"), 1)

            @blk.scalar
            def _(s):
                s.wait_ge(C.sem("mm"), mm0 + 8)
                s.copy(pts[:, :], psb[:, 512:576]).then_inc(C.sem("ev"), 1)
        C.c["gd"] = gd0 + 16 * 8
        C.c["mm"] = mm0 + 8
        C.c["ev"] = ev0 + 1

        gd0 = C.c["gd"]; cc0 = C.c["cc"]
        with nc.Block() as blk:
            @blk.gpsimd
            def _(g):
                g.dma_start(out=arp_in[:, :], in_=pts[:, :]).then_inc(C.sem("gd"), 16)
                g.wait_ge(C.sem("gd"), gd0 + 16)
                g.collective_compute(
                    "AllReduce", ALU.add, replica_groups=[list(range(NCORES))],
                    ins=[arp_in.ap().opt()], outs=[arp_out.ap().opt()],
                ).then_inc(C.sem("cc"), 1)
                g.wait_ge(C.sem("cc"), cc0 + 1)
                g.dma_start(out=pts[:, :], in_=arp_out[:, :]).then_inc(C.sem("gd"), 16)
                g.wait_ge(C.sem("gd"), gd0 + 32)
        C.c["gd"] = gd0 + 32
        C.c["cc"] = cc0 + 1

    pts = C.t["PTS"]
    yh = C.sb("YH", [128, 8192], F32, bs)

    mm0 = C.c["mm"]; ev0 = C.c["ev"]; vv0 = C.c["vv"]
    with nc.Block() as blk:
        @blk.tensor
        def _(t):
            t.wait_ge(C.sem("swl"), sw_need)
            idx = 0
            for mc in range(8):
                for s4 in range(4):
                    if idx >= 2:
                        t.wait_ge(C.sem("vv"), vv0 + idx - 1)
                    for dc in range(NDC):
                        t.matmul(psa[:, 1024 * (idx % 2):1024 * (idx % 2) + 512],
                                 ws1[:, 1024 * dc + 128 * mc:1024 * dc + 128 * (mc + 1)],
                                 xn[:, 2048 * dc + 512 * s4:2048 * dc + 512 * (s4 + 1)],
                                 start=(dc == 0), stop=(dc == NDC - 1))
                    for dc in range(NDC):
                        mmi = t.matmul(
                            psa[:, 1024 * (idx % 2) + 512:1024 * (idx % 2) + 1024],
                            ws2[:, 1024 * dc + 128 * mc:1024 * dc + 128 * (mc + 1)],
                            xn[:, 2048 * dc + 512 * s4:2048 * dc + 512 * (s4 + 1)],
                            start=(dc == 0), stop=(dc == NDC - 1))
                    mmi.then_inc(C.sem("mm"), 1)
                    idx += 1

        @blk.scalar
        def _(s):
            idx = 0
            for mc in range(8):
                for s4 in range(4):
                    s.wait_ge(C.sem("mm"), mm0 + idx + 1)
                    if idx >= 1:
                        s.wait_ge(C.sem("vv"), vv0 + idx)
                    s.activation(out=tmp1[:, 0:512],
                                 in_=psa[:, 1024 * (idx % 2):1024 * (idx % 2) + 512],
                                 func=AF.Silu).then_inc(C.sem("ev"), 1)
                    idx += 1

        @blk.vector
        def _(v):
            idx = 0
            for mc in range(8):
                for s4 in range(4):
                    v.wait_ge(C.sem("ev"), ev0 + idx + 1)
                    v.tensor_tensor(scr[:, 2048 * mc + 512 * s4:2048 * mc + 512 * (s4 + 1)],
                                    psa[:, 1024 * (idx % 2) + 512:1024 * (idx % 2) + 1024],
                                    tmp1[:, 0:512], op=ALU.mult).then_inc(C.sem("vv"), 1)
                    idx += 1
    C.c["mm"] = mm0 + 32
    C.c["ev"] = ev0 + 32
    C.c["vv"] = vv0 + 32

    for th in range(2):
        mm0 = C.c["mm"]; vv0 = C.c["vv"]
        with nc.Block() as blk:
            @blk.tensor
            def _(t):
                idx = 0
                for mc in range(8):
                    for s2 in range(2):
                        if idx >= 2:
                            t.wait_ge(C.sem("vv"), vv0 + idx - 1)
                        for dc in range(NDC):
                            mmi = t.matmul(
                                psa[:, 512 * (idx % 2):512 * (idx % 2) + 512],
                                ws3[:, 1024 * dc + 128 * mc:1024 * dc + 128 * (mc + 1)],
                                scr[:, 2048 * dc + 1024 * th + 512 * s2:
                                    2048 * dc + 1024 * th + 512 * (s2 + 1)],
                                start=(dc == 0), stop=(dc == NDC - 1))
                        mmi.then_inc(C.sem("mm"), 1)
                        idx += 1

            @blk.vector
            def _(v):
                idx = 0
                for mc in range(8):
                    for s2 in range(2):
                        v.wait_ge(C.sem("mm"), mm0 + idx + 1)
                        v.tensor_copy(yh[:, 1024 * mc + 512 * s2:1024 * mc + 512 * (s2 + 1)],
                                      psa[:, 512 * (idx % 2):512 * (idx % 2) + 512]
                                      ).then_inc(C.sem("vv"), 1)
                        idx += 1
        C.c["mm"] = mm0 + 16
        C.c["vv"] = vv0 + 16

        if th == 0:
            vv0 = C.c["vv"]
            with nc.Block() as blk:
                @blk.vector
                def _(v):
                    for mc in range(8):
                        v.drain()
                        v.scalar_tensor_tensor(
                            yh[:, 1024 * mc:1024 * mc + 8],
                            pts[:, 8 * mc:8 * (mc + 1)], coreflag[:, 0:1],
                            yh[:, 1024 * mc:1024 * mc + 8],
                            op0=ALU.mult, op1=ALU.add)
                    v.engine_nop().then_inc(C.sem("vv"), 1)
            C.c["vv"] = vv0 + 1

        def ysrc(c, _th=th):
            return yh[:, 1024 * c:1024 * (c + 1)]
        emit_ln_stats_mid(C, ysrc, 1024, src_dtype=F32)
        emit_ln_tail(C, 1024, 1024 * th, "post", x_dram=x_in, ysrc=ysrc,
                     store=x_out, xst2=ws1[:, 0:2048].bitcast(F32),
                     sq2=ws1[:, 2048:4096].bitcast(F32),
                     acp=ws1[:, 6144:8192].bitcast(F32),
                     bcp=ws2[:, 0:2048].bitcast(F32),
                     fold_stats=fold_out)


def emit_ffn1(C, x_in, x_out, bs, fold_in=False):
    nc = C.nc
    xn = C.t["XN"]
    scr = C.t["SCR16"]
    psa, psb = C.t["PSA"], C.t["PSB"]
    id32 = C.t["id32"]
    ones1f, ones128f = C.t["ones1f"], C.t["ones128f"]
    ustrict = C.t["ustrict"]
    maskcol = C.t["maskcol"]
    tmp1 = C.t["TMP1"]

    agc_in = nc.dram_tensor("f1_agc_in", [1, 8], F32)
    agc_out = nc.dram_tensor("f1_agc_out", [8, 8], F32)
    rt16 = nc.dram_tensor("f1_rt16", [1, 128], F32)

    f1xst2 = scr[:, 0:4096].bitcast(F32)
    f1sq2 = scr[:, 4096:8192].bitcast(F32)
    if not fold_in:
        emit_ln_stats_from_dram(C, x_in, xst2=f1xst2, sq2=f1sq2)
    emit_ln_tail(C, T, 0, "pre", x_dram=x_in, out_bf=xn, xst2=f1xst2, sq2=f1sq2,
                 acp=scr[:, 8192:12288].bitcast(F32),
                 bcp=scr[:, 12288:16384].bitcast(F32),
                 stats_row=32 if fold_in else 0)

    yh = C.sb("YF", [128, 16384], F32, bs)
    comb = C.sb("COMB", [128, 128], F32, bs)

    with ExitStack() as bs2:
        R = _emit_gate_softmax(C, bs2, "b3_gwT", with_groups=False)
        e1, e2, w1c, w2c, sml, sm = (R["e1"], R["e2"], R["w1"], R["w2"],
                                     R["sml"], R["sm"])
        te = C.sb("TE", [128, 128], F32, bs2)
        tss = C.sb("TSS", [8, 16], F32, bs2)
        stb = C.sb("STB", [16, 8], F32, bs2)
        clrow = C.sb("CLROW", [1, 8], F32, bs2)
        agc = C.sb("AGC", [8, 8], F32, bs2)
        pfx = C.sb("PFX", [8, 1], F32, bs2)
        pbt = C.sb("PBT", [8, 16], F32, bs2)
        pbf = C.sb("PBF", [16, 8], F32, bs2)
        pbrow = C.sb("PBROW", [1, 128], F32, bs2)
        km = C.sb("KM", [128, 128], F32, bs2)
        wn = C.sb("WN", [128, 64], F32, bs2)

        vv0 = C.c["vv"]
        with nc.Block() as blk:
            @blk.vector
            def _(v):
                v.tensor_tensor(wn[:, 0:16], w1c, w2c, op=ALU.add)
                v.drain()
                v.tensor_scalar_add(wn[:, 0:16], wn[:, 0:16], 1e-20)
                v.drain()
                v.reciprocal(wn[:, 16:32], wn[:, 0:16])
                v.drain()
                v.tensor_tensor(wn[:, 32:48], w1c, wn[:, 16:32], op=ALU.mult)
                v.tensor_tensor(wn[:, 48:64], w2c, wn[:, 16:32], op=ALU.mult)
                v.drain()
                v.tensor_tensor(te[:, :], e1[:, :], e2[:, :], op=ALU.add
                                ).then_inc(C.sem("vv"), 1)
        C.c["vv"] = vv0 + 1

        mm0 = C.c["mm"]; ev0 = C.c["ev"]; vv0 = C.c["vv"]
        with nc.Block() as blk:
            @blk.tensor
            def _(t):
                mmi = None
                for tl in range(NTC):
                    mmi = t.matmul(psa[0:8, tl:tl + 1], te[:, 8 * tl:8 * (tl + 1)],
                                   ones128f[:, :], start=True, stop=True)
                mmi.then_inc(C.sem("mm"), 1)

            @blk.scalar
            def _(s):
                s.wait_ge(C.sem("mm"), mm0 + 1)
                s.copy(tss[:, :], psa[0:8, 0:16]).then_inc(C.sem("ev"), 1)
        C.c["mm"] = mm0 + 1
        C.c["ev"] = ev0 + 1

        mm0 = C.c["mm"]; vv0 = C.c["vv"]; ev0 = C.c["ev"]
        with nc.Block() as blk:
            @blk.tensor
            def _(t):
                t.transpose(psa[0:16, 16:24], tss[:, :], id32[0:8, 0:8]
                            ).then_inc(C.sem("mm"), 1)

            @blk.vector
            def _(v):
                v.wait_ge(C.sem("mm"), mm0 + 1)
                v.tensor_copy(stb[:, :], psa[0:16, 16:24]).then_inc(C.sem("vv"), 1)
        C.c["mm"] = mm0 + 1
        C.c["vv"] = vv0 + 1

        mm0 = C.c["mm"]; ev0 = C.c["ev"]
        with nc.Block() as blk:
            @blk.tensor
            def _(t):
                t.matmul(psa[0:1, 24:32], ones128f[0:16, :], stb[:, :],
                         start=True, stop=True).then_inc(C.sem("mm"), 1)

            @blk.scalar
            def _(s):
                s.wait_ge(C.sem("mm"), mm0 + 1)
                s.copy(clrow[:, :], psa[0:1, 24:32]).then_inc(C.sem("ev"), 1)
        C.c["mm"] = mm0 + 1
        C.c["ev"] = ev0 + 1

        gd0 = C.c["gd"]; cc0 = C.c["cc"]
        with nc.Block() as blk:
            @blk.gpsimd
            def _(g):
                g.dma_start(out=agc_in[:, :], in_=clrow[:, :]).then_inc(C.sem("gd"), 16)
                g.wait_ge(C.sem("gd"), gd0 + 16)
                g.collective_compute(
                    "AllGather", ALU.bypass, replica_groups=[list(range(NCORES))],
                    ins=[agc_in.ap().opt()], outs=[agc_out.ap().opt()],
                ).then_inc(C.sem("cc"), 1)
                g.wait_ge(C.sem("cc"), cc0 + 1)
                g.dma_start(out=agc[:, :], in_=agc_out[:, :]).then_inc(C.sem("gd"), 16)
                g.wait_ge(C.sem("gd"), gd0 + 32)
        C.c["gd"] = gd0 + 32
        C.c["cc"] = cc0 + 1

        mm0 = C.c["mm"]; vv0 = C.c["vv"]; ev0 = C.c["ev"]
        with nc.Block() as blk:
            @blk.tensor
            def _(t):
                t.matmul(psa[0:8, 32:33], agc[:, :], maskcol[:, :],
                         start=True, stop=True).then_inc(C.sem("mm"), 1)

            @blk.vector
            def _(v):
                v.wait_ge(C.sem("mm"), mm0 + 1)
                v.tensor_copy(pfx[:, :], psa[0:8, 32:33])
                v.drain()
                v.tensor_tensor_scan(pbt[:, :],
                                     bass.AP(ones128f, 0, [[1, 8], [0, 16]]),
                                     tss[:, :], 0.0, op0=ALU.mult, op1=ALU.add)
                v.drain()
                v.tensor_tensor(pbt[:, :], pbt[:, :], tss[:, :], op=ALU.subtract)
                v.drain()
                v.tensor_scalar(pbt[:, :], pbt[:, :], pfx[:, 0:1], None,
                                op0=ALU.add).then_inc(C.sem("vv"), 1)
        C.c["mm"] = mm0 + 1
        C.c["vv"] = vv0 + 1

        mm0 = C.c["mm"]; vv0 = C.c["vv"]
        with nc.Block() as blk:
            @blk.tensor
            def _(t):
                t.transpose(psa[0:16, 40:48], pbt[:, :], id32[0:8, 0:8]
                            ).then_inc(C.sem("mm"), 1)

            @blk.vector
            def _(v):
                v.wait_ge(C.sem("mm"), mm0 + 1)
                v.tensor_copy(pbf[:, :], psa[0:16, 40:48]).then_inc(C.sem("vv"), 1)
        C.c["mm"] = mm0 + 1
        C.c["vv"] = vv0 + 1

        gd0 = C.c["gd"]
        with nc.Block() as blk:
            @blk.gpsimd
            def _(g):
                g.dma_start(out=bass.AP(rt16, 0, [[8, 16], [1, 8]]),
                            in_=pbf[:, :]).then_inc(C.sem("gd"), 16)
                g.wait_ge(C.sem("gd"), gd0 + 16)
                g.dma_start(out=pbrow[:, :], in_=rt16[:, :]).then_inc(C.sem("gd"), 16)
                g.wait_ge(C.sem("gd"), gd0 + 32)
        C.c["gd"] = gd0 + 32

        mm0 = C.c["mm"]; vv0 = C.c["vv"]
        with nc.Block() as blk:
            @blk.tensor
            def _(t):
                mmi = None
                for tl in range(NTC):
                    t.matmul(psb[:, 8 * tl:8 * (tl + 1)], ustrict[:, :],
                             te[:, 8 * tl:8 * (tl + 1)],
                             start=(tl == 0), stop=False)
                    mmi = t.matmul(psb[:, 8 * tl:8 * (tl + 1)], ones1f[0:1, :],
                                   pbrow[0:1, 8 * tl:8 * (tl + 1)],
                                   start=False, stop=(tl == NTC - 1))
                mmi.then_inc(C.sem("mm"), 1)

            @blk.vector
            def _(v):
                v.wait_ge(C.sem("mm"), mm0 + 1)
                v.tensor_scalar(km[:, :], psb[:, 0:128], float(CAPACITY), None,
                                op0=ALU.is_lt)
                v.drain()
                v.tensor_tensor(sm[:, 0:128], km[:, :], e1[:, :], op=ALU.mult)
                v.drain()
                v.reduce_sum(wn[:, 0:16], sm[:, 0:128].rearrange(
                    "p (t e) -> p t e", e=8), axis=AX.X)
                v.drain()
                v.tensor_tensor(sm[:, 0:128], km[:, :], e2[:, :], op=ALU.mult)
                v.drain()
                v.reduce_sum(wn[:, 16:32], sm[:, 0:128].rearrange(
                    "p (t e) -> p t e", e=8), axis=AX.X)
                v.drain()
                v.tensor_tensor(wn[:, 32:48], wn[:, 32:48], wn[:, 0:16], op=ALU.mult)
                v.tensor_tensor(wn[:, 48:64], wn[:, 48:64], wn[:, 16:32], op=ALU.mult)
                v.drain()
                for tl in range(NTC):
                    v.tensor_scalar(sm[:, 8 * tl:8 * (tl + 1)],
                                    e1[:, 8 * tl:8 * (tl + 1)],
                                    wn[:, 32 + tl:33 + tl], None, op0=ALU.mult)
                v.drain()
                for tl in range(NTC):
                    v.scalar_tensor_tensor(comb[:, 8 * tl:8 * (tl + 1)],
                                           e2[:, 8 * tl:8 * (tl + 1)],
                                           wn[:, 48 + tl:49 + tl],
                                           sm[:, 8 * tl:8 * (tl + 1)],
                                           op0=ALU.mult, op1=ALU.add)
                v.engine_nop().then_inc(C.sem("vv"), 1)
        C.c["mm"] = mm0 + 1
        C.c["vv"] = vv0 + 1

    wse1 = C.sb("WSE1", [128, 8192], BF16, bs)
    wse2 = C.sb("WSE2", [128, 8192], BF16, bs)
    wse3 = C.sb("WSE3", [128, 8192], BF16, bs)

    def load_we(g, dram_name, e, slot):
        g.dma_start(out=slot[:, :],
                    in_=_wview(C.t[dram_name], 8, 1024, row0=1024 * e)
                    ).then_inc(C.sem("gd"), 16)
        C.c["gd"] += 16

    def load_sh(g, dram_name, sh, slot):
        if dram_name == "b3_wsdT":
            ap = _wview(C.t[dram_name], 8, 1024, row0=1024 * sh)
        else:
            ap = _wview(C.t[dram_name], 8, 1024, col0=1024 * sh)
        g.dma_start(out=slot[:, :], in_=ap).then_inc(C.sem("gd"), 16)
        C.c["gd"] += 16

    with nc.Block() as blk:
        @blk.gpsimd
        def _(g):
            load_we(g, "b3_wegT", 0, wse1)
            load_we(g, "b3_weuT", 0, wse2)
            g.wait_ge(C.sem("gd"), C.c["gd"])
    gu_need = C.c["gd"]
    dn_need = 0

    for e in range(8):
        mm0 = C.c["mm"]; ev0 = C.c["ev"]; vv0 = C.c["vv"]
        with nc.Block() as blk:
            @blk.gpsimd
            def _(g):
                load_we(g, "b3_wedT", e, wse3)
                g.wait_ge(C.sem("gd"), C.c["gd"])
            dn_need = C.c["gd"]

            @blk.tensor
            def _(t):
                t.wait_ge(C.sem("gd"), gu_need)
                idx = 0
                for ic in range(8):
                    for s4 in range(4):
                        if idx >= 2:
                            t.wait_ge(C.sem("vv"), vv0 + idx - 1)
                        for dc in range(NDC):
                            t.matmul(psa[:, 1024 * (idx % 2):1024 * (idx % 2) + 512],
                                     wse1[:, 1024 * dc + 128 * ic:1024 * dc + 128 * (ic + 1)],
                                     xn[:, 2048 * dc + 512 * s4:2048 * dc + 512 * (s4 + 1)],
                                     start=(dc == 0), stop=(dc == NDC - 1))
                        for dc in range(NDC):
                            mmi = t.matmul(
                                psa[:, 1024 * (idx % 2) + 512:1024 * (idx % 2) + 1024],
                                wse2[:, 1024 * dc + 128 * ic:1024 * dc + 128 * (ic + 1)],
                                xn[:, 2048 * dc + 512 * s4:2048 * dc + 512 * (s4 + 1)],
                                start=(dc == 0), stop=(dc == NDC - 1))
                        mmi.then_inc(C.sem("mm"), 1)
                        idx += 1

            @blk.scalar
            def _(s):
                idx = 0
                for ic in range(8):
                    for s4 in range(4):
                        s.wait_ge(C.sem("mm"), mm0 + idx + 1)
                        if idx >= 1:
                            s.wait_ge(C.sem("vv"), vv0 + idx)
                        s.activation(out=tmp1[:, 0:512],
                                     in_=psa[:, 1024 * (idx % 2):1024 * (idx % 2) + 512],
                                     func=AF.Silu).then_inc(C.sem("ev"), 1)
                        idx += 1

            @blk.vector
            def _(v):
                idx = 0
                for ic in range(8):
                    for s4 in range(4):
                        v.wait_ge(C.sem("ev"), ev0 + idx + 1)
                        v.tensor_tensor(
                            scr[:, 2048 * ic + 512 * s4:2048 * ic + 512 * (s4 + 1)],
                            psa[:, 1024 * (idx % 2) + 512:1024 * (idx % 2) + 1024],
                            tmp1[:, 0:512], op=ALU.mult).then_inc(C.sem("vv"), 1)
                        idx += 1
        C.c["mm"] = mm0 + 32
        C.c["ev"] = ev0 + 32
        C.c["vv"] = vv0 + 32

        mm0 = C.c["mm"]; vv0 = C.c["vv"]
        with nc.Block() as blk:
            @blk.gpsimd
            def _(g):
                if e < 7:
                    load_we(g, "b3_wegT", e + 1, wse1)
                    load_we(g, "b3_weuT", e + 1, wse2)
                else:
                    load_sh(g, "b3_wsgT", 0, wse1)
                    load_sh(g, "b3_wsuT", 0, wse2)
                g.wait_ge(C.sem("gd"), C.c["gd"])
            gu_need = C.c["gd"]

            @blk.tensor
            def _(t):
                t.wait_ge(C.sem("gd"), dn_need)
                for tc in range(NTC):
                    if tc >= 2:
                        t.wait_ge(C.sem("vv"), vv0 + tc - 1)
                    for ic in range(8):
                        for nh in range(2):
                            mmi = t.matmul(
                                psa[:, 1024 * (tc % 2) + 512 * nh:
                                    1024 * (tc % 2) + 512 * (nh + 1)],
                                scr[:, 2048 * ic + 128 * tc:2048 * ic + 128 * (tc + 1)],
                                wse3[:, 1024 * ic + 512 * nh:1024 * ic + 512 * (nh + 1)],
                                start=(ic == 0), stop=(ic == 7))
                    mmi.then_inc(C.sem("mm"), 1)

            @blk.vector
            def _(v):
                for tc in range(NTC):
                    v.wait_ge(C.sem("mm"), mm0 + tc + 1)
                    v.drain()
                    if e == 0:
                        v.tensor_scalar(yh[:, 1024 * tc:1024 * (tc + 1)],
                                        psa[:, 1024 * (tc % 2):1024 * (tc % 2) + 1024],
                                        comb[:, 8 * tc + e:8 * tc + e + 1], None,
                                        op0=ALU.mult).then_inc(C.sem("vv"), 1)
                    else:
                        v.scalar_tensor_tensor(yh[:, 1024 * tc:1024 * (tc + 1)],
                                               psa[:, 1024 * (tc % 2):1024 * (tc % 2) + 1024],
                                               comb[:, 8 * tc + e:8 * tc + e + 1],
                                               yh[:, 1024 * tc:1024 * (tc + 1)],
                                               op0=ALU.mult, op1=ALU.add
                                               ).then_inc(C.sem("vv"), 1)
        C.c["mm"] = mm0 + NTC
        C.c["vv"] = vv0 + NTC

    for sh in range(2):
        mm0 = C.c["mm"]; ev0 = C.c["ev"]; vv0 = C.c["vv"]
        with nc.Block() as blk:
            @blk.gpsimd
            def _(g):
                load_sh(g, "b3_wsdT", sh, wse3)
                g.wait_ge(C.sem("gd"), C.c["gd"])
            dn_need = C.c["gd"]

            @blk.tensor
            def _(t):
                t.wait_ge(C.sem("gd"), gu_need)
                idx = 0
                for ic in range(8):
                    for s4 in range(4):
                        if idx >= 2:
                            t.wait_ge(C.sem("vv"), vv0 + idx - 1)
                        for dc in range(NDC):
                            t.matmul(psa[:, 1024 * (idx % 2):1024 * (idx % 2) + 512],
                                     wse1[:, 1024 * dc + 128 * ic:1024 * dc + 128 * (ic + 1)],
                                     xn[:, 2048 * dc + 512 * s4:2048 * dc + 512 * (s4 + 1)],
                                     start=(dc == 0), stop=(dc == NDC - 1))
                        for dc in range(NDC):
                            mmi = t.matmul(
                                psa[:, 1024 * (idx % 2) + 512:1024 * (idx % 2) + 1024],
                                wse2[:, 1024 * dc + 128 * ic:1024 * dc + 128 * (ic + 1)],
                                xn[:, 2048 * dc + 512 * s4:2048 * dc + 512 * (s4 + 1)],
                                start=(dc == 0), stop=(dc == NDC - 1))
                        mmi.then_inc(C.sem("mm"), 1)
                        idx += 1

            @blk.scalar
            def _(s):
                idx = 0
                for ic in range(8):
                    for s4 in range(4):
                        s.wait_ge(C.sem("mm"), mm0 + idx + 1)
                        if idx >= 1:
                            s.wait_ge(C.sem("vv"), vv0 + idx)
                        s.activation(out=tmp1[:, 0:512],
                                     in_=psa[:, 1024 * (idx % 2):1024 * (idx % 2) + 512],
                                     func=AF.Silu).then_inc(C.sem("ev"), 1)
                        idx += 1

            @blk.vector
            def _(v):
                idx = 0
                for ic in range(8):
                    for s4 in range(4):
                        v.wait_ge(C.sem("ev"), ev0 + idx + 1)
                        v.tensor_tensor(
                            scr[:, 2048 * ic + 512 * s4:2048 * ic + 512 * (s4 + 1)],
                            psa[:, 1024 * (idx % 2) + 512:1024 * (idx % 2) + 1024],
                            tmp1[:, 0:512], op=ALU.mult).then_inc(C.sem("vv"), 1)
                        idx += 1
        C.c["mm"] = mm0 + 32
        C.c["ev"] = ev0 + 32
        C.c["vv"] = vv0 + 32

        mm0 = C.c["mm"]; vv0 = C.c["vv"]
        with nc.Block() as blk:
            @blk.gpsimd
            def _(g):
                if sh == 0:
                    load_sh(g, "b3_wsgT", 1, wse1)
                    load_sh(g, "b3_wsuT", 1, wse2)
                g.wait_ge(C.sem("gd"), C.c["gd"])
            gu_need = C.c["gd"]

            @blk.tensor
            def _(t):
                t.wait_ge(C.sem("gd"), dn_need)
                for tc in range(NTC):
                    if tc >= 2:
                        t.wait_ge(C.sem("vv"), vv0 + tc - 1)
                    for ic in range(8):
                        for nh in range(2):
                            mmi = t.matmul(
                                psa[:, 1024 * (tc % 2) + 512 * nh:
                                    1024 * (tc % 2) + 512 * (nh + 1)],
                                scr[:, 2048 * ic + 128 * tc:2048 * ic + 128 * (tc + 1)],
                                wse3[:, 1024 * ic + 512 * nh:1024 * ic + 512 * (nh + 1)],
                                start=(ic == 0), stop=(ic == 7))
                    mmi.then_inc(C.sem("mm"), 1)

            @blk.vector
            def _(v):
                for tc in range(NTC):
                    v.wait_ge(C.sem("mm"), mm0 + tc + 1)
                    v.drain()
                    v.tensor_tensor(yh[:, 1024 * tc:1024 * (tc + 1)],
                                    psa[:, 1024 * (tc % 2):1024 * (tc % 2) + 1024],
                                    yh[:, 1024 * tc:1024 * (tc + 1)], op=ALU.add
                                    ).then_inc(C.sem("vv"), 1)
        C.c["mm"] = mm0 + NTC
        C.c["vv"] = vv0 + NTC

    sqs = C.sb("SQS", [128, 16], F32, bs)
    sms = C.sb("SMS", [128, 16], F32, bs)
    rst2 = C.sb("RST2", [128, 16], F32, bs)
    bc2 = C.sb("BC2", [128, 16], F32, bs)
    sqscr = C.sb("SQSCR", [128, 1024], F32, bs)

    ev0 = C.c["ev"]; vv0 = C.c["vv"]
    with nc.Block() as blk:
        @blk.scalar
        def _(s):
            for tc in range(NTC):
                s.drain()
                s.activation(out=sqscr[:, :], in_=yh[:, 1024 * tc:1024 * (tc + 1)],
                             func=AF.Square, accum_out=sqs[:, tc:tc + 1]
                             ).then_inc(C.sem("ev"), 1)

        @blk.vector
        def _(v):
            for tc in range(NTC):
                v.drain()
                v.reduce_sum(sms[:, tc:tc + 1], yh[:, 1024 * tc:1024 * (tc + 1)],
                             axis=AX.X).then_inc(C.sem("vv"), 1)
    C.c["ev"] = ev0 + NTC
    C.c["vv"] = vv0 + NTC

    ev0 = C.c["ev"]; vv0 = C.c["vv"]
    with nc.Block() as blk:
        @blk.vector
        def _(v):
            v.tensor_scalar_mul(sms[:, :], sms[:, :], 1.0 / D)
            v.drain()
            v.tensor_scalar_mul(sqs[:, :], sqs[:, :], 1.0 / D)
            v.drain()
            v.tensor_tensor(rst2[:, :], sms[:, :], sms[:, :], op=ALU.mult)
            v.drain()
            v.tensor_tensor(sqs[:, :], sqs[:, :], rst2[:, :], op=ALU.subtract)
            v.drain()
            v.tensor_scalar_add(sqs[:, :], sqs[:, :], EPS).then_inc(C.sem("vv"), 1)
    C.c["vv"] = vv0 + 1

    ev0 = C.c["ev"]; vv0 = C.c["vv"]
    with nc.Block() as blk:
        @blk.scalar
        def _(s):
            s.activation(out=rst2[:, :], in_=sqs[:, :], func=AF.Ln)
            s.drain()
            s.activation(out=rst2[:, :], in_=rst2[:, :], func=AF.Exp, scale=-0.5
                         ).then_inc(C.sem("ev"), 1)

        @blk.vector
        def _(v):
            v.wait_ge(C.sem("ev"), ev0 + 1)
            v.scalar_tensor_tensor(bc2[:, :], sms[:, :], -1.0, rst2[:, :],
                                   op0=ALU.mult, op1=ALU.mult).then_inc(C.sem("vv"), 1)
    C.c["ev"] = ev0 + 1
    C.c["vv"] = vv0 + 1

    ld0 = C.c["ld"]; mm0 = C.c["mm"]; vv0 = C.c["vv"]; st0 = C.c["st"]
    xst = C.t["XST"]
    with nc.Block() as blk:
        @blk.sync
        def _(sync):
            for tc in range(NTC):
                if tc > 0:
                    sync.wait_ge(C.sem("mm"), mm0 + tc)
                for dc in range(NDC):
                    sync.dma_start(out=xst[:, 128 * dc:128 * (dc + 1)],
                                   in_=x_in[128 * dc:128 * (dc + 1),
                                            128 * tc:128 * (tc + 1)]
                                   ).then_inc(C.sem("ld"), 16)

        @blk.tensor
        def _(t):
            for tc in range(NTC):
                t.wait_ge(C.sem("ld"), ld0 + 128 * (tc + 1))
                if tc >= 2:
                    t.wait_ge(C.sem("vv"), vv0 + 2 * (tc - 1))
                tri_ = None
                for dc in range(NDC):
                    tri_ = t.transpose(
                        psb[:, 1024 * (tc % 2) + 128 * dc:1024 * (tc % 2) + 128 * (dc + 1)],
                        xst[:, 128 * dc:128 * (dc + 1)], id32[:, :])
                tri_.then_inc(C.sem("mm"), 1)

        @blk.vector
        def _(v):
            for tc in range(NTC):
                v.wait_ge(C.sem("mm"), mm0 + tc + 1)
                if tc > 0:
                    v.wait_ge(C.sem("st"), st0 + 16 * tc)
                v.scalar_tensor_tensor(tmp1[:, 0:1024],
                                       yh[:, 1024 * tc:1024 * (tc + 1)],
                                       rst2[:, tc:tc + 1],
                                       psb[:, 1024 * (tc % 2):1024 * (tc % 2) + 1024],
                                       op0=ALU.mult, op1=ALU.add
                                       ).then_inc(C.sem("vv"), 1)
                v.drain()
                v.tensor_scalar(tmp1[:, 0:1024], tmp1[:, 0:1024],
                                bc2[:, tc:tc + 1], None, op0=ALU.add
                                ).then_inc(C.sem("vv"), 1)
                v.drain()

        @blk.scalar
        def _(s):
            for tc in range(NTC):
                s.wait_ge(C.sem("vv"), vv0 + 2 * (tc + 1))
                s.dma_start(out=x_out[128 * tc:128 * (tc + 1), :],
                            in_=tmp1[:, 0:1024]).then_inc(C.sem("st"), 16)
            s.wait_ge(C.sem("st"), st0 + 16 * NTC)
    C.c["ld"] = ld0 + 128 * NTC
    C.c["mm"] = mm0 + NTC
    C.c["vv"] = vv0 + 2 * NTC
    C.c["st"] = st0 + 16 * NTC
